# version 04
# baseline (speedup 1.0000x reference)
import numpy as np
import ml_dtypes
import concourse.bass as bass
import concourse.mybir as mybir
from concourse.bass_utils import run_bass_kernel_spmd
from concourse.aot_env import allow_ici_all_to_all

F32 = mybir.dt.float32
BF16 = mybir.dt.bfloat16
FP16 = mybir.dt.float16
OP = mybir.AluOpType
AX = mybir.AxisListType
AF = mybir.ActivationFunctionType

NCORES = 8
B_ = 2
N_ = 4096
DM = 4096
TOKS = B_ * N_
TPC = TOKS // NCORES
NH = 4
DH = 128
NCHUNK = 32
HC = 1408
OFC = HC // 128
KC = DM // 128
INV_SQRT_D = 0.08838834764831845

_NF4 = np.array([-1.0, -0.6961928009986877, -0.5250730514526367,
    -0.39491748809814453, -0.28444138169288635, -0.18477343022823334,
    -0.09105003625154495, 0.0, 0.07958029955625534, 0.16093020141124725,
    0.24611230194568634, 0.33791524171829224, 0.44070982933044434,
    0.5626170039176941, 0.7229568362236023, 1.0], dtype=np.float64)
NF4_BOUNDS = ((_NF4[:-1] + _NF4[1:]) * 0.5).tolist()
NF4_DELTAS = np.diff(_NF4).tolist()

ENGINES = ("sync", "tensor", "scalar", "vector", "gpsimd")
CH_STAGE = 99
DOWN_QBAR = False


class Sem:
    def __init__(self, h):
        self.h = h
        self.n = 0
        self.is_dma = False


class Tile:
    __slots__ = ("wr", "readers", "name", "subs", "parent", "dsem")

    def __init__(self, name=""):
        self.wr = None
        self.readers = []
        self.name = name
        self.subs = {}
        self.parent = None
        self.dsem = None


class TT:
    __slots__ = ("ap", "t")

    def __init__(self, ap, tile=None, name=""):
        self.ap = ap
        self.t = tile if tile is not None else Tile(name)

    def v(self, ap):
        return TT(ap, self.t)

    def sub(self, key):
        st = self.t.subs.get(key)
        if st is None:
            st = Tile(f"{self.t.name}:{key}")
            st.parent = self.t
            self.t.subs[key] = st
        return TT(self.ap, st)


class Builder:
    def __init__(self, nc):
        self.nc = nc
        self.prog = {e: [] for e in ENGINES}
        self.waited = {e: {} for e in ENGINES}
        self._cms = []
        self.qL = self.sem("qL")
        self.qS = self.sem("qS")
        self.qG = self.sem("qG")
        self.cPE = self.sem("cPE")
        self.cACT = self.sem("cACT")
        self.cDVE = self.sem("cDVE")
        self.cGP = self.sem("cGP")
        self.chain = {"tensor": self.cPE, "scalar": self.cACT,
                      "vector": self.cDVE, "gpsimd": self.cGP, "sync": self.cGP}
        self.qsem = {"sync": self.qL, "scalar": self.qS, "gpsimd": self.qG}
        self.all_sems = [self.qL, self.qS, self.qG, self.cPE, self.cACT,
                         self.cDVE, self.cGP]
        self._dsem_n = 0

    def _dma_sem(self, tile):
        top = tile.parent if tile.parent is not None else tile
        if top.dsem is None:
            top.dsem = self.sem(f"d{self._dsem_n}")
            top.dsem.is_dma = True
            self._dsem_n += 1
            self.all_sems.append(top.dsem)
        return top.dsem

    def sem(self, name):
        cm = self.nc.semaphore(name)
        h = cm.__enter__()
        self._cms.append(cm)
        return Sem(h)

    def close(self):
        for cm in reversed(self._cms):
            cm.__exit__(None, None, None)

    def emit(self, eng, fn):
        self.prog[eng].append(fn)

    def wait(self, eng, ev):
        sem, val, peng = ev
        if sem.is_dma:
            val = sem.n
        if val <= 0 or self.waited[eng].get(sem, 0) >= val:
            return
        if eng == "tensor" and peng == "tensor":
            return
        self.waited[eng][sem] = val
        self.emit(eng, lambda h, s=sem.h, v=val: h.wait_ge(s, v))

    def _pre(self, eng, ins, outs):
        for t in ins:
            if t.wr is not None:
                self.wait(eng, t.wr)
        for t in outs:
            for ev in t.readers:
                self.wait(eng, ev)
            if t.wr is not None:
                self.wait(eng, t.wr)
            if t.parent is not None:
                for ev in t.parent.readers:
                    self.wait(eng, ev)

    def _post(self, eng, ins, outs, sem):
        ev = (sem, sem.n, eng)
        for t in ins:
            t.readers.append(ev)
        for t in outs:
            t.wr = ev
            t.readers = []
            if t.parent is not None:
                t.parent.wr = ev

    def do(self, eng, ins, outs, fn, dma=False):
        ins = [x.t if isinstance(x, TT) else x for x in ins]
        outs = [x.t if isinstance(x, TT) else x for x in outs]
        self._pre(eng, ins, outs)
        if dma:
            sem = self._dma_sem(outs[0])
        else:
            sem = self.chain[eng]
        k = 16 if dma else 1
        sem.n += k
        self.emit(eng, lambda h, f=fn, s=sem.h, kk=k: f(h).then_inc(s, kk))
        self._post(eng, ins, outs, sem)

    def group(self, eng, ins, outs, fns):
        ins = [x.t if isinstance(x, TT) else x for x in ins]
        outs = [x.t if isinstance(x, TT) else x for x in outs]
        self._pre(eng, ins, outs)
        sem = self.chain[eng]
        sem.n += 1

        def run(h, fs=tuple(fns), s=sem.h):
            last = None
            for f in fs:
                last = f(h)
            last.then_inc(s, 1)
        self.emit(eng, run)
        self._post(eng, ins, outs, sem)

    def barrier(self):
        evs = [(s, s.n, None) for s in self.all_sems]
        for e in ENGINES:
            for ev in evs:
                self.wait(e, ev)

    def run_block(self):
        with self.nc.Block() as block:
            @block.sync
            def _(h):
                for fn in self.prog["sync"]:
                    fn(h)

            @block.tensor
            def _(h):
                for fn in self.prog["tensor"]:
                    fn(h)

            @block.scalar
            def _(h):
                for fn in self.prog["scalar"]:
                    fn(h)

            @block.vector
            def _(h):
                for fn in self.prog["vector"]:
                    fn(h)

            @block.gpsimd
            def _(h):
                for fn in self.prog["gpsimd"]:
                    fn(h)

    def dma(self, eng, out, in_, oap=None, iap=None):
        o = oap if oap is not None else out.ap
        i = iap if iap is not None else in_.ap
        self.do(eng, [in_], [out], lambda h, o=o, i=i: h.dma_start(out=o, in_=i),
                dma=True)

    def copy(self, eng, out, in_, oap=None, iap=None):
        o = oap if oap is not None else out.ap
        i = iap if iap is not None else in_.ap
        if eng == "vector":
            self.do(eng, [in_], [out], lambda h, o=o, i=i: h.tensor_copy(o, i))
        else:
            self.do(eng, [in_], [out],
                    lambda h, o=o, i=i: h.activation(o, i, AF.Identity))

    def tt(self, out, a, b, op, oap=None, aap=None, bap=None, bdep=()):
        o = oap if oap is not None else out.ap
        x = aap if aap is not None else a.ap
        y = bap if bap is not None else (b.ap if isinstance(b, TT) else b)
        ins = [a] + ([b] if isinstance(b, TT) else []) + list(bdep)
        self.do("vector", ins, [out],
                lambda h, o=o, x=x, y=y, op=op: h.tensor_tensor(o, x, y, op))

    def ts(self, out, in_, s1, s2, op0, op1=None, oap=None, iap=None, sdep=()):
        o = oap if oap is not None else out.ap
        i = iap if iap is not None else in_.ap
        if op1 is None:
            self.do("vector", [in_, *sdep], [out],
                    lambda h, o=o, i=i, s1=s1, op0=op0:
                    h.tensor_single_scalar(o, i, s1, op0))
        else:
            self.do("vector", [in_, *sdep], [out],
                    lambda h, o=o, i=i, s1=s1, s2=s2, op0=op0, op1=op1:
                    h.tensor_scalar(o, i, s1, s2, op0, op1))

    def stt(self, out, in0, s, in1, op0, op1, oap=None, i0ap=None, i1ap=None,
            sdep=()):
        o = oap if oap is not None else out.ap
        a = i0ap if i0ap is not None else in0.ap
        b = i1ap if i1ap is not None else in1.ap
        self.do("vector", [in0, in1, *sdep], [out],
                lambda h, o=o, a=a, b=b, s=s, op0=op0, op1=op1:
                h.scalar_tensor_tensor(o, a, s, b, op0, op1))

    def red(self, out, in_, op, oap=None, iap=None, absval=False, neg=False):
        o = oap if oap is not None else out.ap
        i = iap if iap is not None else in_.ap
        self.do("vector", [in_], [out],
                lambda h, o=o, i=i, op=op, a=absval, n=neg: h.tensor_reduce(
                    o, i, axis=AX.X, op=op, apply_absolute_value=a, negate=n))

    def recip(self, out, in_, oap=None, iap=None):
        o = oap if oap is not None else out.ap
        i = iap if iap is not None else in_.ap
        self.do("vector", [in_], [out], lambda h, o=o, i=i: h.reciprocal(o, i))

    def act(self, out, in_, func, bias=0.0, scale=1.0, accum=None, oap=None,
            iap=None, sdep=(), aout=None):
        o = oap if oap is not None else out.ap
        i = iap if iap is not None else in_.ap
        outs = [out] + ([aout] if aout is not None else [])
        self.do("scalar", [in_, *sdep], outs,
                lambda h, o=o, i=i, f=func, b=bias, s=scale, a=accum:
                h.activation(o, i, f, bias=b, scale=s, accum_out=a))

    def scan_add(self, out, in_, oap=None, iap=None):
        o = oap if oap is not None else out.ap
        i = iap if iap is not None else in_.ap
        self.do("vector", [in_], [out],
                lambda h, o=o, i=i: h.tensor_tensor_scan(o, i, i, 0.0, OP.add,
                                                         OP.bypass))

    def trans8(self, psb, pairs, idb):
        fns = []
        ins = [s for (_, _, s, _) in pairs] + [idb]
        for j, (_, dap, _, sap) in enumerate(pairs):
            w = sap.shape[-1]
            p0 = dap.base_partition()
            fns.append(lambda h, o=psb.ap[p0:p0 + w, j * 128:(j + 1) * 128],
                       i=sap, idn=idb.ap: h.transpose(o, i, idn))
        self.group("tensor", ins, [psb], fns)
        for j, (dst, dap, _, sap) in enumerate(pairs):
            w = sap.shape[-1]
            p0 = dap.base_partition()
            self.copy("scalar", dst.v(dap) if dap is not dst.ap else dst, psb,
                      oap=dap, iap=psb.ap[p0:p0 + w, j * 128:(j + 1) * 128])


class Arena:
    def __init__(self, nc, name, nbytes):
        self.nbytes = nbytes
        self.t = nc.alloc_sbuf_tensor(name, [128, nbytes // 2], BF16)
        self.bot = 0
        self.top = nbytes

    def carve(self, shape, dtype, top=False):
        esz = {F32: 4, BF16: 2, FP16: 2}[dtype]
        nfree = int(np.prod(shape[1:]))
        nb = (nfree * esz + 63) & ~63
        if top:
            self.top -= nb
            off = self.top
        else:
            off = self.bot
            self.bot += nb
        assert self.bot <= self.top, f"arena overflow bot={self.bot} top={self.top}"
        base = self.t[:][:, off // 2: off // 2 + nfree * esz // 2]
        ap = base.bitcast(dtype) if dtype != BF16 else base
        if len(shape) > 2:
            dims = " ".join(f"d{i}" for i in range(1, len(shape)))
            ap = ap.rearrange(f"p ({dims}) -> p {dims}",
                              **{f"d{i}": shape[i] for i in range(1, len(shape))})
        return TT(ap, name=f"ar@{off}")


def build_program(upto=9, taps=(), nb=B_, nchunks=NCHUNK, do_quant=True, do_a2a=True):
    nc = bass.Bass("TRN2", target_bir_lowering=False)
    B = Builder(nc)

    d_x = nc.dram_tensor("xop", [TPC, DM], F32, kind="ExternalInput")
    d_wq = nc.dram_tensor("wq_sl", [DM, NH * DH], F32, kind="ExternalInput")
    d_wk = nc.dram_tensor("wk_sl", [DM, DH], F32, kind="ExternalInput")
    d_wv = nc.dram_tensor("wv_sl", [DM, DH], F32, kind="ExternalInput")
    d_wo = nc.dram_tensor("wo", [DM, DM], F32, kind="ExternalInput")
    d_w1 = nc.dram_tensor("w1_sl", [HC, DM], F32, kind="ExternalInput")
    d_w2 = nc.dram_tensor("w2_sl", [HC, DM], F32, kind="ExternalInput")
    d_w3 = nc.dram_tensor("w3_sl", [DM, HC], F32, kind="ExternalInput")
    d_cos = nc.dram_tensor("cos_t", [128, NCHUNK * DH], F32, kind="ExternalInput")
    d_sin = nc.dram_tensor("sin2_t", [128, NCHUNK * DH], F32, kind="ExternalInput")
    d_mask = nc.dram_tensor("mask_swa", [128, 192], F32, kind="ExternalInput")
    d_u01 = nc.dram_tensor("u01", [128, 128], F32, kind="ExternalInput")
    d_idb = nc.dram_tensor("identb", [128, 128], BF16, kind="ExternalInput")
    d_cb = nc.dram_tensor("cbias", [128, 16], F32, kind="ExternalInput")
    d_out = nc.dram_tensor("out_own", [TPC, DM], F32, kind="ExternalOutput")

    d_ag1i = nc.dram_tensor("ag1i", [128, 32768], BF16)
    d_ag1o = nc.dram_tensor("ag1o", [1024, 32768], BF16)
    d_osl = nc.dram_tensor("osl", [1024, NH * TPC], BF16)
    d_oex = nc.dram_tensor("oex", [1024, NH * TPC], BF16)
    d_x2 = nc.dram_tensor("x2own", [TPC, DM], F32)
    d_ag2i = nc.dram_tensor("ag2i", [128, 32768], BF16)
    d_ag2o = nc.dram_tensor("ag2o", [1024, 32768], BF16)
    d_dq1 = nc.dram_tensor("dq1T", [128, KC * HC], BF16)
    d_dq2 = nc.dram_tensor("dq2T", [128, KC * HC], BF16)
    d_dq3 = nc.dram_tensor("dq3T", [128, OFC * DM], BF16)
    d_gate = nc.dram_tensor("gateb", [128, OFC * TOKS], BF16)
    d_h = nc.dram_tensor("hb", [128, OFC * TOKS], BF16)
    d_part = [nc.dram_tensor(f"part{q}", [TOKS, 1024], F32) for q in range(4)]
    d_rso = [nc.dram_tensor(f"rso{q}", [TPC, 1024], F32) for q in range(4)]

    ag1i = TT(d_ag1i, name="ag1i"); ag1o = TT(d_ag1o, name="ag1o")
    osl = TT(d_osl, name="osl"); oex = TT(d_oex, name="oex")
    x2d = TT(d_x2, name="x2"); ag2i = TT(d_ag2i, name="ag2i")
    ag2o = TT(d_ag2o, name="ag2o")
    dq1d = TT(d_dq1, name="dq1"); dq2d = TT(d_dq2, name="dq2")
    dq3d = TT(d_dq3, name="dq3")
    gated = TT(d_gate, name="gate"); hd = TT(d_h, name="h")
    partd = [TT(d_part[q], name=f"part{q}") for q in range(4)]
    rsod = [TT(d_rso[q], name=f"rso{q}") for q in range(4)]
    outd = TT(d_out, name="out")
    xin = TT(d_x, name="xin"); wod = TT(d_wo, name="wo")
    w1d = TT(d_w1, name="w1"); w2d = TT(d_w2, name="w2"); w3d = TT(d_w3, name="w3")

    s_cos = TT(nc.alloc_sbuf_tensor("s_cos", [128, NCHUNK * DH], F32)[:])
    s_sin = TT(nc.alloc_sbuf_tensor("s_sin", [128, NCHUNK * DH], F32)[:])
    s_mask = TT(nc.alloc_sbuf_tensor("s_mask", [128, 192], F32)[:])
    s_u01 = TT(nc.alloc_sbuf_tensor("s_u01", [128, 128], F32)[:])
    s_idb = TT(nc.alloc_sbuf_tensor("s_idb", [128, 128], BF16)[:])
    s_cb = TT(nc.alloc_sbuf_tensor("s_cb", [128, 16], F32)[:])
    AR = Arena(nc, "AR", 163 * 1024)

    psQ = TT(nc.alloc_psum_tensor("psQ", [128, 512], F32)[:])
    psKV = TT(nc.alloc_psum_tensor("psKV", [128, 512], F32)[:])
    psP = TT(nc.alloc_psum_tensor("psP", [128, 512], F32)[:])
    psS = TT(nc.alloc_psum_tensor("psS", [128, 512], F32)[:])
    psSc = TT(nc.alloc_psum_tensor("psSc", [128, 512], F32)[:])
    psO = TT(nc.alloc_psum_tensor("psO", [128, 512], F32)[:])
    psTR = TT(nc.alloc_psum_tensor("psTR", [128, 1024], BF16)[:])
    psQT = TT(nc.alloc_psum_tensor("psQT", [128, 1024], BF16)[:])
    banks5 = [psQ, psKV, psP, psS, psSc]

    B.dma("sync", s_cos, TT(d_cos, name="cosi"), iap=d_cos[:, :])
    B.dma("sync", s_sin, TT(d_sin, name="sini"), iap=d_sin[:, :])
    B.dma("sync", s_mask, TT(d_mask, name="mski"), iap=d_mask[:, :])
    B.dma("sync", s_u01, TT(d_u01, name="u01i"), iap=d_u01[:, :])
    B.dma("sync", s_idb, TT(d_idb, name="idbi"), iap=d_idb[:, :])
    B.dma("sync", s_cb, TT(d_cb, name="cbi"), iap=d_cb[:, :])

    qzw = [AR.carve([128, 2048], F32, top=True) for _ in range(1)]
    qzv = AR.carve([128, 2048], F32, top=True)
    qzacc = AR.carve([128, 2048], FP16, top=True)
    qzt = AR.carve([128, 2048], FP16, top=True)
    qzam = AR.carve([128, 64], F32, top=True)
    qzrec = AR.carve([128, 64], F32, top=True)
    qzdq = AR.carve([128, 2048], BF16, top=True)
    qztt = AR.carve([128, 2048], BF16, top=True)
    top_quant = AR.top
    s_wq = AR.carve([128, KC * NH * DH], BF16, top=True)
    s_wk = AR.carve([128, KC * DH], BF16, top=True)
    s_wv = AR.carve([128, KC * DH], BF16, top=True)
    wq_r = d_wq[:, :].rearrange("(kc p) f -> p kc f", p=128)
    wqin = TT(d_wq, name="wqi"); wkin = TT(d_wk, name="wki")
    wvin = TT(d_wv, name="wvi")
    wq_s = s_wq.ap.rearrange("p (kc f) -> p kc f", f=NH * DH)
    for j in range(4):
        B.dma("gpsimd", s_wq, wqin, oap=wq_s[:, j * 8:(j + 1) * 8, :],
              iap=wq_r[:, j * 8:(j + 1) * 8, :])
    B.dma("gpsimd", s_wk, wkin,
          oap=s_wk.ap.rearrange("p (kc f) -> p kc f", f=DH),
          iap=d_wk[:, :].rearrange("(kc p) f -> p kc f", p=128))
    B.dma("gpsimd", s_wv, wvin,
          oap=s_wv.ap.rearrange("p (kc f) -> p kc f", f=DH),
          iap=d_wv[:, :].rearrange("(kc p) f -> p kc f", p=128))


    def ln_phase(src_d, src_tt, dst_d, dst_tt, kc_major):
        bot0 = AR.bot
        xt = [AR.carve([128, DM], F32) for _ in range(2)]
        sq = AR.carve([128, DM], F32)
        st = AR.carve([128, 16], F32)
        xlb = AR.carve([128, DM], BF16)
        xlt = AR.carve([128, DM], BF16)
        c_s1, c_s2, c_m, c_mm, c_v, c_rs, c_rl, c_b2 = (
            st.v(st.ap[:, i:i + 1]) for i in range(8))
        for tt in range(8):
            x = xt[tt % 2]
            B.dma("sync", x, src_tt, iap=src_d[tt * 128:(tt + 1) * 128, :])
            B.red(c_s1, x, OP.add)
            B.act(sq, x, AF.Square, accum=c_s2.ap, aout=c_s2)
            B.ts(c_m, c_s1, 1.0 / DM, None, OP.mult)
            B.ts(c_v, c_s2, 1.0 / DM, None, OP.mult)
            B.tt(c_mm, c_m, c_m, OP.mult)
            B.tt(c_v, c_v, c_mm, OP.subtract)
            B.act(c_rl, c_v, AF.Ln, bias=s_cb.ap[:, 0:1], sdep=[s_cb])
            B.act(c_rs, c_rl, AF.Exp, scale=-0.5)
            B.stt(c_b2, c_m, -1.0, c_rs, OP.mult, OP.mult)
            B.act(xlb, x, AF.Identity, scale=c_rs.ap, bias=c_b2.ap,
                  sdep=[c_rs, c_b2])
            for grp in range(4):
                pairs = [(xlt, xlt.ap[:, (grp * 8 + j) * 128:(grp * 8 + j + 1) * 128],
                          xlb, xlb.ap[:, (grp * 8 + j) * 128:(grp * 8 + j + 1) * 128])
                         for j in range(8)]
                B.trans8(psTR, pairs, s_idb)
            if kc_major:
                dstr = dst_d[:, :].rearrange("p (kc q) -> p kc q", q=1024)
                srcr = xlt.ap.rearrange("p (kc t) -> p kc t", t=128)
                for g4 in range(4):
                    B.dma("scalar", dst_tt.sub((tt, g4)), xlt,
                          oap=dstr[:, g4 * 8:(g4 + 1) * 8, tt * 128:(tt + 1) * 128],
                          iap=srcr[:, g4 * 8:(g4 + 1) * 8, :])
            else:
                B.dma("scalar", dst_tt.sub(tt), xlt,
                      oap=dst_d[:, tt * DM:(tt + 1) * DM], iap=xlt.ap)
        AR.bot = bot0

    ln_phase(d_x, xin, d_ag1i, ag1i, kc_major=False)
    B.do("gpsimd", [ag1i], [ag1o],
         lambda h: h.collective_compute(
             "AllGather", OP.bypass, replica_groups=[list(range(NCORES))],
             ins=[d_ag1i.ap().opt()], outs=[d_ag1o.ap().opt()]))

    def quant_tiles():
        for dsrc, wtt, dqd, ddq in ((d_w1, w1d, dq1d, d_dq1),
                                    (d_w2, w2d, dq2d, d_dq2)):
            for hc in range(OFC):
                for half in range(2):
                    yield ("w12", dsrc, wtt, dqd, ddq, hc, half)
        for dc in range(KC):
            yield ("w3", d_w3, w3d, dq3d, d_dq3, dc, 0)
    qgen = quant_tiles()
    qstate = {"i": 0, "done": False, "w12_left": 2 * OFC * 2}

    def emit_quant_tile(w12_only=False):
        if qstate["done"]:
            return False
        if w12_only and qstate["w12_left"] <= 0:
            return False
        try:
            wsel, dsrc, wtt, dqd, ddq, rc, half = next(qgen)
        except StopIteration:
            qstate["done"] = True
            return False
        if wsel == "w12":
            qstate["w12_left"] -= 1
        i = qstate["i"]; qstate["i"] += 1
        wbuf = qzw[0]
        ncol = 2048 if wsel == "w12" else HC
        nblk = ncol // 64
        B.dma("sync", wbuf, wtt, oap=wbuf.ap[:, 0:ncol],
              iap=dsrc[rc * 128:(rc + 1) * 128, half * 2048: half * 2048 + ncol])
        w3v = wbuf.ap[:, 0:ncol].rearrange("p (g e) -> p g e", e=64)
        B.red(qzam, wbuf, OP.max, oap=qzam.ap[:, 0:nblk], iap=w3v, absval=True)
        B.ts(qzam, qzam, 1e-30, None, OP.max, oap=qzam.ap[:, 0:nblk],
             iap=qzam.ap[:, 0:nblk])
        B.recip(qzrec, qzam, oap=qzrec.ap[:, 0:nblk], iap=qzam.ap[:, 0:nblk])
        rb = qzrec.ap[:, 0:nblk][:, :, None].broadcast_to([128, nblk, 64])
        B.tt(qzv, wbuf, qzrec, OP.mult,
             oap=qzv.ap[:, 0:ncol].rearrange("p (g e) -> p g e", e=64),
             aap=w3v, bap=rb)
        vv = qzv.ap[:, 0:ncol]
        first = True
        for bi in range(15):
            b_, dl = NF4_BOUNDS[bi], NF4_DELTAS[bi]
            if bi % 2 == 0:
                B.ts(qzt, qzv, b_, dl, OP.is_gt, OP.mult,
                     oap=qzt.ap[:, 0:ncol], iap=vv)
                if first:
                    B.copy("vector", qzacc, qzt, oap=qzacc.ap[:, 0:ncol],
                           iap=qzt.ap[:, 0:ncol])
                    first = False
                else:
                    B.tt(qzacc, qzacc, qzt, OP.add, oap=qzacc.ap[:, 0:ncol],
                         aap=qzacc.ap[:, 0:ncol], bap=qzt.ap[:, 0:ncol])
            else:
                B.act(qzt, qzv, AF.Sign, bias=s_cb.ap[:, 2 + bi // 2:3 + bi // 2],
                      oap=qzt.ap[:, 0:ncol], iap=vv, sdep=[s_cb])
                B.stt(qzacc, qzt, dl / 2.0, qzacc, OP.mult, OP.add,
                      oap=qzacc.ap[:, 0:ncol], i0ap=qzt.ap[:, 0:ncol],
                      i1ap=qzacc.ap[:, 0:ncol])
        offs = -1.0 + sum(NF4_DELTAS[bi] / 2.0 for bi in range(1, 15, 2))
        amb = qzam.ap[:, 0:nblk][:, :, None].broadcast_to([128, nblk, 64])
        B.stt(qzdq, qzacc, offs, qzam, OP.add, OP.mult,
              oap=qzdq.ap[:, 0:ncol].rearrange("p (g e) -> p g e", e=64),
              i0ap=qzacc.ap[:, 0:ncol].rearrange("p (g e) -> p g e", e=64),
              i1ap=amb)
        nj = ncol // 128
        for grp in range((nj + 7) // 8):
            jn = min(8, nj - grp * 8)
            fns = [lambda h, j=j, jj=grp * 8 + j: h.transpose(
                psQT.ap[:, j * 128:(j + 1) * 128],
                qzdq.ap[:, jj * 128:(jj + 1) * 128], s_idb.ap)
                for j in range(jn)]
            B.group("tensor", [qzdq, s_idb], [psQT], fns)
            B.copy("scalar", qztt, psQT,
                   oap=qztt.ap[:, grp * 1024:grp * 1024 + jn * 128],
                   iap=psQT.ap[:, 0:jn * 128])
        if wsel == "w12":
            dstr = ddq[:, :].rearrange("p (kc q) -> p kc q", q=HC)
            B.dma("scalar", dqd.sub((rc, half)), qztt,
                  oap=dstr[:, half * 16:(half + 1) * 16, rc * 128:(rc + 1) * 128],
                  iap=qztt.ap.rearrange("p (j t) -> p j t", t=128)[:, 0:16, :])
        else:
            dstr = ddq[:, :].rearrange("p (hc q) -> p hc q", q=DM)
            B.dma("scalar", dqd.sub((rc, 0)), qztt,
                  oap=dstr[:, 0:OFC, rc * 128:(rc + 1) * 128],
                  iap=qztt.ap.rearrange("p (j t) -> p j t", t=128)[:, 0:OFC, :])
        return True

    if not do_quant:
        qstate["done"] = True
    if upto >= 1:
        bot0 = AR.bot
        xfm = [AR.carve([128, DM], BF16) for _ in range(3)]

        def mk_stream():
            import types
            T = types.SimpleNamespace()
            T.qtok = AR.carve([128, 512], F32)
            T.ktok = AR.carve([128, 128], F32)
            T.vtok = [AR.carve([128, 128], BF16) for _ in range(2)]
            T.skfm = [AR.carve([128, 128], BF16) for _ in range(2)]
            T.qsn = AR.carve([128, 512], F32)
            T.sqf = AR.carve([128, 512], F32)
            T.rtmp = AR.carve([128, 512], F32)
            T.red = AR.carve([128, 16], F32)
            T.ksn = AR.carve([128, 128], F32)
            T.skf = AR.carve([128, 128], F32)
            T.gst = AR.carve([128, 128], F32)
            T.lsg = [AR.carve([128, 128], F32) for _ in range(2)]
            T.qsb = AR.carve([128, 512], BF16)
            T.sqb = AR.carve([128, 512], BF16)
            T.ksb = AR.carve([128, 128], BF16)
            T.gsb = AR.carve([128, 128], BF16)
            T.skb = AR.carve([128, 128], BF16)
            T.qsfm = AR.carve([128, 512], BF16)
            T.sqfm = AR.carve([128, 512], BF16)
            T.ksfm = AR.carve([128, 128], BF16)
            T.gsfm = AR.carve([128, 128], BF16)
            T.Gfm = AR.carve([128, 128], F32)
            T.tG = AR.carve([128, 128], F32)
            T.eG = AR.carve([128, 128], F32)
            T.eNG = AR.carve([128, 128], F32)
            T.e2t = AR.carve([128, 128], F32)
            T.qtfm = AR.carve([128, 512], BF16)
            T.kbfm = AR.carve([128, 128], BF16)
            T.khfm = AR.carve([128, 128], BF16)
            T.khtok = AR.carve([128, 128], BF16)
            T.Sf = AR.carve([128, 128], F32)
            T.Sb = AR.carve([128, 128], BF16)
            T.Ssc = AR.carve([128, 128], F32)
            T.pglat = AR.carve([128, 512], BF16)
            T.scm = AR.carve([128, 192], F32)
            T.pswa = AR.carve([128, 192], BF16)
            T.ptc = AR.carve([128, 128], BF16)
            T.ptp = AR.carve([128, 128], BF16)
            T.obf = AR.carve([128, 512], BF16)
            return T

        TS = [mk_stream() for _ in range(nb)]
        globals()["DBG_A"] = {"TS": TS, "xfm": xfm}

        def emit_chunk(b, m, T, x):
            cur, prv = m % 2, 1 - m % 2
            qtok, ktok, vtok, skfm = T.qtok, T.ktok, T.vtok, T.skfm
            qsn, sqf, rtmp, red = T.qsn, T.sqf, T.rtmp, T.red
            ksn, skf, gst, lsg = T.ksn, T.skf, T.gst, T.lsg
            qsb, sqb, ksb, gsb, skb = T.qsb, T.sqb, T.ksb, T.gsb, T.skb
            qsfm, sqfm, ksfm, gsfm = T.qsfm, T.sqfm, T.ksfm, T.gsfm
            Gfm, tG, eG, eNG, e2t = T.Gfm, T.tG, T.eG, T.eNG, T.e2t
            qtfm, kbfm, khfm, khtok = T.qtfm, T.kbfm, T.khfm, T.khtok
            Sf, Sb, Ssc = T.Sf, T.Sb, T.Ssc
            pglat, scm, pswa, ptc, ptp, obf = (T.pglat, T.scm, T.pswa, T.ptc,
                                               T.ptp, T.obf)
            t0 = b * N_ + m * 128
            rank, tti, tis = t0 // TPC, (t0 % TPC) // 128, t0 % TPC
            B.dma("sync", x, ag1o,
                  iap=d_ag1o[rank * 128:(rank + 1) * 128,
                             tti * DM:(tti + 1) * DM])
            xv = x.ap.rearrange("p (kc t) -> p kc t", t=128)
            B.group("tensor", [x, s_wq], [psQ],
                    [lambda h, kc=kc, xv=xv: h.matmul(
                        psQ.ap, lhsT=xv[:, kc, :],
                        rhs=s_wq.ap[:, kc * 512:(kc + 1) * 512],
                        start=(kc == 0), stop=(kc == KC - 1))
                     for kc in range(KC)])
            fns = [lambda h, kc=kc, xv=xv: h.matmul(
                psKV.ap[:, 0:128], lhsT=xv[:, kc, :],
                rhs=s_wk.ap[:, kc * 128:(kc + 1) * 128],
                start=(kc == 0), stop=(kc == KC - 1)) for kc in range(KC)]
            fns += [lambda h, kc=kc, xv=xv: h.matmul(
                psKV.ap[:, 128:256], lhsT=xv[:, kc, :],
                rhs=s_wv.ap[:, kc * 128:(kc + 1) * 128],
                start=(kc == 0), stop=(kc == KC - 1)) for kc in range(KC)]
            if CH_STAGE < 0.5:
                return
            B.group("tensor", [x, s_wk, s_wv], [psKV], fns)
            if CH_STAGE < 0.7:
                return
            B.copy("vector", qtok, psQ)
            if CH_STAGE < 0.82:
                return
            B.copy("vector", ktok, psKV, iap=psKV.ap[:, 0:128])
            if CH_STAGE < 0.84:
                return
            B.copy("vector", vtok[cur], psKV, iap=psKV.ap[:, 128:256])
            if CH_STAGE < 2:
                return
            q4 = qtok.ap.rearrange("p (hh t) -> p hh t", t=128)
            qn4 = qsn.ap.rearrange("p (hh t) -> p hh t", t=128)
            B.red(red, qtok, OP.max, oap=red.ap[:, 0:4], iap=q4, neg=True)
            nmx = red.ap[:, 0:4][:, :, None].broadcast_to([128, 4, 128])
            B.tt(qsn, qtok, red, OP.add, oap=qn4, aap=q4, bap=nmx)
            B.act(qsn, qsn, AF.Exp)
            B.red(red, qsn, OP.add, oap=red.ap[:, 4:8], iap=qn4)
            B.recip(red, red, oap=red.ap[:, 8:12], iap=red.ap[:, 4:8])
            rec4 = red.ap[:, 8:12][:, :, None].broadcast_to([128, 4, 128])
            B.tt(qsb, qsn, red, OP.mult,
                 oap=qsb.ap.rearrange("p (hh t) -> p hh t", t=128),
                 aap=qn4, bap=rec4)
            B.red(red, ktok, OP.max, oap=red.ap[:, 12:13], iap=ktok.ap,
                  neg=True)
            B.act(ksn, ktok, AF.Exp, bias=red.ap[:, 12:13],
                  accum=red.ap[:, 13:14], sdep=[red], aout=red)
            B.recip(red, red, oap=red.ap[:, 14:15], iap=red.ap[:, 13:14])
            B.ts(ksb, ksn, red.ap[:, 14:15], 1.0, OP.mult, OP.mult,
                 sdep=[red])
            B.act(lsg[0], ktok, AF.Abs)
            B.act(lsg[1], lsg[0], AF.Exp, scale=-1.0)
            B.act(lsg[1], lsg[1], AF.Ln, bias=s_cb.ap[:, 1:2],
                  sdep=[s_cb])
            B.ts(gst, ktok, 0.0, None, OP.min)
            B.tt(gsb, gst, lsg[1], OP.subtract)
            if CH_STAGE < 3:
                return
            cosc = s_cos.ap[:, m * 128:(m + 1) * 128]
            sinc = s_sin.ap[:, m * 128:(m + 1) * 128]
            cb = cosc[:, None, :].broadcast_to([128, 4, 128])
            s4 = sqf.ap.rearrange("p (hh t) -> p hh t", t=128)
            B.tt(sqf, qtok, s_cos, OP.mult, oap=s4, aap=q4, bap=cb)
            q4h = qtok.ap.rearrange("p (hh hf t) -> p hh hf t", hf=2, t=64)
            r4h = rtmp.ap.rearrange("p (hh hf t) -> p hh hf t", hf=2, t=64)
            sinh = sinc.rearrange("p (hf t) -> p hf t", t=64)
            for hf in range(2):
                sv = sinh[:, hf, :][:, None, :].broadcast_to([128, 4, 64])
                B.tt(rtmp, qtok, s_sin, OP.mult, oap=r4h[:, :, hf, :],
                     aap=q4h[:, :, 1 - hf, :], bap=sv)
            B.tt(sqb, sqf, rtmp, OP.add)
            B.tt(skf, ktok, s_cos, OP.mult, bap=cosc)
            for hf in range(2):
                B.tt(rtmp, ktok, s_sin, OP.mult,
                     oap=rtmp.ap[:, hf * 64:(hf + 1) * 64],
                     aap=ktok.ap[:, (1 - hf) * 64:(2 - hf) * 64],
                     bap=sinh[:, hf, :])
            B.tt(skb, skf, rtmp, OP.add, bap=rtmp.ap[:, 0:128])
            if CH_STAGE < 4:
                return
            B.trans8(psTR,
                     [(qsfm, qsfm.ap[:, j * 128:(j + 1) * 128],
                       qsb, qsb.ap[:, j * 128:(j + 1) * 128])
                      for j in range(4)] +
                     [(sqfm, sqfm.ap[:, j * 128:(j + 1) * 128],
                       sqb, sqb.ap[:, j * 128:(j + 1) * 128])
                      for j in range(4)], s_idb)
            B.trans8(psTR, [(ksfm, ksfm.ap, ksb, ksb.ap),
                            (skfm[cur], skfm[cur].ap, skb, skb.ap),
                            (gsfm, gsfm.ap, gsb, gsb.ap)], s_idb)
            if CH_STAGE < 5:
                return
            B.scan_add(Gfm, gsfm)
            B.act(eG, Gfm, AF.Exp, scale=1.0 / 16.0)
            B.act(eNG, Gfm, AF.Exp, scale=-1.0 / 16.0)
            B.ts(tG, Gfm, Gfm.ap[:, 127:128], 1.0, OP.subtract, OP.mult,
                 sdep=[Gfm])
            B.act(e2t, tG, AF.Exp, scale=-1.0 / 16.0)
            eGb = eG.ap[:, None, :].broadcast_to([128, 4, 128])
            B.stt(qtfm, qsfm, 0.5, eG, OP.mult, OP.mult,
                  oap=qtfm.ap.rearrange("p (hh t) -> p hh t", t=128),
                  i0ap=qsfm.ap.rearrange("p (hh t) -> p hh t", t=128),
                  i1ap=eGb)
            B.tt(kbfm, ksfm, eNG, OP.mult)
            B.tt(khfm, ksfm, e2t, OP.mult)
            B.trans8(psTR, [(khtok, khtok.ap, khfm, khfm.ap)], s_idb)
            if CH_STAGE < 6:
                return
            B.group("tensor", [kbfm, qtfm], [psP],
                    [lambda h, j=j: h.matmul(
                        psP.ap[:, j * 128:(j + 1) * 128], lhsT=kbfm.ap,
                        rhs=qtfm.ap[:, j * 128:(j + 1) * 128],
                        start=True, stop=True) for j in range(4)])
            u4 = s_u01.ap[:, None, :].broadcast_to([128, 4, 128])
            B.tt(pglat, psP, s_u01, OP.mult,
                 oap=pglat.ap.rearrange("p (hh t) -> p hh t", t=128),
                 aap=psP.ap.rearrange("p (hh t) -> p hh t", t=128), bap=u4)
            if CH_STAGE < 7:
                return
            for hh in range(4):
                osli = psO.ap[:, hh * 128:(hh + 1) * 128]
                fns = []
                ins = [vtok[cur], pglat]
                if m > 0:
                    fns.append(lambda h, o=osli, hh=hh: h.matmul(
                        o, lhsT=Sb.ap,
                        rhs=qtfm.ap[:, hh * 128:(hh + 1) * 128],
                        start=True, stop=False))
                    ins += [Sb, qtfm]
                fns.append(lambda h, o=osli, hh=hh, v=vtok[cur].ap, m=m:
                           h.matmul(o, lhsT=v,
                                    rhs=pglat.ap[:, hh * 128:(hh + 1) * 128],
                                    start=(m == 0), stop=False))
                B.group("tensor", ins, [psO], fns)
                if CH_STAGE < 8:
                    continue
                fns = [lambda h, hh=hh, sk=skfm[cur].ap, m=m: h.matmul(
                    psSc.ap[:, 64:192],
                    lhsT=sqfm.ap[:, hh * 128:(hh + 1) * 128],
                    rhs=sk, start=True, stop=(m == 0))]
                ins2 = [sqfm, skfm[cur]]
                if m > 0:
                    fns.append(lambda h, hh=hh, skp=skfm[prv].ap: h.matmul(
                        psSc.ap[:, 0:64],
                        lhsT=sqfm.ap[:, hh * 128:(hh + 1) * 128],
                        rhs=skp[:, 64:128], start=False, stop=True))
                    ins2.append(skfm[prv])
                B.group("tensor", ins2, [psSc], fns)
                lo = 64 if m == 0 else 0
                B.tt(scm, psSc, s_mask, OP.add, oap=scm.ap[:, lo:192],
                     aap=psSc.ap[:, lo:192], bap=s_mask.ap[:, lo:192])
                B.red(red, scm, OP.max, oap=red.ap[:, 0:1],
                      iap=scm.ap[:, lo:192], neg=True)
                B.ts(red, red, INV_SQRT_D, None, OP.mult,
                     oap=red.ap[:, 1:2], iap=red.ap[:, 0:1])
                B.act(scm, scm, AF.Exp, scale=INV_SQRT_D,
                      bias=red.ap[:, 1:2], accum=red.ap[:, 2:3],
                      oap=scm.ap[:, lo:192], iap=scm.ap[:, lo:192],
                      sdep=[red], aout=red)
                B.recip(red, red, oap=red.ap[:, 3:4], iap=red.ap[:, 2:3])
                B.ts(pswa, scm, red.ap[:, 3:4], 0.5, OP.mult, OP.mult,
                     oap=pswa.ap[:, lo:192], iap=scm.ap[:, lo:192],
                     sdep=[red])
                tl = [(ptc, ptc.ap, pswa, pswa.ap[:, 64:192])]
                if m > 0:
                    tl.append((ptp, ptp.ap[64:128, :], pswa, pswa.ap[:, 0:64]))
                B.trans8(psTR, tl, s_idb)
                fns = [lambda h, o=osli, v=vtok[cur].ap, m=m: h.matmul(
                    o, lhsT=v, rhs=ptc.ap, start=False, stop=(m == 0))]
                ins3 = [vtok[cur], ptc]
                if m > 0:
                    fns.append(lambda h, o=osli, vp=vtok[prv].ap: h.matmul(
                        o, lhsT=vp[64:128, :],
                        rhs=ptp.ap[64:128, :], start=False, stop=True))
                    ins3 += [vtok[prv], ptp]
                B.group("tensor", ins3, [psO], fns)
            if CH_STAGE < 9:
                return
            B.copy("vector", obf, psO)
            dsto = d_osl[rank * 128:(rank + 1) * 128, :].rearrange(
                "p (hh c) -> p hh c", c=TPC)
            B.dma("scalar", osl.sub((b, m)), obf,
                  oap=dsto[:, :, tis:tis + 128],
                  iap=obf.ap.rearrange("p (hh t) -> p hh t", t=128))
            if m > 0:
                B.act(Ssc, Sf, AF.Identity, scale=eG.ap[:, 127:128],
                      sdep=[eG])
            B.group("tensor", [khtok, vtok[cur]], [psS],
                    [lambda h, v=vtok[cur].ap: h.matmul(
                        psS.ap[:, 0:128], lhsT=khtok.ap, rhs=v,
                        start=True, stop=True)])
            if m > 0:
                B.tt(Sf, psS, Ssc, OP.add, aap=psS.ap[:, 0:128])
            else:
                B.copy("vector", Sf, psS, iap=psS.ap[:, 0:128])
            B.copy("scalar", Sb, Sf)
        qi = 0
        for m in range(nchunks):
            for b in range(nb):
                emit_chunk(b, m, TS[b], xfm[qi % 3])
                qi += 1
            emit_quant_tile()

        while emit_quant_tile():
            pass
        AR.bot = bot0
        if do_a2a:
            B.do("gpsimd", [osl], [oex],
                 lambda h: _a2a(h, d_osl, d_oex))

    if upto >= 2:
        B.barrier()
        AR.bot = 0
        AR.top = top_quant
        ores = AR.carve([128, KC * 1024], BF16)
        woq = AR.carve([128, KC * 512], BF16)
        xq = [AR.carve([128, 512], F32) for _ in range(2)]
        x2t = [AR.carve([128, 512], F32) for _ in range(2)]
        for hidx in range(KC):
            B.dma("sync", ores.sub(hidx), oex,
                  oap=ores.ap[:, hidx * 1024:(hidx + 1) * 1024],
                  iap=d_oex[(hidx // 4) * 128:(hidx // 4 + 1) * 128,
                            (hidx % 4) * 1024:((hidx % 4) + 1) * 1024])
        for e in range(8):
            for kc in range(KC):
                B.dma("gpsimd", woq.sub((e, kc)), wod,
                      oap=woq.ap[:, kc * 512:(kc + 1) * 512],
                      iap=d_wo[kc * 128:(kc + 1) * 128,
                               e * 512:(e + 1) * 512])
            for tt in range(8):
                emit_quant_tile()
                xg = xq[tt % 2]
                B.dma("sync", xg, xin,
                      iap=d_x[tt * 128:(tt + 1) * 128, e * 512:(e + 1) * 512])
                bank = psQ if tt % 2 == 0 else psKV
                B.group("tensor", [ores, woq], [bank],
                        [lambda h, kc=kc, bank=bank, tt=tt:
                         h.matmul(bank.ap,
                                  lhsT=ores.ap[:, kc * 1024 + tt * 128:
                                               kc * 1024 + (tt + 1) * 128],
                                  rhs=woq.ap[:, kc * 512:(kc + 1) * 512],
                                  start=(kc == 0), stop=(kc == KC - 1))
                         for kc in range(KC)])
                xo = x2t[tt % 2]
                B.tt(xo, bank, xg, OP.add, aap=bank.ap, bap=xg.ap, bdep=[xg])
                B.dma("scalar", x2d.sub((e, tt)), xo,
                      oap=d_x2[tt * 128:(tt + 1) * 128, e * 512:(e + 1) * 512])
        B.barrier()
        AR.bot = 0
        while emit_quant_tile():
            pass
        ln_phase(d_x2, x2d, d_ag2i, ag2i, kc_major=True)
        B.do("gpsimd", [ag2i], [ag2o],
             lambda h: h.collective_compute(
                 "AllGather", OP.bypass, replica_groups=[list(range(NCORES))],
                 ins=[d_ag2i.ap().opt()], outs=[d_ag2o.ap().opt()]))

    if upto >= 3:
        B.barrier()
        AR.bot = 0
        AR.top = AR.nbytes
        sdq = AR.carve([128, KC * HC], BF16)
        xsp = [AR.carve([128, KC * 512], BF16) for _ in range(2)]
        gtile = AR.carve([128, 512], BF16)
        sgt = AR.carve([128, 512], F32)
        htile = AR.carve([128, 512], BF16)

        def up_pass(ddq, dqtt, is_up):
            B.dma("sync", sdq, dqtt, iap=ddq[:, :])
            for s in range(16):
                xg = xsp[s % 2]
                r = s // 2
                off = (s % 2) * 512
                srcr = d_ag2o[r * 128:(r + 1) * 128, :].rearrange(
                    "p (kc q) -> p kc q", q=1024)
                for g8 in range(4):
                    B.dma("sync", xg.sub(g8), ag2o,
                          oap=xg.ap.rearrange("p (kc q) -> p kc q", q=512)
                          [:, g8 * 8:(g8 + 1) * 8, :],
                          iap=srcr[:, g8 * 8:(g8 + 1) * 8, off:off + 512])
                for of in range(OFC):
                    bank = banks5[of % 5]
                    B.group("tensor", [sdq, xg], [bank],
                            [lambda h, kc=kc, of=of, bank=bank, xg=xg: h.matmul(
                                bank.ap,
                                lhsT=sdq.ap[:, kc * HC + of * 128:
                                            kc * HC + (of + 1) * 128],
                                rhs=xg.ap[:, kc * 512:(kc + 1) * 512],
                                start=(kc == 0), stop=(kc == KC - 1))
                             for kc in range(KC)])
                    col = of * TOKS + s * 512
                    if not is_up:
                        B.copy("vector", gtile, bank)
                        B.dma("scalar", gated.sub((of, s)), gtile,
                              oap=d_gate[:, col:col + 512])
                    else:
                        B.dma("sync", gtile, gated,
                              iap=d_gate[:, col:col + 512])
                        B.act(sgt, gtile, AF.Silu)
                        B.tt(htile, bank, sgt, OP.mult, aap=bank.ap)
                        B.dma("scalar", hd.sub((of, s)), htile,
                              oap=d_h[:, col:col + 512])

        up_pass(d_dq1, dq1d, is_up=False)
        up_pass(d_dq2, dq2d, is_up=True)

        B.barrier()
        AR.bot = 0
        sdq3 = AR.carve([128, OFC * DM], BF16)
        hb = [AR.carve([128, OFC * 128], BF16) for _ in range(2)]
        pt = [AR.carve([128, 1024], F32) for _ in range(2)]
        B.dma("sync", sdq3, dq3d, iap=d_dq3[:, :])
        for q in range(4):
            if q > 0 and DOWN_QBAR:
                B.barrier()
            for tt2 in range(64):
                hg = hb[tt2 % 2]
                B.dma("sync", hg, hd,
                      oap=hg.ap.rearrange("p (hc t) -> p hc t", t=128),
                      iap=d_h[:, :].rearrange("p (hc q2) -> p hc q2", q2=TOKS)
                      [:, :, tt2 * 128:(tt2 + 1) * 128])
                for half, bank in ((0, psQ), (1, psKV)):
                    B.group("tensor", [sdq3, hg], [bank],
                            [lambda h, hc=hc, half=half, bank=bank, hg=hg, q=q:
                             h.matmul(bank.ap,
                                      lhsT=hg.ap[:, hc * 128:(hc + 1) * 128],
                                      rhs=sdq3.ap[:, hc * DM + q * 1024 + half * 512:
                                                  hc * DM + q * 1024 + (half + 1) * 512],
                                      start=(hc == 0), stop=(hc == OFC - 1))
                             for hc in range(OFC)])
                po = pt[tt2 % 2]
                B.copy("vector", po, psQ, oap=po.ap[:, 0:512])
                B.copy("vector", po, psKV, oap=po.ap[:, 512:1024])
                B.dma("scalar", partd[q].sub(tt2), po,
                      oap=d_part[q][tt2 * 128:(tt2 + 1) * 128, :])
            B.do("gpsimd", [partd[q]], [rsod[q]],
                 lambda h, q=q: h.collective_compute(
                     "ReduceScatter", OP.add,
                     replica_groups=[list(range(NCORES))],
                     ins=[d_part[q].ap().opt()], outs=[d_rso[q].ap().opt()]))

    if upto >= 4:
        B.barrier()
        AR.bot = 0
        x2f = [AR.carve([128, DM], F32) for _ in range(2)]
        yf = [AR.carve([128, DM], F32) for _ in range(2)]
        of_ = [AR.carve([128, DM], F32) for _ in range(2)]
        for tt in range(8):
            xg, yg, og = x2f[tt % 2], yf[tt % 2], of_[tt % 2]
            B.dma("sync", xg, x2d, iap=d_x2[tt * 128:(tt + 1) * 128, :])
            for q in range(4):
                B.dma("sync", yg.sub(q), rsod[q],
                      oap=yg.ap[:, q * 1024:(q + 1) * 1024],
                      iap=d_rso[q][tt * 128:(tt + 1) * 128, :])
            B.tt(og, xg, yg, OP.add)
            B.dma("scalar", outd.sub(tt), og,
                  oap=d_out[tt * 128:(tt + 1) * 128, :])

    B.barrier()
    tapmap = {"ag1i": (d_ag1i, ag1i), "ag1o": (d_ag1o, ag1o),
              "osl": (d_osl, osl), "oex": (d_oex, oex),
              "x2": (d_x2, x2d), "ag2i": (d_ag2i, ag2i), "ag2o": (d_ag2o, ag2o),
              "dq1": (d_dq1, dq1d), "dq2": (d_dq2, dq2d), "dq3": (d_dq3, dq3d),
              "gate": (d_gate, gated), "h": (d_h, hd),
              "part0": (d_part[0], partd[0]), "rso0": (d_rso[0], rsod[0]),
              "rso1": (d_rso[1], rsod[1]), "rso2": (d_rso[2], rsod[2]),
              "rso3": (d_rso[3], rsod[3]), "part1": (d_part[1], partd[1])}
    for name in taps:
        src, stt = tapmap[name]
        tout = nc.dram_tensor(f"tap_{name}", list(src.shape), src.dtype,
                              kind="ExternalOutput")
        B.dma("gpsimd", TT(tout, name=f"tap{name}"), stt,
              oap=tout[:, :], iap=src[:, :])
    for s in B.all_sems:
        if s.n:
            B.wait("gpsimd", (s, s.n, None))
    B.run_block()
    B.close()
    return nc


def _a2a(h, d_in, d_out):
    with allow_ici_all_to_all():
        return h.collective_compute(
            "AllToAll", OP.bypass, replica_groups=[list(range(NCORES))],
            ins=[d_in.ap().opt()], outs=[d_out.ap().opt()])


def _constants():
    pos = np.arange(N_, dtype=np.float64)
    half = np.arange(64, dtype=np.float64)
    inv = 1.0 / (1e4 ** (2.0 * half / 128.0))
    fr = pos[:, None] * inv[None, :]
    emb = np.concatenate([fr, fr], axis=-1)
    cos = np.cos(emb)
    sin = np.sin(emb)
    sin2 = sin.copy()
    sin2[:, 0:64] = -sin[:, 0:64]
    cos_t = cos.reshape(NCHUNK, 128, 128).transpose(1, 0, 2).reshape(128, NCHUNK * 128)
    sin_t = sin2.reshape(NCHUNK, 128, 128).transpose(1, 0, 2).reshape(128, NCHUNK * 128)
    ii = np.arange(128)[:, None]
    jj = np.arange(192)[None, :]
    rel = ii - (jj - 64)
    mask = np.where((rel >= 0) & (rel <= 64), 0.0, -1e30)
    jm = np.arange(128)
    u01 = (jm[:, None] <= jm[None, :]).astype(np.float32)
    cb = np.zeros((128, 16), np.float32)
    cb[:, 0] = 1e-5
    cb[:, 1] = 1.0
    for bi in range(1, 15, 2):
        cb[:, 2 + bi // 2] = -NF4_BOUNDS[bi]
    return (cos_t.astype(np.float32), sin_t.astype(np.float32),
            mask.astype(np.float32), u01, np.eye(128, dtype=ml_dtypes.bfloat16),
            cb)


def prepare_inputs(inputs):
    x = np.ascontiguousarray(
        inputs["L0_attn_hidden_states"].reshape(TOKS, DM).astype(np.float32))
    Wq = inputs["L0_attn_Wq"].astype(np.float32)
    Wk = inputs["L0_attn_Wk"].astype(np.float32)
    Wv = inputs["L0_attn_Wv"].astype(np.float32)
    Wo = np.ascontiguousarray(inputs["L0_attn_Wo"].astype(np.float32))
    w1 = inputs["L0_ffn_w1"].astype(np.float32)
    w2 = inputs["L0_ffn_w2"].astype(np.float32)
    w3 = inputs["L0_ffn_w3"].astype(np.float32)
    w1p = np.vstack([w1, np.zeros((HC * NCORES - w1.shape[0], DM), np.float32)])
    w2p = np.vstack([w2, np.zeros((HC * NCORES - w2.shape[0], DM), np.float32)])
    w3p = np.hstack([w3, np.zeros((DM, HC * NCORES - w3.shape[1]), np.float32)])
    cos_t, sin_t, mask, u01, idb, cb = _constants()
    in_maps = []
    for c in range(NCORES):
        in_maps.append({
            "xop": np.ascontiguousarray(x[c * TPC:(c + 1) * TPC]),
            "wq_sl": np.ascontiguousarray(Wq[:, c * NH * DH:(c + 1) * NH * DH]),
            "wk_sl": np.ascontiguousarray(Wk[:, c * DH:(c + 1) * DH]),
            "wv_sl": np.ascontiguousarray(Wv[:, c * DH:(c + 1) * DH]),
            "wo": Wo,
            "w1_sl": np.ascontiguousarray(w1p[c * HC:(c + 1) * HC]),
            "w2_sl": np.ascontiguousarray(w2p[c * HC:(c + 1) * HC]),
            "w3_sl": np.ascontiguousarray(w3p[:, c * HC:(c + 1) * HC]),
            "cos_t": cos_t, "sin2_t": sin_t, "mask_swa": mask,
            "u01": u01, "identb": idb, "cbias": cb,
        })
    return in_maps


_CACHED = {}


def kernel(**inputs):
    in_maps = prepare_inputs(inputs)
    if "nc" not in _CACHED:
        _CACHED["nc"] = build_program(upto=9)
    nc = _CACHED["nc"]
    res = run_bass_kernel_spmd(nc, in_maps, core_ids=list(range(NCORES)))
    out = np.concatenate([res.results[c]["out_own"] for c in range(NCORES)],
                         axis=0)
    return out.reshape(B_, N_, DM)


# version 05
# speedup vs baseline: 1.0006x; 1.0006x over previous
import numpy as np
import ml_dtypes
import concourse.bass as bass
import concourse.mybir as mybir
from concourse.bass_utils import run_bass_kernel_spmd
from concourse.aot_env import allow_ici_all_to_all

F32 = mybir.dt.float32
BF16 = mybir.dt.bfloat16
FP16 = mybir.dt.float16
OP = mybir.AluOpType
AX = mybir.AxisListType
AF = mybir.ActivationFunctionType

NCORES = 8
B_ = 2
N_ = 4096
DM = 4096
TOKS = B_ * N_
TPC = TOKS // NCORES
NH = 4
DH = 128
NCHUNK = 32
HC = 1408
OFC = HC // 128
KC = DM // 128
INV_SQRT_D = 0.08838834764831845

_NF4 = np.array([-1.0, -0.6961928009986877, -0.5250730514526367,
    -0.39491748809814453, -0.28444138169288635, -0.18477343022823334,
    -0.09105003625154495, 0.0, 0.07958029955625534, 0.16093020141124725,
    0.24611230194568634, 0.33791524171829224, 0.44070982933044434,
    0.5626170039176941, 0.7229568362236023, 1.0], dtype=np.float64)
NF4_BOUNDS = ((_NF4[:-1] + _NF4[1:]) * 0.5).tolist()
NF4_DELTAS = np.diff(_NF4).tolist()

ENGINES = ("sync", "tensor", "scalar", "vector", "gpsimd")
CH_STAGE = 99
DOWN_QBAR = False


class Sem:
    def __init__(self, h):
        self.h = h
        self.n = 0
        self.is_dma = False


class Tile:
    __slots__ = ("wr", "readers", "name", "subs", "parent", "dsem")

    def __init__(self, name=""):
        self.wr = None
        self.readers = []
        self.name = name
        self.subs = {}
        self.parent = None
        self.dsem = None


class TT:
    __slots__ = ("ap", "t", "bf")

    def __init__(self, ap, tile=None, name=""):
        self.ap = ap
        self.t = tile if tile is not None else Tile(name)

    def v(self, ap):
        return TT(ap, self.t)

    def sub(self, key):
        st = self.t.subs.get(key)
        if st is None:
            st = Tile(f"{self.t.name}:{key}")
            st.parent = self.t
            self.t.subs[key] = st
        return TT(self.ap, st)


class Builder:
    def __init__(self, nc):
        self.nc = nc
        self.prog = {e: [] for e in ENGINES}
        self.waited = {e: {} for e in ENGINES}
        self._cms = []
        self.qL = self.sem("qL")
        self.qS = self.sem("qS")
        self.qG = self.sem("qG")
        self.cPE = self.sem("cPE")
        self.cACT = self.sem("cACT")
        self.cDVE = self.sem("cDVE")
        self.cGP = self.sem("cGP")
        self.chain = {"tensor": self.cPE, "scalar": self.cACT,
                      "vector": self.cDVE, "gpsimd": self.cGP, "sync": self.cGP}
        self.qsem = {"sync": self.qL, "scalar": self.qS, "gpsimd": self.qG}
        self.all_sems = [self.qL, self.qS, self.qG, self.cPE, self.cACT,
                         self.cDVE, self.cGP]
        self._dsem_n = 0

    def _dma_sem(self, tile):
        top = tile.parent if tile.parent is not None else tile
        if top.dsem is None:
            top.dsem = self.sem(f"d{self._dsem_n}")
            top.dsem.is_dma = True
            self._dsem_n += 1
            self.all_sems.append(top.dsem)
        return top.dsem

    def sem(self, name):
        cm = self.nc.semaphore(name)
        h = cm.__enter__()
        self._cms.append(cm)
        return Sem(h)

    def close(self):
        for cm in reversed(self._cms):
            cm.__exit__(None, None, None)

    def emit(self, eng, fn):
        self.prog[eng].append(fn)

    def wait(self, eng, ev):
        sem, val, peng = ev
        if sem.is_dma:
            val = sem.n
        if val <= 0 or self.waited[eng].get(sem, 0) >= val:
            return
        if eng == "tensor" and peng == "tensor":
            return
        self.waited[eng][sem] = val
        self.emit(eng, lambda h, s=sem.h, v=val: h.wait_ge(s, v))

    def _pre(self, eng, ins, outs):
        for t in ins:
            if t.wr is not None:
                self.wait(eng, t.wr)
        for t in outs:
            for ev in t.readers:
                self.wait(eng, ev)
            if t.wr is not None:
                self.wait(eng, t.wr)
            if t.parent is not None:
                for ev in t.parent.readers:
                    self.wait(eng, ev)

    def _post(self, eng, ins, outs, sem):
        ev = (sem, sem.n, eng)
        for t in ins:
            t.readers.append(ev)
        for t in outs:
            t.wr = ev
            t.readers = []
            if t.parent is not None:
                t.parent.wr = ev

    def do(self, eng, ins, outs, fn, dma=False):
        ins = [x.t if isinstance(x, TT) else x for x in ins]
        outs = [x.t if isinstance(x, TT) else x for x in outs]
        self._pre(eng, ins, outs)
        if dma:
            sem = self._dma_sem(outs[0])
        else:
            sem = self.chain[eng]
        k = 16 if dma else 1
        sem.n += k
        self.emit(eng, lambda h, f=fn, s=sem.h, kk=k: f(h).then_inc(s, kk))
        self._post(eng, ins, outs, sem)

    def group(self, eng, ins, outs, fns):
        ins = [x.t if isinstance(x, TT) else x for x in ins]
        outs = [x.t if isinstance(x, TT) else x for x in outs]
        self._pre(eng, ins, outs)
        sem = self.chain[eng]
        sem.n += 1

        def run(h, fs=tuple(fns), s=sem.h):
            last = None
            for f in fs:
                last = f(h)
            last.then_inc(s, 1)
        self.emit(eng, run)
        self._post(eng, ins, outs, sem)

    def barrier(self):
        evs = [(s, s.n, None) for s in self.all_sems]
        for e in ENGINES:
            for ev in evs:
                self.wait(e, ev)

    def run_block(self):
        with self.nc.Block() as block:
            @block.sync
            def _(h):
                for fn in self.prog["sync"]:
                    fn(h)

            @block.tensor
            def _(h):
                for fn in self.prog["tensor"]:
                    fn(h)

            @block.scalar
            def _(h):
                for fn in self.prog["scalar"]:
                    fn(h)

            @block.vector
            def _(h):
                for fn in self.prog["vector"]:
                    fn(h)

            @block.gpsimd
            def _(h):
                for fn in self.prog["gpsimd"]:
                    fn(h)

    def dma(self, eng, out, in_, oap=None, iap=None):
        o = oap if oap is not None else out.ap
        i = iap if iap is not None else in_.ap
        self.do(eng, [in_], [out], lambda h, o=o, i=i: h.dma_start(out=o, in_=i),
                dma=True)

    def copy(self, eng, out, in_, oap=None, iap=None):
        o = oap if oap is not None else out.ap
        i = iap if iap is not None else in_.ap
        if eng == "vector":
            self.do(eng, [in_], [out], lambda h, o=o, i=i: h.tensor_copy(o, i))
        else:
            self.do(eng, [in_], [out],
                    lambda h, o=o, i=i: h.activation(o, i, AF.Identity))

    def tt(self, out, a, b, op, oap=None, aap=None, bap=None, bdep=()):
        o = oap if oap is not None else out.ap
        x = aap if aap is not None else a.ap
        y = bap if bap is not None else (b.ap if isinstance(b, TT) else b)
        ins = [a] + ([b] if isinstance(b, TT) else []) + list(bdep)
        self.do("vector", ins, [out],
                lambda h, o=o, x=x, y=y, op=op: h.tensor_tensor(o, x, y, op))

    def ts(self, out, in_, s1, s2, op0, op1=None, oap=None, iap=None, sdep=()):
        o = oap if oap is not None else out.ap
        i = iap if iap is not None else in_.ap
        if op1 is None:
            self.do("vector", [in_, *sdep], [out],
                    lambda h, o=o, i=i, s1=s1, op0=op0:
                    h.tensor_single_scalar(o, i, s1, op0))
        else:
            self.do("vector", [in_, *sdep], [out],
                    lambda h, o=o, i=i, s1=s1, s2=s2, op0=op0, op1=op1:
                    h.tensor_scalar(o, i, s1, s2, op0, op1))

    def stt(self, out, in0, s, in1, op0, op1, oap=None, i0ap=None, i1ap=None,
            sdep=()):
        o = oap if oap is not None else out.ap
        a = i0ap if i0ap is not None else in0.ap
        b = i1ap if i1ap is not None else in1.ap
        self.do("vector", [in0, in1, *sdep], [out],
                lambda h, o=o, a=a, b=b, s=s, op0=op0, op1=op1:
                h.scalar_tensor_tensor(o, a, s, b, op0, op1))

    def red(self, out, in_, op, oap=None, iap=None, absval=False, neg=False):
        o = oap if oap is not None else out.ap
        i = iap if iap is not None else in_.ap
        self.do("vector", [in_], [out],
                lambda h, o=o, i=i, op=op, a=absval, n=neg: h.tensor_reduce(
                    o, i, axis=AX.X, op=op, apply_absolute_value=a, negate=n))

    def recip(self, out, in_, oap=None, iap=None):
        o = oap if oap is not None else out.ap
        i = iap if iap is not None else in_.ap
        self.do("vector", [in_], [out], lambda h, o=o, i=i: h.reciprocal(o, i))

    def act(self, out, in_, func, bias=0.0, scale=1.0, accum=None, oap=None,
            iap=None, sdep=(), aout=None):
        o = oap if oap is not None else out.ap
        i = iap if iap is not None else in_.ap
        outs = [out] + ([aout] if aout is not None else [])
        self.do("scalar", [in_, *sdep], outs,
                lambda h, o=o, i=i, f=func, b=bias, s=scale, a=accum:
                h.activation(o, i, f, bias=b, scale=s, accum_out=a))

    def scan_add(self, out, in_, oap=None, iap=None):
        o = oap if oap is not None else out.ap
        i = iap if iap is not None else in_.ap
        self.do("vector", [in_], [out],
                lambda h, o=o, i=i: h.tensor_tensor_scan(o, i, i, 0.0, OP.add,
                                                         OP.bypass))

    def trans8(self, psb, pairs, idb):
        fns = []
        ins = [s for (_, _, s, _) in pairs] + [idb]
        for j, (_, dap, _, sap) in enumerate(pairs):
            w = sap.shape[-1]
            p0 = dap.base_partition()
            fns.append(lambda h, o=psb.ap[p0:p0 + w, j * 128:(j + 1) * 128],
                       i=sap, idn=idb.ap: h.transpose(o, i, idn))
        self.group("tensor", ins, [psb], fns)
        for j, (dst, dap, _, sap) in enumerate(pairs):
            w = sap.shape[-1]
            p0 = dap.base_partition()
            self.copy("scalar", dst.v(dap) if dap is not dst.ap else dst, psb,
                      oap=dap, iap=psb.ap[p0:p0 + w, j * 128:(j + 1) * 128])


class Arena:
    def __init__(self, nc, name, nbytes):
        self.nbytes = nbytes
        self.t = nc.alloc_sbuf_tensor(name, [128, nbytes // 2], BF16)
        self.bot = 0
        self.top = nbytes

    def carve(self, shape, dtype, top=False):
        esz = {F32: 4, BF16: 2, FP16: 2}[dtype]
        nfree = int(np.prod(shape[1:]))
        nb = (nfree * esz + 63) & ~63
        if top:
            self.top -= nb
            off = self.top
        else:
            off = self.bot
            self.bot += nb
        assert self.bot <= self.top, f"arena overflow bot={self.bot} top={self.top}"
        base = self.t[:][:, off // 2: off // 2 + nfree * esz // 2]
        ap = base.bitcast(dtype) if dtype != BF16 else base
        if len(shape) > 2:
            dims = " ".join(f"d{i}" for i in range(1, len(shape)))
            ap = ap.rearrange(f"p ({dims}) -> p {dims}",
                              **{f"d{i}": shape[i] for i in range(1, len(shape))})
        return TT(ap, name=f"ar@{off}")


def build_program(upto=9, taps=(), nb=B_, nchunks=NCHUNK, do_quant=True, do_a2a=True):
    nc = bass.Bass("TRN2", target_bir_lowering=False)
    B = Builder(nc)

    d_x = nc.dram_tensor("xop", [TPC, DM], F32, kind="ExternalInput")
    d_wq = nc.dram_tensor("wq_sl", [DM, NH * DH], F32, kind="ExternalInput")
    d_wk = nc.dram_tensor("wk_sl", [DM, DH], F32, kind="ExternalInput")
    d_wv = nc.dram_tensor("wv_sl", [DM, DH], F32, kind="ExternalInput")
    d_wo = nc.dram_tensor("wo", [DM, DM], F32, kind="ExternalInput")
    d_w1 = nc.dram_tensor("w1_sl", [HC, DM], F32, kind="ExternalInput")
    d_w2 = nc.dram_tensor("w2_sl", [HC, DM], F32, kind="ExternalInput")
    d_w3 = nc.dram_tensor("w3_sl", [DM, HC], F32, kind="ExternalInput")
    d_cos = nc.dram_tensor("cos_t", [128, NCHUNK * DH], F32, kind="ExternalInput")
    d_sin = nc.dram_tensor("sin2_t", [128, NCHUNK * DH], F32, kind="ExternalInput")
    d_mask = nc.dram_tensor("mask_swa", [128, 192], F32, kind="ExternalInput")
    d_u01 = nc.dram_tensor("u01", [128, 128], F32, kind="ExternalInput")
    d_idb = nc.dram_tensor("identb", [128, 128], BF16, kind="ExternalInput")
    d_cb = nc.dram_tensor("cbias", [128, 16], F32, kind="ExternalInput")
    d_out = nc.dram_tensor("out_own", [TPC, DM], F32, kind="ExternalOutput")

    d_ag1i = nc.dram_tensor("ag1i", [128, 32768], BF16)
    d_ag1o = nc.dram_tensor("ag1o", [1024, 32768], BF16)
    d_osl = nc.dram_tensor("osl", [1024, NH * TPC], BF16)
    d_oex = nc.dram_tensor("oex", [1024, NH * TPC], BF16)
    d_x2 = nc.dram_tensor("x2own", [TPC, DM], F32)
    d_ag2i = nc.dram_tensor("ag2i", [128, 32768], BF16)
    d_ag2o = nc.dram_tensor("ag2o", [1024, 32768], BF16)
    d_dq1 = nc.dram_tensor("dq1T", [128, KC * HC], BF16)
    d_dq2 = nc.dram_tensor("dq2T", [128, KC * HC], BF16)
    d_dq3 = nc.dram_tensor("dq3T", [128, OFC * DM], BF16)
    d_gate = nc.dram_tensor("gateb", [128, OFC * TOKS], BF16)
    d_h = nc.dram_tensor("hb", [128, OFC * TOKS], BF16)
    d_part = [nc.dram_tensor(f"part{q}", [TOKS, 1024], F32) for q in range(4)]
    d_rso = [nc.dram_tensor(f"rso{q}", [TPC, 1024], F32) for q in range(4)]

    ag1i = TT(d_ag1i, name="ag1i"); ag1o = TT(d_ag1o, name="ag1o")
    osl = TT(d_osl, name="osl"); oex = TT(d_oex, name="oex")
    x2d = TT(d_x2, name="x2"); ag2i = TT(d_ag2i, name="ag2i")
    ag2o = TT(d_ag2o, name="ag2o")
    dq1d = TT(d_dq1, name="dq1"); dq2d = TT(d_dq2, name="dq2")
    dq3d = TT(d_dq3, name="dq3")
    gated = TT(d_gate, name="gate"); hd = TT(d_h, name="h")
    partd = [TT(d_part[q], name=f"part{q}") for q in range(4)]
    rsod = [TT(d_rso[q], name=f"rso{q}") for q in range(4)]
    outd = TT(d_out, name="out")
    xin = TT(d_x, name="xin"); wod = TT(d_wo, name="wo")
    w1d = TT(d_w1, name="w1"); w2d = TT(d_w2, name="w2"); w3d = TT(d_w3, name="w3")

    s_cos = TT(nc.alloc_sbuf_tensor("s_cos", [128, NCHUNK * DH], F32)[:])
    s_sin = TT(nc.alloc_sbuf_tensor("s_sin", [128, NCHUNK * DH], F32)[:])
    s_mask = TT(nc.alloc_sbuf_tensor("s_mask", [128, 192], F32)[:])
    s_u01 = TT(nc.alloc_sbuf_tensor("s_u01", [128, 128], F32)[:])
    s_idb = TT(nc.alloc_sbuf_tensor("s_idb", [128, 128], BF16)[:])
    s_cb = TT(nc.alloc_sbuf_tensor("s_cb", [128, 16], F32)[:])
    AR = Arena(nc, "AR", 163 * 1024)

    psQs = [TT(nc.alloc_psum_tensor(f"psQ{i}", [128, 512], F32)[:])
            for i in range(2)]
    psKVs = [TT(nc.alloc_psum_tensor(f"psKV{i}", [128, 512], F32)[:])
             for i in range(2)]
    psOs = [TT(nc.alloc_psum_tensor(f"psO{i}", [128, 512], F32)[:])
            for i in range(2)]
    psWs = [TT(nc.alloc_psum_tensor(f"psW{i}", [128, 512], F32)[:])
            for i in range(2)]
    for w in psWs:
        w.bf = w.ap.bitcast(BF16)
    psQ, psKV, psO = psQs[0], psKVs[0], psOs[0]
    psTR = TT.__new__(TT); psTR.ap = psWs[0].bf; psTR.t = psWs[0].t
    psQT = psTR
    banks5 = [psQs[0], psKVs[0], psOs[0], psQs[1], psKVs[1]]

    B.dma("sync", s_cos, TT(d_cos, name="cosi"), iap=d_cos[:, :])
    B.dma("sync", s_sin, TT(d_sin, name="sini"), iap=d_sin[:, :])
    B.dma("sync", s_mask, TT(d_mask, name="mski"), iap=d_mask[:, :])
    B.dma("sync", s_u01, TT(d_u01, name="u01i"), iap=d_u01[:, :])
    B.dma("sync", s_idb, TT(d_idb, name="idbi"), iap=d_idb[:, :])
    B.dma("sync", s_cb, TT(d_cb, name="cbi"), iap=d_cb[:, :])

    qzw = [AR.carve([128, 2048], F32, top=True) for _ in range(1)]
    qzv = AR.carve([128, 2048], F32, top=True)
    qzacc = AR.carve([128, 2048], FP16, top=True)
    qzt = AR.carve([128, 2048], FP16, top=True)
    qzam = AR.carve([128, 64], F32, top=True)
    qzrec = AR.carve([128, 64], F32, top=True)
    qzdq = AR.carve([128, 2048], BF16, top=True)
    qztt = AR.carve([128, 2048], BF16, top=True)
    top_quant = AR.top
    s_wq = AR.carve([128, KC * NH * DH], BF16, top=True)
    s_wk = AR.carve([128, KC * DH], BF16, top=True)
    s_wv = AR.carve([128, KC * DH], BF16, top=True)
    wq_r = d_wq[:, :].rearrange("(kc p) f -> p kc f", p=128)
    wqin = TT(d_wq, name="wqi"); wkin = TT(d_wk, name="wki")
    wvin = TT(d_wv, name="wvi")
    wq_s = s_wq.ap.rearrange("p (kc f) -> p kc f", f=NH * DH)
    for j in range(4):
        B.dma("gpsimd", s_wq, wqin, oap=wq_s[:, j * 8:(j + 1) * 8, :],
              iap=wq_r[:, j * 8:(j + 1) * 8, :])
    B.dma("gpsimd", s_wk, wkin,
          oap=s_wk.ap.rearrange("p (kc f) -> p kc f", f=DH),
          iap=d_wk[:, :].rearrange("(kc p) f -> p kc f", p=128))
    B.dma("gpsimd", s_wv, wvin,
          oap=s_wv.ap.rearrange("p (kc f) -> p kc f", f=DH),
          iap=d_wv[:, :].rearrange("(kc p) f -> p kc f", p=128))


    def ln_phase(src_d, src_tt, dst_d, dst_tt, kc_major):
        bot0 = AR.bot
        xt = [AR.carve([128, DM], F32) for _ in range(2)]
        sq = AR.carve([128, DM], F32)
        st = AR.carve([128, 16], F32)
        xlb = AR.carve([128, DM], BF16)
        xlt = AR.carve([128, DM], BF16)
        c_s1, c_s2, c_m, c_mm, c_v, c_rs, c_rl, c_b2 = (
            st.v(st.ap[:, i:i + 1]) for i in range(8))
        for tt in range(8):
            x = xt[tt % 2]
            B.dma("sync", x, src_tt, iap=src_d[tt * 128:(tt + 1) * 128, :])
            B.red(c_s1, x, OP.add)
            B.act(sq, x, AF.Square, accum=c_s2.ap, aout=c_s2)
            B.ts(c_m, c_s1, 1.0 / DM, None, OP.mult)
            B.ts(c_v, c_s2, 1.0 / DM, None, OP.mult)
            B.tt(c_mm, c_m, c_m, OP.mult)
            B.tt(c_v, c_v, c_mm, OP.subtract)
            B.act(c_rl, c_v, AF.Ln, bias=s_cb.ap[:, 0:1], sdep=[s_cb])
            B.act(c_rs, c_rl, AF.Exp, scale=-0.5)
            B.stt(c_b2, c_m, -1.0, c_rs, OP.mult, OP.mult)
            B.act(xlb, x, AF.Identity, scale=c_rs.ap, bias=c_b2.ap,
                  sdep=[c_rs, c_b2])
            for grp in range(4):
                pairs = [(xlt, xlt.ap[:, (grp * 8 + j) * 128:(grp * 8 + j + 1) * 128],
                          xlb, xlb.ap[:, (grp * 8 + j) * 128:(grp * 8 + j + 1) * 128])
                         for j in range(8)]
                B.trans8(psTR, pairs, s_idb)
            if kc_major:
                dstr = dst_d[:, :].rearrange("p (kc q) -> p kc q", q=1024)
                srcr = xlt.ap.rearrange("p (kc t) -> p kc t", t=128)
                for g4 in range(4):
                    B.dma("scalar", dst_tt.sub((tt, g4)), xlt,
                          oap=dstr[:, g4 * 8:(g4 + 1) * 8, tt * 128:(tt + 1) * 128],
                          iap=srcr[:, g4 * 8:(g4 + 1) * 8, :])
            else:
                B.dma("scalar", dst_tt.sub(tt), xlt,
                      oap=dst_d[:, tt * DM:(tt + 1) * DM], iap=xlt.ap)
        AR.bot = bot0

    ln_phase(d_x, xin, d_ag1i, ag1i, kc_major=False)
    B.do("gpsimd", [ag1i], [ag1o],
         lambda h: h.collective_compute(
             "AllGather", OP.bypass, replica_groups=[list(range(NCORES))],
             ins=[d_ag1i.ap().opt()], outs=[d_ag1o.ap().opt()]))

    def quant_tiles():
        for dsrc, wtt, dqd, ddq in ((d_w1, w1d, dq1d, d_dq1),
                                    (d_w2, w2d, dq2d, d_dq2)):
            for hc in range(OFC):
                for half in range(2):
                    yield ("w12", dsrc, wtt, dqd, ddq, hc, half)
        for dc in range(KC):
            yield ("w3", d_w3, w3d, dq3d, d_dq3, dc, 0)
    qgen = quant_tiles()
    qstate = {"i": 0, "done": False, "w12_left": 2 * OFC * 2}

    def emit_quant_tile(w12_only=False):
        if qstate["done"]:
            return False
        if w12_only and qstate["w12_left"] <= 0:
            return False
        try:
            wsel, dsrc, wtt, dqd, ddq, rc, half = next(qgen)
        except StopIteration:
            qstate["done"] = True
            return False
        if wsel == "w12":
            qstate["w12_left"] -= 1
        i = qstate["i"]; qstate["i"] += 1
        wbuf = qzw[0]
        ncol = 2048 if wsel == "w12" else HC
        nblk = ncol // 64
        B.dma("sync", wbuf, wtt, oap=wbuf.ap[:, 0:ncol],
              iap=dsrc[rc * 128:(rc + 1) * 128, half * 2048: half * 2048 + ncol])
        w3v = wbuf.ap[:, 0:ncol].rearrange("p (g e) -> p g e", e=64)
        B.red(qzam, wbuf, OP.max, oap=qzam.ap[:, 0:nblk], iap=w3v, absval=True)
        B.ts(qzam, qzam, 1e-30, None, OP.max, oap=qzam.ap[:, 0:nblk],
             iap=qzam.ap[:, 0:nblk])
        B.recip(qzrec, qzam, oap=qzrec.ap[:, 0:nblk], iap=qzam.ap[:, 0:nblk])
        rb = qzrec.ap[:, 0:nblk][:, :, None].broadcast_to([128, nblk, 64])
        B.tt(qzv, wbuf, qzrec, OP.mult,
             oap=qzv.ap[:, 0:ncol].rearrange("p (g e) -> p g e", e=64),
             aap=w3v, bap=rb)
        vv = qzv.ap[:, 0:ncol]
        first = True
        for bi in range(15):
            b_, dl = NF4_BOUNDS[bi], NF4_DELTAS[bi]
            if bi % 2 == 0:
                B.ts(qzt, qzv, b_, dl, OP.is_gt, OP.mult,
                     oap=qzt.ap[:, 0:ncol], iap=vv)
                if first:
                    B.copy("vector", qzacc, qzt, oap=qzacc.ap[:, 0:ncol],
                           iap=qzt.ap[:, 0:ncol])
                    first = False
                else:
                    B.tt(qzacc, qzacc, qzt, OP.add, oap=qzacc.ap[:, 0:ncol],
                         aap=qzacc.ap[:, 0:ncol], bap=qzt.ap[:, 0:ncol])
            else:
                B.act(qzt, qzv, AF.Sign, bias=s_cb.ap[:, 2 + bi // 2:3 + bi // 2],
                      oap=qzt.ap[:, 0:ncol], iap=vv, sdep=[s_cb])
                B.stt(qzacc, qzt, dl / 2.0, qzacc, OP.mult, OP.add,
                      oap=qzacc.ap[:, 0:ncol], i0ap=qzt.ap[:, 0:ncol],
                      i1ap=qzacc.ap[:, 0:ncol])
        offs = -1.0 + sum(NF4_DELTAS[bi] / 2.0 for bi in range(1, 15, 2))
        amb = qzam.ap[:, 0:nblk][:, :, None].broadcast_to([128, nblk, 64])
        B.stt(qzdq, qzacc, offs, qzam, OP.add, OP.mult,
              oap=qzdq.ap[:, 0:ncol].rearrange("p (g e) -> p g e", e=64),
              i0ap=qzacc.ap[:, 0:ncol].rearrange("p (g e) -> p g e", e=64),
              i1ap=amb)
        nj = ncol // 128
        for grp in range((nj + 7) // 8):
            jn = min(8, nj - grp * 8)
            fns = [lambda h, j=j, jj=grp * 8 + j: h.transpose(
                psQT.ap[:, j * 128:(j + 1) * 128],
                qzdq.ap[:, jj * 128:(jj + 1) * 128], s_idb.ap)
                for j in range(jn)]
            B.group("tensor", [qzdq, s_idb], [psQT], fns)
            B.copy("scalar", qztt, psQT,
                   oap=qztt.ap[:, grp * 1024:grp * 1024 + jn * 128],
                   iap=psQT.ap[:, 0:jn * 128])
        if wsel == "w12":
            dstr = ddq[:, :].rearrange("p (kc q) -> p kc q", q=HC)
            B.dma("scalar", dqd.sub((rc, half)), qztt,
                  oap=dstr[:, half * 16:(half + 1) * 16, rc * 128:(rc + 1) * 128],
                  iap=qztt.ap.rearrange("p (j t) -> p j t", t=128)[:, 0:16, :])
        else:
            dstr = ddq[:, :].rearrange("p (hc q) -> p hc q", q=DM)
            B.dma("scalar", dqd.sub((rc, 0)), qztt,
                  oap=dstr[:, 0:OFC, rc * 128:(rc + 1) * 128],
                  iap=qztt.ap.rearrange("p (j t) -> p j t", t=128)[:, 0:OFC, :])
        return True

    if not do_quant:
        qstate["done"] = True
    if upto >= 1:
        bot0 = AR.bot
        xfm = [AR.carve([128, DM], BF16) for _ in range(3)]

        def mk_stream():
            import types
            T = types.SimpleNamespace()
            T.qtok = AR.carve([128, 512], F32)
            T.ktok = AR.carve([128, 128], F32)
            T.vtok = [AR.carve([128, 128], BF16) for _ in range(2)]
            T.skfm = [AR.carve([128, 128], BF16) for _ in range(2)]
            T.red = AR.carve([128, 16], F32)
            T.ksn = AR.carve([128, 128], F32)
            T.skf = AR.carve([128, 128], F32)
            T.gst = AR.carve([128, 128], F32)
            T.lsg = [AR.carve([128, 128], F32) for _ in range(2)]
            T.sqf = AR.carve([128, 512], F32)
            T.rtmp = AR.carve([128, 512], F32)
            T.qsb = AR.carve([128, 512], BF16)
            T.sqb = AR.carve([128, 512], BF16)
            T.ksb = AR.carve([128, 128], BF16)
            T.gsb = AR.carve([128, 128], BF16)
            T.skb = AR.carve([128, 128], BF16)
            T.qsfm = AR.carve([128, 512], BF16)
            T.sqfm = AR.carve([128, 512], BF16)
            T.ksfm = AR.carve([128, 128], BF16)
            T.gsfm = AR.carve([128, 128], BF16)
            T.Gfm = AR.carve([128, 128], F32)
            T.tG = AR.carve([128, 128], F32)
            T.eG = AR.carve([128, 128], F32)
            T.eNG = AR.carve([128, 128], F32)
            T.e2t = AR.carve([128, 128], F32)
            T.qtfm = AR.carve([128, 512], BF16)
            T.kbfm = AR.carve([128, 128], BF16)
            T.khfm = AR.carve([128, 128], BF16)
            T.khtok = AR.carve([128, 128], BF16)
            T.Sf = AR.carve([128, 128], F32)
            T.Sb = AR.carve([128, 128], BF16)
            T.Ssc = AR.carve([128, 128], F32)
            T.pglat = AR.carve([128, 512], BF16)
            T.scm = AR.carve([128, 192], F32)
            T.pswa = AR.carve([128, 192], BF16)
            T.ptc = AR.carve([128, 128], BF16)
            T.ptp = AR.carve([128, 128], BF16)
            T.obf = AR.carve([128, 512], BF16)
            return T

        TS = [mk_stream() for _ in range(nb)]
        globals()["DBG_A"] = {"TS": TS, "xfm": xfm}
        psWb = []
        for w in psWs:
            v = TT.__new__(TT); v.ap = w.bf; v.t = w.t
            psWb.append(v)

        def st0(b, m, T, x):
            psQb, psKVb = psQs[b], psKVs[b]
            t0 = b * N_ + m * 128
            rank, tti = t0 // TPC, (t0 % TPC) // 128
            B.dma("sync", x, ag1o,
                  iap=d_ag1o[rank * 128:(rank + 1) * 128,
                             tti * DM:(tti + 1) * DM])
            xv = x.ap.rearrange("p (kc t) -> p kc t", t=128)
            B.group("tensor", [x, s_wq], [psQb],
                    [lambda h, kc=kc, xv=xv, o=psQb.ap: h.matmul(
                        o, lhsT=xv[:, kc, :],
                        rhs=s_wq.ap[:, kc * 512:(kc + 1) * 512],
                        start=(kc == 0), stop=(kc == KC - 1))
                     for kc in range(KC)])
            fns = [lambda h, kc=kc, xv=xv, o=psKVb.ap: h.matmul(
                o[:, 0:128], lhsT=xv[:, kc, :],
                rhs=s_wk.ap[:, kc * 128:(kc + 1) * 128],
                start=(kc == 0), stop=(kc == KC - 1)) for kc in range(KC)]
            fns += [lambda h, kc=kc, xv=xv, o=psKVb.ap: h.matmul(
                o[:, 128:256], lhsT=xv[:, kc, :],
                rhs=s_wv.ap[:, kc * 128:(kc + 1) * 128],
                start=(kc == 0), stop=(kc == KC - 1)) for kc in range(KC)]
            B.group("tensor", [x, s_wk, s_wv], [psKVb], fns)
            B.copy("vector", T.qtok, psQb)
            B.copy("vector", T.ktok, psKVb, iap=psKVb.ap[:, 0:128])
            B.copy("vector", T.vtok[m % 2], psKVb, iap=psKVb.ap[:, 128:256])

        def st1(b, m, T):
            qtok, ktok, red = T.qtok, T.ktok, T.red
            q4 = qtok.ap.rearrange("p (hh t) -> p hh t", t=128)
            B.red(red, qtok, OP.max, oap=red.ap[:, 0:4], iap=q4, neg=True)
            for hh in range(4):
                B.act(T.qsb, qtok, AF.Exp, bias=red.ap[:, hh:hh + 1],
                      accum=red.ap[:, 4 + hh:5 + hh],
                      oap=T.qsb.ap[:, hh * 128:(hh + 1) * 128],
                      iap=qtok.ap[:, hh * 128:(hh + 1) * 128],
                      sdep=[red], aout=red)
            B.recip(red, red, oap=red.ap[:, 8:12], iap=red.ap[:, 4:8])
            for hh in range(4):
                B.act(T.qsb, T.qsb, AF.Identity, scale=red.ap[:, 8 + hh:9 + hh],
                      oap=T.qsb.ap[:, hh * 128:(hh + 1) * 128],
                      iap=T.qsb.ap[:, hh * 128:(hh + 1) * 128], sdep=[red])
            B.red(red, ktok, OP.max, oap=red.ap[:, 12:13], iap=ktok.ap, neg=True)
            B.act(T.ksn, ktok, AF.Exp, bias=red.ap[:, 12:13],
                  accum=red.ap[:, 13:14], sdep=[red], aout=red)
            B.recip(red, red, oap=red.ap[:, 14:15], iap=red.ap[:, 13:14])
            B.act(T.ksb, T.ksn, AF.Identity, scale=red.ap[:, 14:15], sdep=[red])
            B.act(T.lsg[0], ktok, AF.Abs)
            B.act(T.lsg[1], T.lsg[0], AF.Exp, scale=-1.0)
            B.act(T.lsg[1], T.lsg[1], AF.Ln, bias=s_cb.ap[:, 1:2], sdep=[s_cb])
            B.ts(T.gst, ktok, 0.0, None, OP.min)
            B.tt(T.gsb, T.gst, T.lsg[1], OP.subtract)
            cosc = s_cos.ap[:, m * 128:(m + 1) * 128]
            sinc = s_sin.ap[:, m * 128:(m + 1) * 128]
            cb4 = cosc[:, None, :].broadcast_to([128, 4, 128])
            s4 = T.sqf.ap.rearrange("p (hh t) -> p hh t", t=128)
            B.tt(T.sqf, qtok, s_cos, OP.mult, oap=s4, aap=q4, bap=cb4)
            q4h = qtok.ap.rearrange("p (hh hf t) -> p hh hf t", hf=2, t=64)
            r4h = T.rtmp.ap.rearrange("p (hh hf t) -> p hh hf t", hf=2, t=64)
            sinh = sinc.rearrange("p (hf t) -> p hf t", t=64)
            for hf in range(2):
                sv = sinh[:, hf, :][:, None, :].broadcast_to([128, 4, 64])
                B.tt(T.rtmp, qtok, s_sin, OP.mult, oap=r4h[:, :, hf, :],
                     aap=q4h[:, :, 1 - hf, :], bap=sv)
            B.tt(T.sqb, T.sqf, T.rtmp, OP.add)
            B.tt(T.skf, ktok, s_cos, OP.mult, bap=cosc)
            for hf in range(2):
                B.tt(T.rtmp, ktok, s_sin, OP.mult,
                     oap=T.rtmp.ap[:, hf * 64:(hf + 1) * 64],
                     aap=ktok.ap[:, (1 - hf) * 64:(2 - hf) * 64],
                     bap=sinh[:, hf, :])
            B.tt(T.skb, T.skf, T.rtmp, OP.add, bap=T.rtmp.ap[:, 0:128])

        def st2(b, m, T):
            W = psWb[b]
            cur = m % 2
            B.trans8(W, [(T.qsfm, T.qsfm.ap[:, j * 128:(j + 1) * 128],
                          T.qsb, T.qsb.ap[:, j * 128:(j + 1) * 128])
                         for j in range(4)] +
                        [(T.sqfm, T.sqfm.ap[:, j * 128:(j + 1) * 128],
                          T.sqb, T.sqb.ap[:, j * 128:(j + 1) * 128])
                         for j in range(4)], s_idb)
            B.trans8(W, [(T.ksfm, T.ksfm.ap, T.ksb, T.ksb.ap),
                         (T.skfm[cur], T.skfm[cur].ap, T.skb, T.skb.ap),
                         (T.gsfm, T.gsfm.ap, T.gsb, T.gsb.ap)], s_idb)
            B.scan_add(T.Gfm, T.gsfm)
            B.act(T.eG, T.Gfm, AF.Exp, scale=1.0 / 16.0)
            B.act(T.eNG, T.Gfm, AF.Exp, scale=-1.0 / 16.0)
            B.ts(T.tG, T.Gfm, T.Gfm.ap[:, 127:128], 1.0, OP.subtract, OP.mult,
                 sdep=[T.Gfm])
            B.act(T.e2t, T.tG, AF.Exp, scale=-1.0 / 16.0)
            eGb = T.eG.ap[:, None, :].broadcast_to([128, 4, 128])
            B.stt(T.qtfm, T.qsfm, 0.5, T.eG, OP.mult, OP.mult,
                  oap=T.qtfm.ap.rearrange("p (hh t) -> p hh t", t=128),
                  i0ap=T.qsfm.ap.rearrange("p (hh t) -> p hh t", t=128),
                  i1ap=eGb)
            B.tt(T.kbfm, T.ksfm, T.eNG, OP.mult)
            B.tt(T.khfm, T.ksfm, T.e2t, OP.mult)
            B.trans8(W, [(T.khtok, T.khtok.ap, T.khfm, T.khfm.ap)], s_idb)

        def st3p(b, m, T):
            W = psWs[b]
            B.group("tensor", [T.kbfm, T.qtfm], [W],
                    [lambda h, j=j, o=W.ap, kb=T.kbfm.ap, qt=T.qtfm.ap:
                     h.matmul(o[:, j * 128:(j + 1) * 128], lhsT=kb,
                              rhs=qt[:, j * 128:(j + 1) * 128],
                              start=True, stop=True) for j in range(4)])
            u4 = s_u01.ap[:, None, :].broadcast_to([128, 4, 128])
            B.tt(T.pglat, W, s_u01, OP.mult,
                 oap=T.pglat.ap.rearrange("p (hh t) -> p hh t", t=128),
                 aap=W.ap.rearrange("p (hh t) -> p hh t", t=128), bap=u4)

        def st3h(b, m, T, hh):
            W, Wv, Ob = psWs[b], psWb[b], psOs[b]
            cur, prv = m % 2, 1 - m % 2
            osli = Ob.ap[:, hh * 128:(hh + 1) * 128]
            fns = []
            ins = [T.vtok[cur], T.pglat]
            if m > 0:
                fns.append(lambda h, o=osli, hh=hh, sb=T.Sb.ap, qt=T.qtfm.ap:
                           h.matmul(o, lhsT=sb,
                                    rhs=qt[:, hh * 128:(hh + 1) * 128],
                                    start=True, stop=False))
                ins += [T.Sb, T.qtfm]
            fns.append(lambda h, o=osli, hh=hh, v=T.vtok[cur].ap,
                       pg=T.pglat.ap, m=m: h.matmul(
                           o, lhsT=v, rhs=pg[:, hh * 128:(hh + 1) * 128],
                           start=(m == 0), stop=False))
            B.group("tensor", ins, [Ob], fns)
            fns = [lambda h, hh=hh, sk=T.skfm[cur].ap, sq=T.sqfm.ap,
                   o=W.ap, m=m: h.matmul(
                       o[:, 64:192], lhsT=sq[:, hh * 128:(hh + 1) * 128],
                       rhs=sk, start=True, stop=(m == 0))]
            ins2 = [T.sqfm, T.skfm[cur]]
            if m > 0:
                fns.append(lambda h, hh=hh, skp=T.skfm[prv].ap,
                           sq=T.sqfm.ap, o=W.ap: h.matmul(
                               o[:, 0:64], lhsT=sq[:, hh * 128:(hh + 1) * 128],
                               rhs=skp[:, 64:128], start=False, stop=True))
                ins2.append(T.skfm[prv])
            B.group("tensor", ins2, [W], fns)
            lo = 64 if m == 0 else 0
            red = T.red
            B.tt(T.scm, W, s_mask, OP.add, oap=T.scm.ap[:, lo:192],
                 aap=W.ap[:, lo:192], bap=s_mask.ap[:, lo:192])
            B.red(red, T.scm, OP.max, oap=red.ap[:, 0:1],
                  iap=T.scm.ap[:, lo:192], neg=True)
            B.ts(red, red, INV_SQRT_D, None, OP.mult,
                 oap=red.ap[:, 1:2], iap=red.ap[:, 0:1])
            B.act(T.scm, T.scm, AF.Exp, scale=INV_SQRT_D,
                  bias=red.ap[:, 1:2], accum=red.ap[:, 2:3],
                  oap=T.scm.ap[:, lo:192], iap=T.scm.ap[:, lo:192],
                  sdep=[red], aout=red)
            B.recip(red, red, oap=red.ap[:, 3:4], iap=red.ap[:, 2:3])
            B.ts(T.pswa, T.scm, red.ap[:, 3:4], 0.5, OP.mult, OP.mult,
                 oap=T.pswa.ap[:, lo:192], iap=T.scm.ap[:, lo:192],
                 sdep=[red])
            tl = [(T.ptc, T.ptc.ap, T.pswa, T.pswa.ap[:, 64:192])]
            if m > 0:
                tl.append((T.ptp, T.ptp.ap[64:128, :], T.pswa,
                           T.pswa.ap[:, 0:64]))
            B.trans8(Wv, tl, s_idb)
            fns = [lambda h, o=osli, v=T.vtok[cur].ap, pt=T.ptc.ap, m=m:
                   h.matmul(o, lhsT=v, rhs=pt, start=False, stop=(m == 0))]
            ins3 = [T.vtok[cur], T.ptc]
            if m > 0:
                fns.append(lambda h, o=osli, vp=T.vtok[prv].ap, pt=T.ptp.ap:
                           h.matmul(o, lhsT=vp[64:128, :], rhs=pt[64:128, :],
                                    start=False, stop=True))
                ins3 += [T.vtok[prv], T.ptp]
            B.group("tensor", ins3, [Ob], fns)

        def st4(b, m, T):
            W, Ob = psWs[b], psOs[b]
            cur = m % 2
            t0 = b * N_ + m * 128
            rank, tis = t0 // TPC, t0 % TPC
            B.copy("vector", T.obf, Ob)
            dsto = d_osl[rank * 128:(rank + 1) * 128, :].rearrange(
                "p (hh c) -> p hh c", c=TPC)
            B.dma("scalar", osl.sub((b, m)), T.obf,
                  oap=dsto[:, :, tis:tis + 128],
                  iap=T.obf.ap.rearrange("p (hh t) -> p hh t", t=128))
            if m > 0:
                B.act(T.Ssc, T.Sf, AF.Identity, scale=T.eG.ap[:, 127:128],
                      sdep=[T.eG])
            B.group("tensor", [T.khtok, T.vtok[cur]], [W],
                    [lambda h, v=T.vtok[cur].ap, kh=T.khtok.ap, o=W.ap:
                     h.matmul(o[:, 0:128], lhsT=kh, rhs=v,
                              start=True, stop=True)])
            if m > 0:
                B.tt(T.Sf, W, T.Ssc, OP.add, aap=W.ap[:, 0:128])
            else:
                B.copy("vector", T.Sf, W, iap=W.ap[:, 0:128])
            B.copy("scalar", T.Sb, T.Sf)

        qi = 0
        for m in range(nchunks):
            for b in range(nb):
                st0(b, m, TS[b], xfm[(m * nb + b) % 3])
            for b in range(nb):
                st1(b, m, TS[b])
            for b in range(nb):
                st2(b, m, TS[b])
            for b in range(nb):
                st3p(b, m, TS[b])
            for hh in range(4):
                for b in range(nb):
                    st3h(b, m, TS[b], hh)
            for b in range(nb):
                st4(b, m, TS[b])
            emit_quant_tile(w12_only=True)
        while emit_quant_tile(w12_only=True):
            pass
        AR.bot = bot0
        if do_a2a:
            B.do("gpsimd", [osl], [oex],
                 lambda h: _a2a(h, d_osl, d_oex))

    if upto >= 2:
        B.barrier()
        AR.bot = 0
        AR.top = top_quant
        ores = AR.carve([128, KC * 1024], BF16)
        woq = AR.carve([128, KC * 512], BF16)
        xq = [AR.carve([128, 512], F32) for _ in range(2)]
        x2t = [AR.carve([128, 512], F32) for _ in range(2)]
        for hidx in range(KC):
            B.dma("sync", ores.sub(hidx), oex,
                  oap=ores.ap[:, hidx * 1024:(hidx + 1) * 1024],
                  iap=d_oex[(hidx // 4) * 128:(hidx // 4 + 1) * 128,
                            (hidx % 4) * 1024:((hidx % 4) + 1) * 1024])
        for e in range(8):
            for kc in range(KC):
                B.dma("gpsimd", woq.sub((e, kc)), wod,
                      oap=woq.ap[:, kc * 512:(kc + 1) * 512],
                      iap=d_wo[kc * 128:(kc + 1) * 128,
                               e * 512:(e + 1) * 512])
            for tt in range(8):
                emit_quant_tile()
                xg = xq[tt % 2]
                B.dma("sync", xg, xin,
                      iap=d_x[tt * 128:(tt + 1) * 128, e * 512:(e + 1) * 512])
                bank = psQ if tt % 2 == 0 else psKV
                B.group("tensor", [ores, woq], [bank],
                        [lambda h, kc=kc, bank=bank, tt=tt:
                         h.matmul(bank.ap,
                                  lhsT=ores.ap[:, kc * 1024 + tt * 128:
                                               kc * 1024 + (tt + 1) * 128],
                                  rhs=woq.ap[:, kc * 512:(kc + 1) * 512],
                                  start=(kc == 0), stop=(kc == KC - 1))
                         for kc in range(KC)])
                xo = x2t[tt % 2]
                B.tt(xo, bank, xg, OP.add, aap=bank.ap, bap=xg.ap, bdep=[xg])
                B.dma("scalar", x2d.sub((e, tt)), xo,
                      oap=d_x2[tt * 128:(tt + 1) * 128, e * 512:(e + 1) * 512])
        B.barrier()
        AR.bot = 0
        while emit_quant_tile():
            pass
        ln_phase(d_x2, x2d, d_ag2i, ag2i, kc_major=True)
        B.do("gpsimd", [ag2i], [ag2o],
             lambda h: h.collective_compute(
                 "AllGather", OP.bypass, replica_groups=[list(range(NCORES))],
                 ins=[d_ag2i.ap().opt()], outs=[d_ag2o.ap().opt()]))

    if upto >= 3:
        B.barrier()
        AR.bot = 0
        AR.top = AR.nbytes
        sdq = AR.carve([128, KC * HC], BF16)
        xsp = [AR.carve([128, KC * 512], BF16) for _ in range(2)]
        gtile = AR.carve([128, 512], BF16)
        sgt = AR.carve([128, 512], F32)
        htile = AR.carve([128, 512], BF16)

        def up_pass(ddq, dqtt, is_up):
            B.dma("sync", sdq, dqtt, iap=ddq[:, :])
            for s in range(16):
                xg = xsp[s % 2]
                r = s // 2
                off = (s % 2) * 512
                srcr = d_ag2o[r * 128:(r + 1) * 128, :].rearrange(
                    "p (kc q) -> p kc q", q=1024)
                for g8 in range(4):
                    B.dma("sync", xg.sub(g8), ag2o,
                          oap=xg.ap.rearrange("p (kc q) -> p kc q", q=512)
                          [:, g8 * 8:(g8 + 1) * 8, :],
                          iap=srcr[:, g8 * 8:(g8 + 1) * 8, off:off + 512])
                for of in range(OFC):
                    bank = banks5[of % 5]
                    B.group("tensor", [sdq, xg], [bank],
                            [lambda h, kc=kc, of=of, bank=bank, xg=xg: h.matmul(
                                bank.ap,
                                lhsT=sdq.ap[:, kc * HC + of * 128:
                                            kc * HC + (of + 1) * 128],
                                rhs=xg.ap[:, kc * 512:(kc + 1) * 512],
                                start=(kc == 0), stop=(kc == KC - 1))
                             for kc in range(KC)])
                    col = of * TOKS + s * 512
                    if not is_up:
                        B.copy("vector", gtile, bank)
                        B.dma("scalar", gated.sub((of, s)), gtile,
                              oap=d_gate[:, col:col + 512])
                    else:
                        B.dma("sync", gtile, gated,
                              iap=d_gate[:, col:col + 512])
                        B.act(sgt, gtile, AF.Silu)
                        B.tt(htile, bank, sgt, OP.mult, aap=bank.ap)
                        B.dma("scalar", hd.sub((of, s)), htile,
                              oap=d_h[:, col:col + 512])

        up_pass(d_dq1, dq1d, is_up=False)
        up_pass(d_dq2, dq2d, is_up=True)

        B.barrier()
        AR.bot = 0
        sdq3 = AR.carve([128, OFC * DM], BF16)
        hb = [AR.carve([128, OFC * 128], BF16) for _ in range(2)]
        pt = [AR.carve([128, 1024], F32) for _ in range(2)]
        B.dma("sync", sdq3, dq3d, iap=d_dq3[:, :])
        for q in range(4):
            if q > 0 and DOWN_QBAR:
                B.barrier()
            for tt2 in range(64):
                hg = hb[tt2 % 2]
                B.dma("sync", hg, hd,
                      oap=hg.ap.rearrange("p (hc t) -> p hc t", t=128),
                      iap=d_h[:, :].rearrange("p (hc q2) -> p hc q2", q2=TOKS)
                      [:, :, tt2 * 128:(tt2 + 1) * 128])
                for half, bank in ((0, psQ), (1, psKV)):
                    B.group("tensor", [sdq3, hg], [bank],
                            [lambda h, hc=hc, half=half, bank=bank, hg=hg, q=q:
                             h.matmul(bank.ap,
                                      lhsT=hg.ap[:, hc * 128:(hc + 1) * 128],
                                      rhs=sdq3.ap[:, hc * DM + q * 1024 + half * 512:
                                                  hc * DM + q * 1024 + (half + 1) * 512],
                                      start=(hc == 0), stop=(hc == OFC - 1))
                             for hc in range(OFC)])
                po = pt[tt2 % 2]
                B.copy("vector", po, psQ, oap=po.ap[:, 0:512])
                B.copy("vector", po, psKV, oap=po.ap[:, 512:1024])
                B.dma("scalar", partd[q].sub(tt2), po,
                      oap=d_part[q][tt2 * 128:(tt2 + 1) * 128, :])
            B.do("gpsimd", [partd[q]], [rsod[q]],
                 lambda h, q=q: h.collective_compute(
                     "ReduceScatter", OP.add,
                     replica_groups=[list(range(NCORES))],
                     ins=[d_part[q].ap().opt()], outs=[d_rso[q].ap().opt()]))

    if upto >= 4:
        B.barrier()
        AR.bot = 0
        x2f = [AR.carve([128, DM], F32) for _ in range(2)]
        yf = [AR.carve([128, DM], F32) for _ in range(2)]
        of_ = [AR.carve([128, DM], F32) for _ in range(2)]
        for tt in range(8):
            xg, yg, og = x2f[tt % 2], yf[tt % 2], of_[tt % 2]
            B.dma("sync", xg, x2d, iap=d_x2[tt * 128:(tt + 1) * 128, :])
            for q in range(4):
                B.dma("sync", yg.sub(q), rsod[q],
                      oap=yg.ap[:, q * 1024:(q + 1) * 1024],
                      iap=d_rso[q][tt * 128:(tt + 1) * 128, :])
            B.tt(og, xg, yg, OP.add)
            B.dma("scalar", outd.sub(tt), og,
                  oap=d_out[tt * 128:(tt + 1) * 128, :])

    B.barrier()
    tapmap = {"ag1i": (d_ag1i, ag1i), "ag1o": (d_ag1o, ag1o),
              "osl": (d_osl, osl), "oex": (d_oex, oex),
              "x2": (d_x2, x2d), "ag2i": (d_ag2i, ag2i), "ag2o": (d_ag2o, ag2o),
              "dq1": (d_dq1, dq1d), "dq2": (d_dq2, dq2d), "dq3": (d_dq3, dq3d),
              "gate": (d_gate, gated), "h": (d_h, hd),
              "part0": (d_part[0], partd[0]), "rso0": (d_rso[0], rsod[0]),
              "rso1": (d_rso[1], rsod[1]), "rso2": (d_rso[2], rsod[2]),
              "rso3": (d_rso[3], rsod[3]), "part1": (d_part[1], partd[1])}
    for name in taps:
        src, stt = tapmap[name]
        tout = nc.dram_tensor(f"tap_{name}", list(src.shape), src.dtype,
                              kind="ExternalOutput")
        B.dma("gpsimd", TT(tout, name=f"tap{name}"), stt,
              oap=tout[:, :], iap=src[:, :])
    for s in B.all_sems:
        if s.n:
            B.wait("gpsimd", (s, s.n, None))
    B.run_block()
    B.close()
    return nc


def _a2a(h, d_in, d_out):
    with allow_ici_all_to_all():
        return h.collective_compute(
            "AllToAll", OP.bypass, replica_groups=[list(range(NCORES))],
            ins=[d_in.ap().opt()], outs=[d_out.ap().opt()])


def _constants():
    pos = np.arange(N_, dtype=np.float64)
    half = np.arange(64, dtype=np.float64)
    inv = 1.0 / (1e4 ** (2.0 * half / 128.0))
    fr = pos[:, None] * inv[None, :]
    emb = np.concatenate([fr, fr], axis=-1)
    cos = np.cos(emb)
    sin = np.sin(emb)
    sin2 = sin.copy()
    sin2[:, 0:64] = -sin[:, 0:64]
    cos_t = cos.reshape(NCHUNK, 128, 128).transpose(1, 0, 2).reshape(128, NCHUNK * 128)
    sin_t = sin2.reshape(NCHUNK, 128, 128).transpose(1, 0, 2).reshape(128, NCHUNK * 128)
    ii = np.arange(128)[:, None]
    jj = np.arange(192)[None, :]
    rel = ii - (jj - 64)
    mask = np.where((rel >= 0) & (rel <= 64), 0.0, -1e30)
    jm = np.arange(128)
    u01 = (jm[:, None] <= jm[None, :]).astype(np.float32)
    cb = np.zeros((128, 16), np.float32)
    cb[:, 0] = 1e-5
    cb[:, 1] = 1.0
    for bi in range(1, 15, 2):
        cb[:, 2 + bi // 2] = -NF4_BOUNDS[bi]
    return (cos_t.astype(np.float32), sin_t.astype(np.float32),
            mask.astype(np.float32), u01, np.eye(128, dtype=ml_dtypes.bfloat16),
            cb)


def prepare_inputs(inputs):
    x = np.ascontiguousarray(
        inputs["L0_attn_hidden_states"].reshape(TOKS, DM).astype(np.float32))
    Wq = inputs["L0_attn_Wq"].astype(np.float32)
    Wk = inputs["L0_attn_Wk"].astype(np.float32)
    Wv = inputs["L0_attn_Wv"].astype(np.float32)
    Wo = np.ascontiguousarray(inputs["L0_attn_Wo"].astype(np.float32))
    w1 = inputs["L0_ffn_w1"].astype(np.float32)
    w2 = inputs["L0_ffn_w2"].astype(np.float32)
    w3 = inputs["L0_ffn_w3"].astype(np.float32)
    w1p = np.vstack([w1, np.zeros((HC * NCORES - w1.shape[0], DM), np.float32)])
    w2p = np.vstack([w2, np.zeros((HC * NCORES - w2.shape[0], DM), np.float32)])
    w3p = np.hstack([w3, np.zeros((DM, HC * NCORES - w3.shape[1]), np.float32)])
    cos_t, sin_t, mask, u01, idb, cb = _constants()
    in_maps = []
    for c in range(NCORES):
        in_maps.append({
            "xop": np.ascontiguousarray(x[c * TPC:(c + 1) * TPC]),
            "wq_sl": np.ascontiguousarray(Wq[:, c * NH * DH:(c + 1) * NH * DH]),
            "wk_sl": np.ascontiguousarray(Wk[:, c * DH:(c + 1) * DH]),
            "wv_sl": np.ascontiguousarray(Wv[:, c * DH:(c + 1) * DH]),
            "wo": Wo,
            "w1_sl": np.ascontiguousarray(w1p[c * HC:(c + 1) * HC]),
            "w2_sl": np.ascontiguousarray(w2p[c * HC:(c + 1) * HC]),
            "w3_sl": np.ascontiguousarray(w3p[:, c * HC:(c + 1) * HC]),
            "cos_t": cos_t, "sin2_t": sin_t, "mask_swa": mask,
            "u01": u01, "identb": idb, "cbias": cb,
        })
    return in_maps


_CACHED = {}


def kernel(**inputs):
    in_maps = prepare_inputs(inputs)
    if "nc" not in _CACHED:
        _CACHED["nc"] = build_program(upto=9)
    nc = _CACHED["nc"]
    res = run_bass_kernel_spmd(nc, in_maps, core_ids=list(range(NCORES)))
    out = np.concatenate([res.results[c]["out_own"] for c in range(NCORES)],
                         axis=0)
    return out.reshape(B_, N_, DM)


# version 06
# speedup vs baseline: 1.1086x; 1.1080x over previous
import numpy as np
import ml_dtypes
import concourse.bass as bass
import concourse.mybir as mybir
from concourse.bass_utils import run_bass_kernel_spmd
from concourse.aot_env import allow_ici_all_to_all

F32 = mybir.dt.float32
BF16 = mybir.dt.bfloat16
FP16 = mybir.dt.float16
OP = mybir.AluOpType
AX = mybir.AxisListType
AF = mybir.ActivationFunctionType

NCORES = 8
B_ = 2
N_ = 4096
DM = 4096
TOKS = B_ * N_
TPC = TOKS // NCORES
NH = 4
DH = 128
NCHUNK = 32
HC = 1408
OFC = HC // 128
KC = DM // 128
INV_SQRT_D = 0.08838834764831845

_NF4 = np.array([-1.0, -0.6961928009986877, -0.5250730514526367,
    -0.39491748809814453, -0.28444138169288635, -0.18477343022823334,
    -0.09105003625154495, 0.0, 0.07958029955625534, 0.16093020141124725,
    0.24611230194568634, 0.33791524171829224, 0.44070982933044434,
    0.5626170039176941, 0.7229568362236023, 1.0], dtype=np.float64)
NF4_BOUNDS = ((_NF4[:-1] + _NF4[1:]) * 0.5).tolist()
NF4_DELTAS = np.diff(_NF4).tolist()

ENGINES = ("sync", "tensor", "scalar", "vector", "gpsimd")
CH_STAGE = 99
DOWN_QBAR = False


class Sem:
    def __init__(self, h):
        self.h = h
        self.n = 0
        self.is_dma = False


class Tile:
    __slots__ = ("wr", "readers", "name", "subs", "parent", "dsem")

    def __init__(self, name=""):
        self.wr = None
        self.readers = []
        self.name = name
        self.subs = {}
        self.parent = None
        self.dsem = None


class TT:
    __slots__ = ("ap", "t", "bf")

    def __init__(self, ap, tile=None, name=""):
        self.ap = ap
        self.t = tile if tile is not None else Tile(name)

    def v(self, ap):
        return TT(ap, self.t)

    def sub(self, key):
        st = self.t.subs.get(key)
        if st is None:
            st = Tile(f"{self.t.name}:{key}")
            st.parent = self.t
            self.t.subs[key] = st
        return TT(self.ap, st)


class Builder:
    def __init__(self, nc):
        self.nc = nc
        self.prog = {e: [] for e in ENGINES}
        self.waited = {e: {} for e in ENGINES}
        self._cms = []
        self.qL = self.sem("qL")
        self.qS = self.sem("qS")
        self.qG = self.sem("qG")
        self.cPE = self.sem("cPE")
        self.cACT = self.sem("cACT")
        self.cDVE = self.sem("cDVE")
        self.cGP = self.sem("cGP")
        self.chain = {"tensor": self.cPE, "scalar": self.cACT,
                      "vector": self.cDVE, "gpsimd": self.cGP, "sync": self.cGP}
        self.qsem = {"sync": self.qL, "scalar": self.qS, "gpsimd": self.qG}
        self.all_sems = [self.qL, self.qS, self.qG, self.cPE, self.cACT,
                         self.cDVE, self.cGP]
        self._dsem_n = 0

    def _dma_sem(self, tile):
        top = tile.parent if tile.parent is not None else tile
        if top.dsem is None:
            top.dsem = self.sem(f"d{self._dsem_n}")
            top.dsem.is_dma = True
            self._dsem_n += 1
            self.all_sems.append(top.dsem)
        return top.dsem

    def sem(self, name):
        cm = self.nc.semaphore(name)
        h = cm.__enter__()
        self._cms.append(cm)
        return Sem(h)

    def close(self):
        for cm in reversed(self._cms):
            cm.__exit__(None, None, None)

    def emit(self, eng, fn):
        self.prog[eng].append(fn)

    def wait(self, eng, ev):
        sem, val, peng = ev
        if sem.is_dma:
            val = sem.n
        if val <= 0 or self.waited[eng].get(sem, 0) >= val:
            return
        if eng == "tensor" and peng == "tensor":
            return
        self.waited[eng][sem] = val
        self.emit(eng, lambda h, s=sem.h, v=val: h.wait_ge(s, v))

    def _pre(self, eng, ins, outs):
        for t in ins:
            if t.wr is not None:
                self.wait(eng, t.wr)
        for t in outs:
            for ev in t.readers:
                self.wait(eng, ev)
            if t.wr is not None:
                self.wait(eng, t.wr)
            if t.parent is not None:
                for ev in t.parent.readers:
                    self.wait(eng, ev)

    def _post(self, eng, ins, outs, sem):
        ev = (sem, sem.n, eng)
        for t in ins:
            t.readers.append(ev)
        for t in outs:
            t.wr = ev
            t.readers = []
            if t.parent is not None:
                t.parent.wr = ev

    def do(self, eng, ins, outs, fn, dma=False):
        ins = [x.t if isinstance(x, TT) else x for x in ins]
        outs = [x.t if isinstance(x, TT) else x for x in outs]
        self._pre(eng, ins, outs)
        if dma:
            sem = self._dma_sem(outs[0])
        else:
            sem = self.chain[eng]
        k = 16 if dma else 1
        sem.n += k
        self.emit(eng, lambda h, f=fn, s=sem.h, kk=k: f(h).then_inc(s, kk))
        self._post(eng, ins, outs, sem)

    def group(self, eng, ins, outs, fns):
        ins = [x.t if isinstance(x, TT) else x for x in ins]
        outs = [x.t if isinstance(x, TT) else x for x in outs]
        self._pre(eng, ins, outs)
        sem = self.chain[eng]
        sem.n += 1

        def run(h, fs=tuple(fns), s=sem.h):
            last = None
            for f in fs:
                last = f(h)
            last.then_inc(s, 1)
        self.emit(eng, run)
        self._post(eng, ins, outs, sem)

    def barrier(self):
        evs = [(s, s.n, None) for s in self.all_sems]
        for e in ENGINES:
            for ev in evs:
                self.wait(e, ev)

    def run_block(self):
        with self.nc.Block() as block:
            @block.sync
            def _(h):
                for fn in self.prog["sync"]:
                    fn(h)

            @block.tensor
            def _(h):
                for fn in self.prog["tensor"]:
                    fn(h)

            @block.scalar
            def _(h):
                for fn in self.prog["scalar"]:
                    fn(h)

            @block.vector
            def _(h):
                for fn in self.prog["vector"]:
                    fn(h)

            @block.gpsimd
            def _(h):
                for fn in self.prog["gpsimd"]:
                    fn(h)

    def dma(self, eng, out, in_, oap=None, iap=None):
        o = oap if oap is not None else out.ap
        i = iap if iap is not None else in_.ap
        self.do(eng, [in_], [out], lambda h, o=o, i=i: h.dma_start(out=o, in_=i),
                dma=True)

    def copy(self, eng, out, in_, oap=None, iap=None):
        o = oap if oap is not None else out.ap
        i = iap if iap is not None else in_.ap
        if eng == "vector":
            self.do(eng, [in_], [out], lambda h, o=o, i=i: h.tensor_copy(o, i))
        else:
            self.do(eng, [in_], [out],
                    lambda h, o=o, i=i: h.activation(o, i, AF.Identity))

    def tt(self, out, a, b, op, oap=None, aap=None, bap=None, bdep=()):
        o = oap if oap is not None else out.ap
        x = aap if aap is not None else a.ap
        y = bap if bap is not None else (b.ap if isinstance(b, TT) else b)
        ins = [a] + ([b] if isinstance(b, TT) else []) + list(bdep)
        self.do("vector", ins, [out],
                lambda h, o=o, x=x, y=y, op=op: h.tensor_tensor(o, x, y, op))

    def ts(self, out, in_, s1, s2, op0, op1=None, oap=None, iap=None, sdep=()):
        o = oap if oap is not None else out.ap
        i = iap if iap is not None else in_.ap
        if op1 is None:
            self.do("vector", [in_, *sdep], [out],
                    lambda h, o=o, i=i, s1=s1, op0=op0:
                    h.tensor_single_scalar(o, i, s1, op0))
        else:
            self.do("vector", [in_, *sdep], [out],
                    lambda h, o=o, i=i, s1=s1, s2=s2, op0=op0, op1=op1:
                    h.tensor_scalar(o, i, s1, s2, op0, op1))

    def stt(self, out, in0, s, in1, op0, op1, oap=None, i0ap=None, i1ap=None,
            sdep=()):
        o = oap if oap is not None else out.ap
        a = i0ap if i0ap is not None else in0.ap
        b = i1ap if i1ap is not None else in1.ap
        self.do("vector", [in0, in1, *sdep], [out],
                lambda h, o=o, a=a, b=b, s=s, op0=op0, op1=op1:
                h.scalar_tensor_tensor(o, a, s, b, op0, op1))

    def red(self, out, in_, op, oap=None, iap=None, absval=False, neg=False):
        o = oap if oap is not None else out.ap
        i = iap if iap is not None else in_.ap
        self.do("vector", [in_], [out],
                lambda h, o=o, i=i, op=op, a=absval, n=neg: h.tensor_reduce(
                    o, i, axis=AX.X, op=op, apply_absolute_value=a, negate=n))

    def recip(self, out, in_, oap=None, iap=None):
        o = oap if oap is not None else out.ap
        i = iap if iap is not None else in_.ap
        self.do("vector", [in_], [out], lambda h, o=o, i=i: h.reciprocal(o, i))

    def act(self, out, in_, func, bias=0.0, scale=1.0, accum=None, oap=None,
            iap=None, sdep=(), aout=None):
        o = oap if oap is not None else out.ap
        i = iap if iap is not None else in_.ap
        outs = [out] + ([aout] if aout is not None else [])
        self.do("scalar", [in_, *sdep], outs,
                lambda h, o=o, i=i, f=func, b=bias, s=scale, a=accum:
                h.activation(o, i, f, bias=b, scale=s, accum_out=a))

    def scan_add(self, out, in_, oap=None, iap=None):
        o = oap if oap is not None else out.ap
        i = iap if iap is not None else in_.ap
        self.do("vector", [in_], [out],
                lambda h, o=o, i=i: h.tensor_tensor_scan(o, i, i, 0.0, OP.add,
                                                         OP.bypass))

    def trans8(self, psb, pairs, idb):
        fns = []
        ins = [s for (_, _, s, _) in pairs] + [idb]
        for j, (_, dap, _, sap) in enumerate(pairs):
            w = sap.shape[-1]
            p0 = dap.base_partition()
            fns.append(lambda h, o=psb.ap[p0:p0 + w, j * 128:(j + 1) * 128],
                       i=sap, idn=idb.ap: h.transpose(o, i, idn))
        self.group("tensor", ins, [psb], fns)
        for j, (dst, dap, _, sap) in enumerate(pairs):
            w = sap.shape[-1]
            p0 = dap.base_partition()
            self.copy("scalar", dst.v(dap) if dap is not dst.ap else dst, psb,
                      oap=dap, iap=psb.ap[p0:p0 + w, j * 128:(j + 1) * 128])


class Arena:
    def __init__(self, nc, name, nbytes):
        self.nbytes = nbytes
        self.t = nc.alloc_sbuf_tensor(name, [128, nbytes // 2], BF16)
        self.bot = 0
        self.top = nbytes

    def carve(self, shape, dtype, top=False):
        esz = {F32: 4, BF16: 2, FP16: 2}[dtype]
        nfree = int(np.prod(shape[1:]))
        nb = (nfree * esz + 63) & ~63
        if top:
            self.top -= nb
            off = self.top
        else:
            off = self.bot
            self.bot += nb
        assert self.bot <= self.top, f"arena overflow bot={self.bot} top={self.top}"
        base = self.t[:][:, off // 2: off // 2 + nfree * esz // 2]
        ap = base.bitcast(dtype) if dtype != BF16 else base
        if len(shape) > 2:
            dims = " ".join(f"d{i}" for i in range(1, len(shape)))
            ap = ap.rearrange(f"p ({dims}) -> p {dims}",
                              **{f"d{i}": shape[i] for i in range(1, len(shape))})
        return TT(ap, name=f"ar@{off}")


def build_program(upto=9, taps=(), nb=B_, nchunks=NCHUNK, do_quant=True, do_a2a=True):
    nc = bass.Bass("TRN2", target_bir_lowering=False)
    B = Builder(nc)

    d_x = nc.dram_tensor("xop", [TPC, DM], F32, kind="ExternalInput")
    d_wq = nc.dram_tensor("wq_sl", [DM, NH * DH], F32, kind="ExternalInput")
    d_wk = nc.dram_tensor("wk_sl", [DM, DH], F32, kind="ExternalInput")
    d_wv = nc.dram_tensor("wv_sl", [DM, DH], F32, kind="ExternalInput")
    d_wo = nc.dram_tensor("wo", [DM, DM], F32, kind="ExternalInput")
    d_w1 = nc.dram_tensor("w1_sl", [HC, DM], F32, kind="ExternalInput")
    d_w2 = nc.dram_tensor("w2_sl", [HC, DM], F32, kind="ExternalInput")
    d_w3 = nc.dram_tensor("w3_sl", [DM, HC], F32, kind="ExternalInput")
    d_cos = nc.dram_tensor("cos_t", [128, NCHUNK * DH], F32, kind="ExternalInput")
    d_sin = nc.dram_tensor("sin2_t", [128, NCHUNK * DH], F32, kind="ExternalInput")
    d_mask = nc.dram_tensor("mask_swa", [128, 192], F32, kind="ExternalInput")
    d_u01 = nc.dram_tensor("u01", [128, 128], F32, kind="ExternalInput")
    d_idb = nc.dram_tensor("identb", [128, 128], BF16, kind="ExternalInput")
    d_cb = nc.dram_tensor("cbias", [128, 16], F32, kind="ExternalInput")
    d_out = nc.dram_tensor("out_own", [TPC, DM], F32, kind="ExternalOutput")

    d_ag1i = nc.dram_tensor("ag1i", [128, 32768], BF16)
    d_ag1o = nc.dram_tensor("ag1o", [1024, 32768], BF16)
    d_osl = nc.dram_tensor("osl", [1024, NH * TPC], BF16)
    d_oex = nc.dram_tensor("oex", [1024, NH * TPC], BF16)
    d_x2 = nc.dram_tensor("x2own", [TPC, DM], F32)
    d_ag2i = nc.dram_tensor("ag2i", [128, 32768], BF16)
    d_ag2o = nc.dram_tensor("ag2o", [1024, 32768], BF16)
    d_dq1 = nc.dram_tensor("dq1T", [128, KC * HC], BF16)
    d_dq2 = nc.dram_tensor("dq2T", [128, KC * HC], BF16)
    d_dq3 = nc.dram_tensor("dq3T", [128, OFC * DM], BF16)
    d_gate = nc.dram_tensor("gateb", [128, OFC * TOKS], BF16)
    d_h = nc.dram_tensor("hb", [128, OFC * TOKS], BF16)
    d_part = [nc.dram_tensor(f"part{q}", [TOKS, 1024], F32) for q in range(4)]
    d_rso = [nc.dram_tensor(f"rso{q}", [TPC, 1024], F32) for q in range(4)]

    ag1i = TT(d_ag1i, name="ag1i"); ag1o = TT(d_ag1o, name="ag1o")
    osl = TT(d_osl, name="osl"); oex = TT(d_oex, name="oex")
    x2d = TT(d_x2, name="x2"); ag2i = TT(d_ag2i, name="ag2i")
    ag2o = TT(d_ag2o, name="ag2o")
    dq1d = TT(d_dq1, name="dq1"); dq2d = TT(d_dq2, name="dq2")
    dq3d = TT(d_dq3, name="dq3")
    gated = TT(d_gate, name="gate"); hd = TT(d_h, name="h")
    partd = [TT(d_part[q], name=f"part{q}") for q in range(4)]
    rsod = [TT(d_rso[q], name=f"rso{q}") for q in range(4)]
    outd = TT(d_out, name="out")
    xin = TT(d_x, name="xin"); wod = TT(d_wo, name="wo")
    w1d = TT(d_w1, name="w1"); w2d = TT(d_w2, name="w2"); w3d = TT(d_w3, name="w3")

    s_cos = TT(nc.alloc_sbuf_tensor("s_cos", [128, NCHUNK * DH], F32)[:])
    s_sin = TT(nc.alloc_sbuf_tensor("s_sin", [128, NCHUNK * DH], F32)[:])
    s_mask = TT(nc.alloc_sbuf_tensor("s_mask", [128, 192], F32)[:])
    s_u01 = TT(nc.alloc_sbuf_tensor("s_u01", [128, 128], F32)[:])
    s_idb = TT(nc.alloc_sbuf_tensor("s_idb", [128, 128], BF16)[:])
    s_cb = TT(nc.alloc_sbuf_tensor("s_cb", [128, 16], F32)[:])
    AR = Arena(nc, "AR", 163 * 1024)

    psQs = [TT(nc.alloc_psum_tensor(f"psQ{i}", [128, 512], F32)[:])
            for i in range(2)]
    psKVs = [TT(nc.alloc_psum_tensor(f"psKV{i}", [128, 512], F32)[:])
             for i in range(2)]
    psOs = [TT(nc.alloc_psum_tensor(f"psO{i}", [128, 512], F32)[:])
            for i in range(2)]
    psWs = [TT(nc.alloc_psum_tensor(f"psW{i}", [128, 512], F32)[:])
            for i in range(2)]
    for w in psWs:
        w.bf = w.ap.bitcast(BF16)
    psQ, psKV, psO = psQs[0], psKVs[0], psOs[0]
    psTR = TT.__new__(TT); psTR.ap = psWs[0].bf; psTR.t = psWs[0].t
    psQT = psTR
    banks5 = [psQs[0], psKVs[0], psOs[0], psQs[1], psKVs[1]]

    B.dma("sync", s_cos, TT(d_cos, name="cosi"), iap=d_cos[:, :])
    B.dma("sync", s_sin, TT(d_sin, name="sini"), iap=d_sin[:, :])
    B.dma("sync", s_mask, TT(d_mask, name="mski"), iap=d_mask[:, :])
    B.dma("sync", s_u01, TT(d_u01, name="u01i"), iap=d_u01[:, :])
    B.dma("sync", s_idb, TT(d_idb, name="idbi"), iap=d_idb[:, :])
    B.dma("sync", s_cb, TT(d_cb, name="cbi"), iap=d_cb[:, :])

    qzw = [AR.carve([128, 2048], F32, top=True) for _ in range(1)]
    qzv = AR.carve([128, 2048], F32, top=True)
    qzacc = AR.carve([128, 2048], FP16, top=True)
    qzac2 = AR.carve([128, 2048], FP16, top=True)
    qzt = [AR.carve([128, 2048], FP16, top=True) for _ in range(2)]
    qzam = AR.carve([128, 64], F32, top=True)
    qzrec = AR.carve([128, 64], F32, top=True)
    qzdq = AR.carve([128, 2048], BF16, top=True)
    qztt = AR.carve([128, 2048], BF16, top=True)
    top_quant = AR.top
    s_wq = AR.carve([128, KC * NH * DH], BF16, top=True)
    s_wk = AR.carve([128, KC * DH], BF16, top=True)
    s_wv = AR.carve([128, KC * DH], BF16, top=True)
    wq_r = d_wq[:, :].rearrange("(kc p) f -> p kc f", p=128)
    wqin = TT(d_wq, name="wqi"); wkin = TT(d_wk, name="wki")
    wvin = TT(d_wv, name="wvi")
    wq_s = s_wq.ap.rearrange("p (kc f) -> p kc f", f=NH * DH)
    for j in range(4):
        B.dma("gpsimd", s_wq, wqin, oap=wq_s[:, j * 8:(j + 1) * 8, :],
              iap=wq_r[:, j * 8:(j + 1) * 8, :])
    B.dma("gpsimd", s_wk, wkin,
          oap=s_wk.ap.rearrange("p (kc f) -> p kc f", f=DH),
          iap=d_wk[:, :].rearrange("(kc p) f -> p kc f", p=128))
    B.dma("gpsimd", s_wv, wvin,
          oap=s_wv.ap.rearrange("p (kc f) -> p kc f", f=DH),
          iap=d_wv[:, :].rearrange("(kc p) f -> p kc f", p=128))


    def ln_phase(src_d, src_tt, dst_d, dst_tt, kc_major):
        bot0 = AR.bot
        xt = [AR.carve([128, DM], F32) for _ in range(2)]
        sq = AR.carve([128, DM], F32)
        st = AR.carve([128, 16], F32)
        xlb = AR.carve([128, DM], BF16)
        xlt = AR.carve([128, DM], BF16)
        c_s1, c_s2, c_m, c_mm, c_v, c_rs, c_rl, c_b2 = (
            st.v(st.ap[:, i:i + 1]) for i in range(8))
        for tt in range(8):
            x = xt[tt % 2]
            B.dma("sync", x, src_tt, iap=src_d[tt * 128:(tt + 1) * 128, :])
            B.red(c_s1, x, OP.add)
            B.act(sq, x, AF.Square, accum=c_s2.ap, aout=c_s2)
            B.ts(c_m, c_s1, 1.0 / DM, None, OP.mult)
            B.ts(c_v, c_s2, 1.0 / DM, None, OP.mult)
            B.tt(c_mm, c_m, c_m, OP.mult)
            B.tt(c_v, c_v, c_mm, OP.subtract)
            B.act(c_rl, c_v, AF.Ln, bias=s_cb.ap[:, 0:1], sdep=[s_cb])
            B.act(c_rs, c_rl, AF.Exp, scale=-0.5)
            B.stt(c_b2, c_m, -1.0, c_rs, OP.mult, OP.mult)
            B.act(xlb, x, AF.Identity, scale=c_rs.ap, bias=c_b2.ap,
                  sdep=[c_rs, c_b2])
            for grp in range(4):
                pairs = [(xlt, xlt.ap[:, (grp * 8 + j) * 128:(grp * 8 + j + 1) * 128],
                          xlb, xlb.ap[:, (grp * 8 + j) * 128:(grp * 8 + j + 1) * 128])
                         for j in range(8)]
                B.trans8(psTR, pairs, s_idb)
            if kc_major:
                dstr = dst_d[:, :].rearrange("p (kc q) -> p kc q", q=1024)
                srcr = xlt.ap.rearrange("p (kc t) -> p kc t", t=128)
                for g4 in range(4):
                    B.dma("scalar", dst_tt.sub((tt, g4)), xlt,
                          oap=dstr[:, g4 * 8:(g4 + 1) * 8, tt * 128:(tt + 1) * 128],
                          iap=srcr[:, g4 * 8:(g4 + 1) * 8, :])
            else:
                B.dma("scalar", dst_tt.sub(tt), xlt,
                      oap=dst_d[:, tt * DM:(tt + 1) * DM], iap=xlt.ap)
        AR.bot = bot0

    ln_phase(d_x, xin, d_ag1i, ag1i, kc_major=False)
    B.do("gpsimd", [ag1i], [ag1o],
         lambda h: h.collective_compute(
             "AllGather", OP.bypass, replica_groups=[list(range(NCORES))],
             ins=[d_ag1i.ap().opt()], outs=[d_ag1o.ap().opt()]))

    def quant_tiles():
        for dsrc, wtt, dqd, ddq in ((d_w1, w1d, dq1d, d_dq1),
                                    (d_w2, w2d, dq2d, d_dq2)):
            for hc in range(OFC):
                for half in range(2):
                    yield ("w12", dsrc, wtt, dqd, ddq, hc, half)
        for dc in range(KC):
            yield ("w3", d_w3, w3d, dq3d, d_dq3, dc, 0)
    qgen = quant_tiles()
    qstate = {"i": 0, "done": False, "w12_left": 2 * OFC * 2}

    def emit_quant_tile(w12_only=False):
        if qstate["done"]:
            return False
        if w12_only and qstate["w12_left"] <= 0:
            return False
        try:
            wsel, dsrc, wtt, dqd, ddq, rc, half = next(qgen)
        except StopIteration:
            qstate["done"] = True
            return False
        if wsel == "w12":
            qstate["w12_left"] -= 1
        i = qstate["i"]; qstate["i"] += 1
        wbuf = qzw[0]
        ncol = 2048 if wsel == "w12" else HC
        nblk = ncol // 64
        B.dma("sync", wbuf, wtt, oap=wbuf.ap[:, 0:ncol],
              iap=dsrc[rc * 128:(rc + 1) * 128, half * 2048: half * 2048 + ncol])
        w3v = wbuf.ap[:, 0:ncol].rearrange("p (g e) -> p g e", e=64)
        B.red(qzam, wbuf, OP.max, oap=qzam.ap[:, 0:nblk], iap=w3v, absval=True)
        B.ts(qzam, qzam, 1e-30, None, OP.max, oap=qzam.ap[:, 0:nblk],
             iap=qzam.ap[:, 0:nblk])
        B.recip(qzrec, qzam, oap=qzrec.ap[:, 0:nblk], iap=qzam.ap[:, 0:nblk])
        rb = qzrec.ap[:, 0:nblk][:, :, None].broadcast_to([128, nblk, 64])
        B.tt(qzv, wbuf, qzrec, OP.mult,
             oap=qzv.ap[:, 0:ncol].rearrange("p (g e) -> p g e", e=64),
             aap=w3v, bap=rb)
        vv = qzv.ap[:, 0:ncol]
        ne = 0
        for bi in range(0, 15, 2):
            b_, dl = NF4_BOUNDS[bi], NF4_DELTAS[bi]
            tb = qzt[0]
            B.ts(tb, qzv, b_, dl, OP.is_gt, OP.mult,
                 oap=tb.ap[:, 0:ncol], iap=vv)
            if ne == 0:
                B.copy("vector", qzacc, tb, oap=qzacc.ap[:, 0:ncol],
                       iap=tb.ap[:, 0:ncol])
            else:
                B.tt(qzacc, qzacc, tb, OP.add, oap=qzacc.ap[:, 0:ncol],
                     aap=qzacc.ap[:, 0:ncol], bap=tb.ap[:, 0:ncol])
            ne += 1
            if bi + 1 < 15:
                b2, dl2 = NF4_BOUNDS[bi + 1], NF4_DELTAS[bi + 1]
                tb2 = qzt[1]
                B.act(tb2, qzv, AF.Sign,
                      bias=s_cb.ap[:, 2 + bi // 2:3 + bi // 2],
                      oap=tb2.ap[:, 0:ncol], iap=vv, sdep=[s_cb])
                if ne == 1:
                    B.ts(qzac2, tb2, dl2 / 2.0, None, OP.mult,
                         oap=qzac2.ap[:, 0:ncol], iap=tb2.ap[:, 0:ncol])
                else:
                    B.stt(qzac2, tb2, dl2 / 2.0, qzac2, OP.mult, OP.add,
                          oap=qzac2.ap[:, 0:ncol], i0ap=tb2.ap[:, 0:ncol],
                          i1ap=qzac2.ap[:, 0:ncol])
        B.tt(qzacc, qzacc, qzac2, OP.add, oap=qzacc.ap[:, 0:ncol],
             aap=qzacc.ap[:, 0:ncol], bap=qzac2.ap[:, 0:ncol])
        offs = -1.0 + sum(NF4_DELTAS[bi] / 2.0 for bi in range(1, 15, 2))
        amb = qzam.ap[:, 0:nblk][:, :, None].broadcast_to([128, nblk, 64])
        B.stt(qzdq, qzacc, offs, qzam, OP.add, OP.mult,
              oap=qzdq.ap[:, 0:ncol].rearrange("p (g e) -> p g e", e=64),
              i0ap=qzacc.ap[:, 0:ncol].rearrange("p (g e) -> p g e", e=64),
              i1ap=amb)
        nj = ncol // 128
        for grp in range((nj + 7) // 8):
            jn = min(8, nj - grp * 8)
            fns = [lambda h, j=j, jj=grp * 8 + j: h.transpose(
                psQT.ap[:, j * 128:(j + 1) * 128],
                qzdq.ap[:, jj * 128:(jj + 1) * 128], s_idb.ap)
                for j in range(jn)]
            B.group("tensor", [qzdq, s_idb], [psQT], fns)
            B.copy("scalar", qztt, psQT,
                   oap=qztt.ap[:, grp * 1024:grp * 1024 + jn * 128],
                   iap=psQT.ap[:, 0:jn * 128])
        if wsel == "w12":
            dstr = ddq[:, :].rearrange("p (kc q) -> p kc q", q=HC)
            B.dma("scalar", dqd.sub((rc, half)), qztt,
                  oap=dstr[:, half * 16:(half + 1) * 16, rc * 128:(rc + 1) * 128],
                  iap=qztt.ap.rearrange("p (j t) -> p j t", t=128)[:, 0:16, :])
        else:
            dstr = ddq[:, :].rearrange("p (hc q) -> p hc q", q=DM)
            B.dma("scalar", dqd.sub((rc, 0)), qztt,
                  oap=dstr[:, 0:OFC, rc * 128:(rc + 1) * 128],
                  iap=qztt.ap.rearrange("p (j t) -> p j t", t=128)[:, 0:OFC, :])
        return True

    if not do_quant:
        qstate["done"] = True
    if upto >= 1:
        bot0 = AR.bot
        xfm = [AR.carve([128, DM], BF16) for _ in range(3)]

        def mk_stream():
            import types
            T = types.SimpleNamespace()
            T.qtok = AR.carve([128, 512], F32)
            T.ktok = AR.carve([128, 128], F32)
            T.vtok = [AR.carve([128, 128], BF16) for _ in range(2)]
            T.skfm = [AR.carve([128, 128], BF16) for _ in range(2)]
            T.red = AR.carve([128, 16], F32)
            T.ksn = AR.carve([128, 128], F32)
            T.skf = AR.carve([128, 128], F32)
            T.gst = AR.carve([128, 128], F32)
            T.lsg = [AR.carve([128, 128], F32) for _ in range(2)]
            T.sqf = AR.carve([128, 512], F32)
            T.rtmp = AR.carve([128, 512], F32)
            T.qsb = AR.carve([128, 512], BF16)
            T.sqb = AR.carve([128, 512], BF16)
            T.ksb = AR.carve([128, 128], BF16)
            T.gsb = AR.carve([128, 128], BF16)
            T.skb = AR.carve([128, 128], BF16)
            T.qsfm = AR.carve([128, 512], BF16)
            T.sqfm = AR.carve([128, 512], BF16)
            T.ksfm = AR.carve([128, 128], BF16)
            T.gsfm = AR.carve([128, 128], BF16)
            T.Gfm = AR.carve([128, 128], F32)
            T.tG = AR.carve([128, 128], F32)
            T.eG = AR.carve([128, 128], F32)
            T.eNG = AR.carve([128, 128], F32)
            T.e2t = AR.carve([128, 128], F32)
            T.qtfm = AR.carve([128, 512], BF16)
            T.kbfm = AR.carve([128, 128], BF16)
            T.khfm = AR.carve([128, 128], BF16)
            T.khtok = AR.carve([128, 128], BF16)
            T.Sf = AR.carve([128, 128], F32)
            T.Sb = AR.carve([128, 128], BF16)
            T.Ssc = AR.carve([128, 128], F32)
            T.pglat = AR.carve([128, 512], BF16)
            T.scm = AR.carve([128, 192], F32)
            T.pswa = AR.carve([128, 192], BF16)
            T.ptc = AR.carve([128, 128], BF16)
            T.ptp = AR.carve([128, 128], BF16)
            T.obf = AR.carve([128, 512], BF16)
            return T

        TS = [mk_stream() for _ in range(nb)]
        globals()["DBG_A"] = {"TS": TS, "xfm": xfm}
        psWb = []
        for w in psWs:
            v = TT.__new__(TT); v.ap = w.bf; v.t = w.t
            psWb.append(v)

        def st0(b, m, T, x):
            psQb, psKVb = psQs[b], psKVs[b]
            t0 = b * N_ + m * 128
            rank, tti = t0 // TPC, (t0 % TPC) // 128
            B.dma("sync", x, ag1o,
                  iap=d_ag1o[rank * 128:(rank + 1) * 128,
                             tti * DM:(tti + 1) * DM])
            xv = x.ap.rearrange("p (kc t) -> p kc t", t=128)
            B.group("tensor", [x, s_wq], [psQb],
                    [lambda h, kc=kc, xv=xv, o=psQb.ap: h.matmul(
                        o, lhsT=xv[:, kc, :],
                        rhs=s_wq.ap[:, kc * 512:(kc + 1) * 512],
                        start=(kc == 0), stop=(kc == KC - 1))
                     for kc in range(KC)])
            fns = [lambda h, kc=kc, xv=xv, o=psKVb.ap: h.matmul(
                o[:, 0:128], lhsT=xv[:, kc, :],
                rhs=s_wk.ap[:, kc * 128:(kc + 1) * 128],
                start=(kc == 0), stop=(kc == KC - 1)) for kc in range(KC)]
            fns += [lambda h, kc=kc, xv=xv, o=psKVb.ap: h.matmul(
                o[:, 128:256], lhsT=xv[:, kc, :],
                rhs=s_wv.ap[:, kc * 128:(kc + 1) * 128],
                start=(kc == 0), stop=(kc == KC - 1)) for kc in range(KC)]
            B.group("tensor", [x, s_wk, s_wv], [psKVb], fns)
            B.copy("vector", T.qtok, psQb)
            B.copy("vector", T.ktok, psKVb, iap=psKVb.ap[:, 0:128])
            B.copy("vector", T.vtok[m % 2], psKVb, iap=psKVb.ap[:, 128:256])

        def st1(b, m, T):
            qtok, ktok, red = T.qtok, T.ktok, T.red
            q4 = qtok.ap.rearrange("p (hh t) -> p hh t", t=128)
            for hh in range(4):
                B.act(T.qsb, qtok, AF.Exp,
                      accum=red.ap[:, 4 + hh:5 + hh],
                      oap=T.qsb.ap[:, hh * 128:(hh + 1) * 128],
                      iap=qtok.ap[:, hh * 128:(hh + 1) * 128], aout=red)
            B.recip(red, red, oap=red.ap[:, 8:12], iap=red.ap[:, 4:8])
            for hh in range(4):
                B.act(T.qsb, T.qsb, AF.Identity, scale=red.ap[:, 8 + hh:9 + hh],
                      oap=T.qsb.ap[:, hh * 128:(hh + 1) * 128],
                      iap=T.qsb.ap[:, hh * 128:(hh + 1) * 128], sdep=[red])
            B.act(T.ksn, ktok, AF.Exp,
                  accum=red.ap[:, 13:14], aout=red)
            B.recip(red, red, oap=red.ap[:, 14:15], iap=red.ap[:, 13:14])
            B.act(T.ksb, T.ksn, AF.Identity, scale=red.ap[:, 14:15], sdep=[red])
            B.act(T.lsg[0], ktok, AF.Abs)
            B.act(T.lsg[1], T.lsg[0], AF.Exp, scale=-1.0)
            B.act(T.lsg[1], T.lsg[1], AF.Ln, bias=s_cb.ap[:, 1:2], sdep=[s_cb])
            B.ts(T.gst, ktok, 0.0, None, OP.min)
            B.tt(T.gsb, T.gst, T.lsg[1], OP.subtract)
            cosc = s_cos.ap[:, m * 128:(m + 1) * 128]
            sinc = s_sin.ap[:, m * 128:(m + 1) * 128]
            cb4 = cosc[:, None, :].broadcast_to([128, 4, 128])
            s4 = T.sqf.ap.rearrange("p (hh t) -> p hh t", t=128)
            B.tt(T.sqf, qtok, s_cos, OP.mult, oap=s4, aap=q4, bap=cb4)
            q4h = qtok.ap.rearrange("p (hh hf t) -> p hh hf t", hf=2, t=64)
            r4h = T.rtmp.ap.rearrange("p (hh hf t) -> p hh hf t", hf=2, t=64)
            sinh = sinc.rearrange("p (hf t) -> p hf t", t=64)
            for hf in range(2):
                sv = sinh[:, hf, :][:, None, :].broadcast_to([128, 4, 64])
                B.tt(T.rtmp, qtok, s_sin, OP.mult, oap=r4h[:, :, hf, :],
                     aap=q4h[:, :, 1 - hf, :], bap=sv)
            B.tt(T.sqb, T.sqf, T.rtmp, OP.add)
            B.tt(T.skf, ktok, s_cos, OP.mult, bap=cosc)
            for hf in range(2):
                B.tt(T.rtmp, ktok, s_sin, OP.mult,
                     oap=T.rtmp.ap[:, hf * 64:(hf + 1) * 64],
                     aap=ktok.ap[:, (1 - hf) * 64:(2 - hf) * 64],
                     bap=sinh[:, hf, :])
            B.tt(T.skb, T.skf, T.rtmp, OP.add, bap=T.rtmp.ap[:, 0:128])

        def st2(b, m, T):
            W = psWb[b]
            cur = m % 2
            B.trans8(W, [(T.qsfm, T.qsfm.ap[:, j * 128:(j + 1) * 128],
                          T.qsb, T.qsb.ap[:, j * 128:(j + 1) * 128])
                         for j in range(4)] +
                        [(T.sqfm, T.sqfm.ap[:, j * 128:(j + 1) * 128],
                          T.sqb, T.sqb.ap[:, j * 128:(j + 1) * 128])
                         for j in range(4)], s_idb)
            B.trans8(W, [(T.ksfm, T.ksfm.ap, T.ksb, T.ksb.ap),
                         (T.skfm[cur], T.skfm[cur].ap, T.skb, T.skb.ap),
                         (T.gsfm, T.gsfm.ap, T.gsb, T.gsb.ap)], s_idb)
            B.scan_add(T.Gfm, T.gsfm)
            B.act(T.eG, T.Gfm, AF.Exp, scale=1.0 / 16.0)
            B.act(T.eNG, T.Gfm, AF.Exp, scale=-1.0 / 16.0)
            B.ts(T.tG, T.Gfm, T.Gfm.ap[:, 127:128], 1.0, OP.subtract, OP.mult,
                 sdep=[T.Gfm])
            B.act(T.e2t, T.tG, AF.Exp, scale=-1.0 / 16.0)
            eGb = T.eG.ap[:, None, :].broadcast_to([128, 4, 128])
            B.stt(T.qtfm, T.qsfm, 0.5, T.eG, OP.mult, OP.mult,
                  oap=T.qtfm.ap.rearrange("p (hh t) -> p hh t", t=128),
                  i0ap=T.qsfm.ap.rearrange("p (hh t) -> p hh t", t=128),
                  i1ap=eGb)
            B.tt(T.kbfm, T.ksfm, T.eNG, OP.mult)
            B.tt(T.khfm, T.ksfm, T.e2t, OP.mult)
            B.trans8(W, [(T.khtok, T.khtok.ap, T.khfm, T.khfm.ap)], s_idb)

        def st3p(b, m, T):
            W = psWs[b]
            B.group("tensor", [T.kbfm, T.qtfm], [W],
                    [lambda h, j=j, o=W.ap, kb=T.kbfm.ap, qt=T.qtfm.ap:
                     h.matmul(o[:, j * 128:(j + 1) * 128], lhsT=kb,
                              rhs=qt[:, j * 128:(j + 1) * 128],
                              start=True, stop=True) for j in range(4)])
            u4 = s_u01.ap[:, None, :].broadcast_to([128, 4, 128])
            B.tt(T.pglat, W, s_u01, OP.mult,
                 oap=T.pglat.ap.rearrange("p (hh t) -> p hh t", t=128),
                 aap=W.ap.rearrange("p (hh t) -> p hh t", t=128), bap=u4)

        def st3h(b, m, T, hh):
            W, Wv, Ob = psWs[b], psWb[b], psOs[b]
            cur, prv = m % 2, 1 - m % 2
            osli = Ob.ap[:, hh * 128:(hh + 1) * 128]
            fns = []
            ins = [T.vtok[cur], T.pglat]
            if m > 0:
                fns.append(lambda h, o=osli, hh=hh, sb=T.Sb.ap, qt=T.qtfm.ap:
                           h.matmul(o, lhsT=sb,
                                    rhs=qt[:, hh * 128:(hh + 1) * 128],
                                    start=True, stop=False))
                ins += [T.Sb, T.qtfm]
            fns.append(lambda h, o=osli, hh=hh, v=T.vtok[cur].ap,
                       pg=T.pglat.ap, m=m: h.matmul(
                           o, lhsT=v, rhs=pg[:, hh * 128:(hh + 1) * 128],
                           start=(m == 0), stop=False))
            B.group("tensor", ins, [Ob], fns)
            fns = [lambda h, hh=hh, sk=T.skfm[cur].ap, sq=T.sqfm.ap,
                   o=W.ap, m=m: h.matmul(
                       o[:, 64:192], lhsT=sq[:, hh * 128:(hh + 1) * 128],
                       rhs=sk, start=True, stop=(m == 0))]
            ins2 = [T.sqfm, T.skfm[cur]]
            if m > 0:
                fns.append(lambda h, hh=hh, skp=T.skfm[prv].ap,
                           sq=T.sqfm.ap, o=W.ap: h.matmul(
                               o[:, 0:64], lhsT=sq[:, hh * 128:(hh + 1) * 128],
                               rhs=skp[:, 64:128], start=False, stop=True))
                ins2.append(T.skfm[prv])
            B.group("tensor", ins2, [W], fns)
            lo = 64 if m == 0 else 0
            red = T.red
            B.tt(T.scm, W, s_mask, OP.add, oap=T.scm.ap[:, lo:192],
                 aap=W.ap[:, lo:192], bap=s_mask.ap[:, lo:192])
            B.act(T.scm, T.scm, AF.Exp, scale=INV_SQRT_D,
                  accum=red.ap[:, 2:3],
                  oap=T.scm.ap[:, lo:192], iap=T.scm.ap[:, lo:192],
                  aout=red)
            B.recip(red, red, oap=red.ap[:, 3:4], iap=red.ap[:, 2:3])
            B.ts(T.pswa, T.scm, red.ap[:, 3:4], 0.5, OP.mult, OP.mult,
                 oap=T.pswa.ap[:, lo:192], iap=T.scm.ap[:, lo:192],
                 sdep=[red])
            tl = [(T.ptc, T.ptc.ap, T.pswa, T.pswa.ap[:, 64:192])]
            if m > 0:
                tl.append((T.ptp, T.ptp.ap[64:128, :], T.pswa,
                           T.pswa.ap[:, 0:64]))
            B.trans8(Wv, tl, s_idb)
            fns = [lambda h, o=osli, v=T.vtok[cur].ap, pt=T.ptc.ap, m=m:
                   h.matmul(o, lhsT=v, rhs=pt, start=False, stop=(m == 0))]
            ins3 = [T.vtok[cur], T.ptc]
            if m > 0:
                fns.append(lambda h, o=osli, vp=T.vtok[prv].ap, pt=T.ptp.ap:
                           h.matmul(o, lhsT=vp[64:128, :], rhs=pt[64:128, :],
                                    start=False, stop=True))
                ins3 += [T.vtok[prv], T.ptp]
            B.group("tensor", ins3, [Ob], fns)

        def st4(b, m, T):
            W, Ob = psWs[b], psOs[b]
            cur = m % 2
            t0 = b * N_ + m * 128
            rank, tis = t0 // TPC, t0 % TPC
            B.copy("vector", T.obf, Ob)
            dsto = d_osl[rank * 128:(rank + 1) * 128, :].rearrange(
                "p (hh c) -> p hh c", c=TPC)
            B.dma("scalar", osl.sub((b, m)), T.obf,
                  oap=dsto[:, :, tis:tis + 128],
                  iap=T.obf.ap.rearrange("p (hh t) -> p hh t", t=128))
            if m > 0:
                B.act(T.Ssc, T.Sf, AF.Identity, scale=T.eG.ap[:, 127:128],
                      sdep=[T.eG])
            B.group("tensor", [T.khtok, T.vtok[cur]], [W],
                    [lambda h, v=T.vtok[cur].ap, kh=T.khtok.ap, o=W.ap:
                     h.matmul(o[:, 0:128], lhsT=kh, rhs=v,
                              start=True, stop=True)])
            if m > 0:
                B.tt(T.Sf, W, T.Ssc, OP.add, aap=W.ap[:, 0:128])
            else:
                B.copy("vector", T.Sf, W, iap=W.ap[:, 0:128])
            B.copy("scalar", T.Sb, T.Sf)

        qi = 0
        for m in range(nchunks):
            for b in range(nb):
                st0(b, m, TS[b], xfm[(m * nb + b) % 3])
            for b in range(nb):
                st1(b, m, TS[b])
            for b in range(nb):
                st2(b, m, TS[b])
            for b in range(nb):
                st3p(b, m, TS[b])
            for hh in range(4):
                for b in range(nb):
                    st3h(b, m, TS[b], hh)
            for b in range(nb):
                st4(b, m, TS[b])
            emit_quant_tile(w12_only=True)
        while emit_quant_tile(w12_only=True):
            pass
        AR.bot = bot0
        if do_a2a:
            B.do("gpsimd", [osl], [oex],
                 lambda h: _a2a(h, d_osl, d_oex))

    if upto >= 2:
        B.barrier()
        AR.bot = 0
        AR.top = top_quant
        ores = AR.carve([128, KC * 1024], BF16)
        woq = AR.carve([128, KC * 512], BF16)
        xq = [AR.carve([128, 512], F32) for _ in range(2)]
        x2t = [AR.carve([128, 512], F32) for _ in range(2)]
        for hidx in range(KC):
            B.dma("sync", ores.sub(hidx), oex,
                  oap=ores.ap[:, hidx * 1024:(hidx + 1) * 1024],
                  iap=d_oex[(hidx // 4) * 128:(hidx // 4 + 1) * 128,
                            (hidx % 4) * 1024:((hidx % 4) + 1) * 1024])
        for e in range(8):
            for kc in range(KC):
                B.dma("gpsimd", woq.sub((e, kc)), wod,
                      oap=woq.ap[:, kc * 512:(kc + 1) * 512],
                      iap=d_wo[kc * 128:(kc + 1) * 128,
                               e * 512:(e + 1) * 512])
            for tt in range(8):
                emit_quant_tile()
                xg = xq[tt % 2]
                B.dma("sync", xg, xin,
                      iap=d_x[tt * 128:(tt + 1) * 128, e * 512:(e + 1) * 512])
                bank = psQ if tt % 2 == 0 else psKV
                B.group("tensor", [ores, woq], [bank],
                        [lambda h, kc=kc, bank=bank, tt=tt:
                         h.matmul(bank.ap,
                                  lhsT=ores.ap[:, kc * 1024 + tt * 128:
                                               kc * 1024 + (tt + 1) * 128],
                                  rhs=woq.ap[:, kc * 512:(kc + 1) * 512],
                                  start=(kc == 0), stop=(kc == KC - 1))
                         for kc in range(KC)])
                xo = x2t[tt % 2]
                B.tt(xo, bank, xg, OP.add, aap=bank.ap, bap=xg.ap, bdep=[xg])
                B.dma("scalar", x2d.sub((e, tt)), xo,
                      oap=d_x2[tt * 128:(tt + 1) * 128, e * 512:(e + 1) * 512])
        B.barrier()
        AR.bot = 0
        while emit_quant_tile():
            pass
        ln_phase(d_x2, x2d, d_ag2i, ag2i, kc_major=True)
        B.do("gpsimd", [ag2i], [ag2o],
             lambda h: h.collective_compute(
                 "AllGather", OP.bypass, replica_groups=[list(range(NCORES))],
                 ins=[d_ag2i.ap().opt()], outs=[d_ag2o.ap().opt()]))

    if upto >= 3:
        B.barrier()
        AR.bot = 0
        AR.top = AR.nbytes
        sdq = AR.carve([128, KC * HC], BF16)
        xsp = [AR.carve([128, KC * 512], BF16) for _ in range(2)]
        gtile = AR.carve([128, 512], BF16)
        sgt = AR.carve([128, 512], F32)
        htile = AR.carve([128, 512], BF16)

        def up_pass(ddq, dqtt, is_up):
            B.dma("sync", sdq, dqtt, iap=ddq[:, :])
            for s in range(16):
                xg = xsp[s % 2]
                r = s // 2
                off = (s % 2) * 512
                srcr = d_ag2o[r * 128:(r + 1) * 128, :].rearrange(
                    "p (kc q) -> p kc q", q=1024)
                for g8 in range(4):
                    B.dma("sync", xg.sub(g8), ag2o,
                          oap=xg.ap.rearrange("p (kc q) -> p kc q", q=512)
                          [:, g8 * 8:(g8 + 1) * 8, :],
                          iap=srcr[:, g8 * 8:(g8 + 1) * 8, off:off + 512])
                for of in range(OFC):
                    bank = banks5[of % 5]
                    B.group("tensor", [sdq, xg], [bank],
                            [lambda h, kc=kc, of=of, bank=bank, xg=xg: h.matmul(
                                bank.ap,
                                lhsT=sdq.ap[:, kc * HC + of * 128:
                                            kc * HC + (of + 1) * 128],
                                rhs=xg.ap[:, kc * 512:(kc + 1) * 512],
                                start=(kc == 0), stop=(kc == KC - 1))
                             for kc in range(KC)])
                    col = of * TOKS + s * 512
                    if not is_up:
                        B.copy("vector", gtile, bank)
                        B.dma("scalar", gated.sub((of, s)), gtile,
                              oap=d_gate[:, col:col + 512])
                    else:
                        B.dma("sync", gtile, gated,
                              iap=d_gate[:, col:col + 512])
                        B.act(sgt, gtile, AF.Silu)
                        B.tt(htile, bank, sgt, OP.mult, aap=bank.ap)
                        B.dma("scalar", hd.sub((of, s)), htile,
                              oap=d_h[:, col:col + 512])

        up_pass(d_dq1, dq1d, is_up=False)
        up_pass(d_dq2, dq2d, is_up=True)

        B.barrier()
        AR.bot = 0
        sdq3 = AR.carve([128, OFC * DM], BF16)
        hb = [AR.carve([128, OFC * 128], BF16) for _ in range(2)]
        pt = [AR.carve([128, 1024], F32) for _ in range(2)]
        B.dma("sync", sdq3, dq3d, iap=d_dq3[:, :])
        for q in range(4):
            if q > 0 and DOWN_QBAR:
                B.barrier()
            for tt2 in range(64):
                hg = hb[tt2 % 2]
                B.dma("sync", hg, hd,
                      oap=hg.ap.rearrange("p (hc t) -> p hc t", t=128),
                      iap=d_h[:, :].rearrange("p (hc q2) -> p hc q2", q2=TOKS)
                      [:, :, tt2 * 128:(tt2 + 1) * 128])
                for half, bank in ((0, psQ), (1, psKV)):
                    B.group("tensor", [sdq3, hg], [bank],
                            [lambda h, hc=hc, half=half, bank=bank, hg=hg, q=q:
                             h.matmul(bank.ap,
                                      lhsT=hg.ap[:, hc * 128:(hc + 1) * 128],
                                      rhs=sdq3.ap[:, hc * DM + q * 1024 + half * 512:
                                                  hc * DM + q * 1024 + (half + 1) * 512],
                                      start=(hc == 0), stop=(hc == OFC - 1))
                             for hc in range(OFC)])
                po = pt[tt2 % 2]
                B.copy("vector", po, psQ, oap=po.ap[:, 0:512])
                B.copy("vector", po, psKV, oap=po.ap[:, 512:1024])
                B.dma("scalar", partd[q].sub(tt2), po,
                      oap=d_part[q][tt2 * 128:(tt2 + 1) * 128, :])
            B.do("gpsimd", [partd[q]], [rsod[q]],
                 lambda h, q=q: h.collective_compute(
                     "ReduceScatter", OP.add,
                     replica_groups=[list(range(NCORES))],
                     ins=[d_part[q].ap().opt()], outs=[d_rso[q].ap().opt()]))

    if upto >= 4:
        B.barrier()
        AR.bot = 0
        x2f = [AR.carve([128, DM], F32) for _ in range(2)]
        yf = [AR.carve([128, DM], F32) for _ in range(2)]
        of_ = [AR.carve([128, DM], F32) for _ in range(2)]
        for tt in range(8):
            xg, yg, og = x2f[tt % 2], yf[tt % 2], of_[tt % 2]
            B.dma("sync", xg, x2d, iap=d_x2[tt * 128:(tt + 1) * 128, :])
            for q in range(4):
                B.dma("sync", yg.sub(q), rsod[q],
                      oap=yg.ap[:, q * 1024:(q + 1) * 1024],
                      iap=d_rso[q][tt * 128:(tt + 1) * 128, :])
            B.tt(og, xg, yg, OP.add)
            B.dma("scalar", outd.sub(tt), og,
                  oap=d_out[tt * 128:(tt + 1) * 128, :])

    B.barrier()
    tapmap = {"ag1i": (d_ag1i, ag1i), "ag1o": (d_ag1o, ag1o),
              "osl": (d_osl, osl), "oex": (d_oex, oex),
              "x2": (d_x2, x2d), "ag2i": (d_ag2i, ag2i), "ag2o": (d_ag2o, ag2o),
              "dq1": (d_dq1, dq1d), "dq2": (d_dq2, dq2d), "dq3": (d_dq3, dq3d),
              "gate": (d_gate, gated), "h": (d_h, hd),
              "part0": (d_part[0], partd[0]), "rso0": (d_rso[0], rsod[0]),
              "rso1": (d_rso[1], rsod[1]), "rso2": (d_rso[2], rsod[2]),
              "rso3": (d_rso[3], rsod[3]), "part1": (d_part[1], partd[1])}
    for name in taps:
        src, stt = tapmap[name]
        tout = nc.dram_tensor(f"tap_{name}", list(src.shape), src.dtype,
                              kind="ExternalOutput")
        B.dma("gpsimd", TT(tout, name=f"tap{name}"), stt,
              oap=tout[:, :], iap=src[:, :])
    for s in B.all_sems:
        if s.n:
            B.wait("gpsimd", (s, s.n, None))
    B.run_block()
    B.close()
    return nc


def _a2a(h, d_in, d_out):
    with allow_ici_all_to_all():
        return h.collective_compute(
            "AllToAll", OP.bypass, replica_groups=[list(range(NCORES))],
            ins=[d_in.ap().opt()], outs=[d_out.ap().opt()])


def _constants():
    pos = np.arange(N_, dtype=np.float64)
    half = np.arange(64, dtype=np.float64)
    inv = 1.0 / (1e4 ** (2.0 * half / 128.0))
    fr = pos[:, None] * inv[None, :]
    emb = np.concatenate([fr, fr], axis=-1)
    cos = np.cos(emb)
    sin = np.sin(emb)
    sin2 = sin.copy()
    sin2[:, 0:64] = -sin[:, 0:64]
    cos_t = cos.reshape(NCHUNK, 128, 128).transpose(1, 0, 2).reshape(128, NCHUNK * 128)
    sin_t = sin2.reshape(NCHUNK, 128, 128).transpose(1, 0, 2).reshape(128, NCHUNK * 128)
    ii = np.arange(128)[:, None]
    jj = np.arange(192)[None, :]
    rel = ii - (jj - 64)
    mask = np.where((rel >= 0) & (rel <= 64), 0.0, -1e30)
    jm = np.arange(128)
    u01 = (jm[:, None] <= jm[None, :]).astype(np.float32)
    cb = np.zeros((128, 16), np.float32)
    cb[:, 0] = 1e-5
    cb[:, 1] = 1.0
    for bi in range(1, 15, 2):
        cb[:, 2 + bi // 2] = -NF4_BOUNDS[bi]
    return (cos_t.astype(np.float32), sin_t.astype(np.float32),
            mask.astype(np.float32), u01, np.eye(128, dtype=ml_dtypes.bfloat16),
            cb)


def prepare_inputs(inputs):
    x = np.ascontiguousarray(
        inputs["L0_attn_hidden_states"].reshape(TOKS, DM).astype(np.float32))
    Wq = inputs["L0_attn_Wq"].astype(np.float32)
    Wk = inputs["L0_attn_Wk"].astype(np.float32)
    Wv = inputs["L0_attn_Wv"].astype(np.float32)
    Wo = np.ascontiguousarray(inputs["L0_attn_Wo"].astype(np.float32))
    w1 = inputs["L0_ffn_w1"].astype(np.float32)
    w2 = inputs["L0_ffn_w2"].astype(np.float32)
    w3 = inputs["L0_ffn_w3"].astype(np.float32)
    w1p = np.vstack([w1, np.zeros((HC * NCORES - w1.shape[0], DM), np.float32)])
    w2p = np.vstack([w2, np.zeros((HC * NCORES - w2.shape[0], DM), np.float32)])
    w3p = np.hstack([w3, np.zeros((DM, HC * NCORES - w3.shape[1]), np.float32)])
    cos_t, sin_t, mask, u01, idb, cb = _constants()
    in_maps = []
    for c in range(NCORES):
        in_maps.append({
            "xop": np.ascontiguousarray(x[c * TPC:(c + 1) * TPC]),
            "wq_sl": np.ascontiguousarray(Wq[:, c * NH * DH:(c + 1) * NH * DH]),
            "wk_sl": np.ascontiguousarray(Wk[:, c * DH:(c + 1) * DH]),
            "wv_sl": np.ascontiguousarray(Wv[:, c * DH:(c + 1) * DH]),
            "wo": Wo,
            "w1_sl": np.ascontiguousarray(w1p[c * HC:(c + 1) * HC]),
            "w2_sl": np.ascontiguousarray(w2p[c * HC:(c + 1) * HC]),
            "w3_sl": np.ascontiguousarray(w3p[:, c * HC:(c + 1) * HC]),
            "cos_t": cos_t, "sin2_t": sin_t, "mask_swa": mask,
            "u01": u01, "identb": idb, "cbias": cb,
        })
    return in_maps


_CACHED = {}


def kernel(**inputs):
    in_maps = prepare_inputs(inputs)
    if "nc" not in _CACHED:
        _CACHED["nc"] = build_program(upto=9)
    nc = _CACHED["nc"]
    res = run_bass_kernel_spmd(nc, in_maps, core_ids=list(range(NCORES)))
    out = np.concatenate([res.results[c]["out_own"] for c in range(NCORES)],
                         axis=0)
    return out.reshape(B_, N_, DM)


# version 07
# speedup vs baseline: 1.1522x; 1.0393x over previous
import numpy as np
import ml_dtypes
import concourse.bass as bass
import concourse.mybir as mybir
from concourse.bass_utils import run_bass_kernel_spmd
from concourse.aot_env import allow_ici_all_to_all

F32 = mybir.dt.float32
BF16 = mybir.dt.bfloat16
FP16 = mybir.dt.float16
OP = mybir.AluOpType
AX = mybir.AxisListType
AF = mybir.ActivationFunctionType

NCORES = 8
B_ = 2
N_ = 4096
DM = 4096
TOKS = B_ * N_
TPC = TOKS // NCORES
NH = 4
DH = 128
NCHUNK = 32
HC = 1408
OFC = HC // 128
KC = DM // 128
INV_SQRT_D = 0.08838834764831845

_NF4 = np.array([-1.0, -0.6961928009986877, -0.5250730514526367,
    -0.39491748809814453, -0.28444138169288635, -0.18477343022823334,
    -0.09105003625154495, 0.0, 0.07958029955625534, 0.16093020141124725,
    0.24611230194568634, 0.33791524171829224, 0.44070982933044434,
    0.5626170039176941, 0.7229568362236023, 1.0], dtype=np.float64)
NF4_BOUNDS = ((_NF4[:-1] + _NF4[1:]) * 0.5).tolist()
NF4_DELTAS = np.diff(_NF4).tolist()

ENGINES = ("sync", "tensor", "scalar", "vector", "gpsimd")
CH_STAGE = 99
DOWN_QBAR = False


class Sem:
    def __init__(self, h):
        self.h = h
        self.n = 0
        self.is_dma = False


class Tile:
    __slots__ = ("wr", "readers", "name", "subs", "parent", "dsem")

    def __init__(self, name=""):
        self.wr = None
        self.readers = []
        self.name = name
        self.subs = {}
        self.parent = None
        self.dsem = None


class TT:
    __slots__ = ("ap", "t", "bf")

    def __init__(self, ap, tile=None, name=""):
        self.ap = ap
        self.t = tile if tile is not None else Tile(name)

    def v(self, ap):
        return TT(ap, self.t)

    def sub(self, key):
        st = self.t.subs.get(key)
        if st is None:
            st = Tile(f"{self.t.name}:{key}")
            st.parent = self.t
            self.t.subs[key] = st
        return TT(self.ap, st)


class Builder:
    def __init__(self, nc):
        self.nc = nc
        self.prog = {e: [] for e in ENGINES}
        self.waited = {e: {} for e in ENGINES}
        self._cms = []
        self.qL = self.sem("qL")
        self.qS = self.sem("qS")
        self.qG = self.sem("qG")
        self.cPE = self.sem("cPE")
        self.cACT = self.sem("cACT")
        self.cDVE = self.sem("cDVE")
        self.cGP = self.sem("cGP")
        self.chain = {"tensor": self.cPE, "scalar": self.cACT,
                      "vector": self.cDVE, "gpsimd": self.cGP, "sync": self.cGP}
        self.qsem = {"sync": self.qL, "scalar": self.qS, "gpsimd": self.qG}
        self.all_sems = [self.qL, self.qS, self.qG, self.cPE, self.cACT,
                         self.cDVE, self.cGP]
        self._dsem_n = 0

    def _dma_sem(self, tile):
        top = tile.parent if tile.parent is not None else tile
        if top.dsem is None:
            top.dsem = self.sem(f"d{self._dsem_n}")
            top.dsem.is_dma = True
            self._dsem_n += 1
            self.all_sems.append(top.dsem)
        return top.dsem

    def sem(self, name):
        cm = self.nc.semaphore(name)
        h = cm.__enter__()
        self._cms.append(cm)
        return Sem(h)

    def close(self):
        for cm in reversed(self._cms):
            cm.__exit__(None, None, None)

    def emit(self, eng, fn):
        self.prog[eng].append(fn)

    def wait(self, eng, ev):
        sem, val, peng = ev
        if sem.is_dma:
            val = sem.n
        if val <= 0 or self.waited[eng].get(sem, 0) >= val:
            return
        if eng == "tensor" and peng == "tensor":
            return
        self.waited[eng][sem] = val
        self.emit(eng, lambda h, s=sem.h, v=val: h.wait_ge(s, v))

    def _pre(self, eng, ins, outs):
        for t in ins:
            if t.wr is not None:
                self.wait(eng, t.wr)
        for t in outs:
            for ev in t.readers:
                self.wait(eng, ev)
            if t.wr is not None:
                self.wait(eng, t.wr)
            if t.parent is not None:
                for ev in t.parent.readers:
                    self.wait(eng, ev)

    def _post(self, eng, ins, outs, sem):
        ev = (sem, sem.n, eng)
        for t in ins:
            t.readers.append(ev)
        for t in outs:
            t.wr = ev
            t.readers = []
            if t.parent is not None:
                t.parent.wr = ev

    def do(self, eng, ins, outs, fn, dma=False):
        ins = [x.t if isinstance(x, TT) else x for x in ins]
        outs = [x.t if isinstance(x, TT) else x for x in outs]
        self._pre(eng, ins, outs)
        if dma:
            sem = self._dma_sem(outs[0])
        else:
            sem = self.chain[eng]
        k = 16 if dma else 1
        sem.n += k
        self.emit(eng, lambda h, f=fn, s=sem.h, kk=k: f(h).then_inc(s, kk))
        self._post(eng, ins, outs, sem)

    def group(self, eng, ins, outs, fns):
        ins = [x.t if isinstance(x, TT) else x for x in ins]
        outs = [x.t if isinstance(x, TT) else x for x in outs]
        self._pre(eng, ins, outs)
        sem = self.chain[eng]
        sem.n += 1

        def run(h, fs=tuple(fns), s=sem.h):
            last = None
            for f in fs:
                last = f(h)
            last.then_inc(s, 1)
        self.emit(eng, run)
        self._post(eng, ins, outs, sem)

    def barrier(self):
        evs = [(s, s.n, None) for s in self.all_sems]
        for e in ENGINES:
            for ev in evs:
                self.wait(e, ev)

    def run_block(self):
        with self.nc.Block() as block:
            @block.sync
            def _(h):
                for fn in self.prog["sync"]:
                    fn(h)

            @block.tensor
            def _(h):
                for fn in self.prog["tensor"]:
                    fn(h)

            @block.scalar
            def _(h):
                for fn in self.prog["scalar"]:
                    fn(h)

            @block.vector
            def _(h):
                for fn in self.prog["vector"]:
                    fn(h)

            @block.gpsimd
            def _(h):
                for fn in self.prog["gpsimd"]:
                    fn(h)

    def dma(self, eng, out, in_, oap=None, iap=None):
        o = oap if oap is not None else out.ap
        i = iap if iap is not None else in_.ap
        self.do(eng, [in_], [out], lambda h, o=o, i=i: h.dma_start(out=o, in_=i),
                dma=True)

    def copy(self, eng, out, in_, oap=None, iap=None):
        o = oap if oap is not None else out.ap
        i = iap if iap is not None else in_.ap
        if eng == "vector":
            self.do(eng, [in_], [out], lambda h, o=o, i=i: h.tensor_copy(o, i))
        else:
            self.do(eng, [in_], [out],
                    lambda h, o=o, i=i: h.activation(o, i, AF.Identity))

    def tt(self, out, a, b, op, oap=None, aap=None, bap=None, bdep=()):
        o = oap if oap is not None else out.ap
        x = aap if aap is not None else a.ap
        y = bap if bap is not None else (b.ap if isinstance(b, TT) else b)
        ins = [a] + ([b] if isinstance(b, TT) else []) + list(bdep)
        self.do("vector", ins, [out],
                lambda h, o=o, x=x, y=y, op=op: h.tensor_tensor(o, x, y, op))

    def ts(self, out, in_, s1, s2, op0, op1=None, oap=None, iap=None, sdep=()):
        o = oap if oap is not None else out.ap
        i = iap if iap is not None else in_.ap
        if op1 is None:
            self.do("vector", [in_, *sdep], [out],
                    lambda h, o=o, i=i, s1=s1, op0=op0:
                    h.tensor_single_scalar(o, i, s1, op0))
        else:
            self.do("vector", [in_, *sdep], [out],
                    lambda h, o=o, i=i, s1=s1, s2=s2, op0=op0, op1=op1:
                    h.tensor_scalar(o, i, s1, s2, op0, op1))

    def stt(self, out, in0, s, in1, op0, op1, oap=None, i0ap=None, i1ap=None,
            sdep=()):
        o = oap if oap is not None else out.ap
        a = i0ap if i0ap is not None else in0.ap
        b = i1ap if i1ap is not None else in1.ap
        self.do("vector", [in0, in1, *sdep], [out],
                lambda h, o=o, a=a, b=b, s=s, op0=op0, op1=op1:
                h.scalar_tensor_tensor(o, a, s, b, op0, op1))

    def red(self, out, in_, op, oap=None, iap=None, absval=False, neg=False):
        o = oap if oap is not None else out.ap
        i = iap if iap is not None else in_.ap
        self.do("vector", [in_], [out],
                lambda h, o=o, i=i, op=op, a=absval, n=neg: h.tensor_reduce(
                    o, i, axis=AX.X, op=op, apply_absolute_value=a, negate=n))

    def recip(self, out, in_, oap=None, iap=None):
        o = oap if oap is not None else out.ap
        i = iap if iap is not None else in_.ap
        self.do("vector", [in_], [out], lambda h, o=o, i=i: h.reciprocal(o, i))

    def act(self, out, in_, func, bias=0.0, scale=1.0, accum=None, oap=None,
            iap=None, sdep=(), aout=None):
        o = oap if oap is not None else out.ap
        i = iap if iap is not None else in_.ap
        outs = [out] + ([aout] if aout is not None else [])
        self.do("scalar", [in_, *sdep], outs,
                lambda h, o=o, i=i, f=func, b=bias, s=scale, a=accum:
                h.activation(o, i, f, bias=b, scale=s, accum_out=a))

    def scan_add(self, out, in_, oap=None, iap=None):
        o = oap if oap is not None else out.ap
        i = iap if iap is not None else in_.ap
        self.do("vector", [in_], [out],
                lambda h, o=o, i=i: h.tensor_tensor_scan(o, i, i, 0.0, OP.add,
                                                         OP.bypass))

    def trans8(self, psb, pairs, idb):
        fns = []
        ins = [s for (_, _, s, _) in pairs] + [idb]
        for j, (_, dap, _, sap) in enumerate(pairs):
            w = sap.shape[-1]
            p0 = dap.base_partition()
            fns.append(lambda h, o=psb.ap[p0:p0 + w, j * 128:(j + 1) * 128],
                       i=sap, idn=idb.ap: h.transpose(o, i, idn))
        self.group("tensor", ins, [psb], fns)
        for j, (dst, dap, _, sap) in enumerate(pairs):
            w = sap.shape[-1]
            p0 = dap.base_partition()
            self.copy("scalar", dst.v(dap) if dap is not dst.ap else dst, psb,
                      oap=dap, iap=psb.ap[p0:p0 + w, j * 128:(j + 1) * 128])


class Arena:
    def __init__(self, nc, name, nbytes):
        self.nbytes = nbytes
        self.t = nc.alloc_sbuf_tensor(name, [128, nbytes // 2], BF16)
        self.bot = 0
        self.top = nbytes

    def carve(self, shape, dtype, top=False):
        esz = {F32: 4, BF16: 2, FP16: 2}[dtype]
        nfree = int(np.prod(shape[1:]))
        nb = (nfree * esz + 63) & ~63
        if top:
            self.top -= nb
            off = self.top
        else:
            off = self.bot
            self.bot += nb
        assert self.bot <= self.top, f"arena overflow bot={self.bot} top={self.top}"
        base = self.t[:][:, off // 2: off // 2 + nfree * esz // 2]
        ap = base.bitcast(dtype) if dtype != BF16 else base
        if len(shape) > 2:
            dims = " ".join(f"d{i}" for i in range(1, len(shape)))
            ap = ap.rearrange(f"p ({dims}) -> p {dims}",
                              **{f"d{i}": shape[i] for i in range(1, len(shape))})
        return TT(ap, name=f"ar@{off}")


def build_program(upto=9, taps=(), nb=B_, nchunks=NCHUNK, do_quant=True, do_a2a=True):
    nc = bass.Bass("TRN2", target_bir_lowering=False)
    B = Builder(nc)

    d_x = nc.dram_tensor("xop", [TPC, DM], F32, kind="ExternalInput")
    d_wq = nc.dram_tensor("wq_sl", [DM, NH * DH], F32, kind="ExternalInput")
    d_wk = nc.dram_tensor("wk_sl", [DM, DH], F32, kind="ExternalInput")
    d_wv = nc.dram_tensor("wv_sl", [DM, DH], F32, kind="ExternalInput")
    d_wo = nc.dram_tensor("wo", [DM, DM], F32, kind="ExternalInput")
    d_w1 = nc.dram_tensor("w1_sl", [HC, DM], F32, kind="ExternalInput")
    d_w2 = nc.dram_tensor("w2_sl", [HC, DM], F32, kind="ExternalInput")
    d_w3 = nc.dram_tensor("w3_sl", [DM, HC], F32, kind="ExternalInput")
    d_cos = nc.dram_tensor("cos_t", [128, NCHUNK * DH], F32, kind="ExternalInput")
    d_sin = nc.dram_tensor("sin2_t", [128, NCHUNK * DH], F32, kind="ExternalInput")
    d_mask = nc.dram_tensor("mask_swa", [128, 192], F32, kind="ExternalInput")
    d_u01 = nc.dram_tensor("u01", [128, 128], F32, kind="ExternalInput")
    d_idb = nc.dram_tensor("identb", [128, 128], BF16, kind="ExternalInput")
    d_cb = nc.dram_tensor("cbias", [128, 16], F32, kind="ExternalInput")
    d_out = nc.dram_tensor("out_own", [TPC, DM], F32, kind="ExternalOutput")

    d_ag1i = nc.dram_tensor("ag1i", [128, 32768], BF16)
    d_ag1o = nc.dram_tensor("ag1o", [1024, 32768], BF16)
    d_osl = nc.dram_tensor("osl", [1024, NH * TPC], BF16)
    d_oex = nc.dram_tensor("oex", [1024, NH * TPC], BF16)
    d_x2 = nc.dram_tensor("x2own", [TPC, DM], F32)
    d_ag2i = nc.dram_tensor("ag2i", [128, 32768], BF16)
    d_ag2o = nc.dram_tensor("ag2o", [1024, 32768], BF16)
    d_dq1 = nc.dram_tensor("dq1T", [128, KC * HC], BF16)
    d_dq2 = nc.dram_tensor("dq2T", [128, KC * HC], BF16)
    d_dq3 = nc.dram_tensor("dq3T", [128, OFC * DM], BF16)
    d_gate = nc.dram_tensor("gateb", [128, OFC * TOKS], BF16)
    d_h = nc.dram_tensor("hb", [128, OFC * TOKS], BF16)
    d_part = [nc.dram_tensor(f"part{q}", [TOKS, 1024], F32) for q in range(4)]
    d_rso = [nc.dram_tensor(f"rso{q}", [TPC, 1024], F32) for q in range(4)]

    ag1i = TT(d_ag1i, name="ag1i"); ag1o = TT(d_ag1o, name="ag1o")
    osl = TT(d_osl, name="osl"); oex = TT(d_oex, name="oex")
    x2d = TT(d_x2, name="x2"); ag2i = TT(d_ag2i, name="ag2i")
    ag2o = TT(d_ag2o, name="ag2o")
    dq1d = TT(d_dq1, name="dq1"); dq2d = TT(d_dq2, name="dq2")
    dq3d = TT(d_dq3, name="dq3")
    gated = TT(d_gate, name="gate"); hd = TT(d_h, name="h")
    partd = [TT(d_part[q], name=f"part{q}") for q in range(4)]
    rsod = [TT(d_rso[q], name=f"rso{q}") for q in range(4)]
    outd = TT(d_out, name="out")
    xin = TT(d_x, name="xin"); wod = TT(d_wo, name="wo")
    w1d = TT(d_w1, name="w1"); w2d = TT(d_w2, name="w2"); w3d = TT(d_w3, name="w3")

    s_cos = TT(nc.alloc_sbuf_tensor("s_cos", [128, NCHUNK * DH], F32)[:])
    s_sin = TT(nc.alloc_sbuf_tensor("s_sin", [128, NCHUNK * DH], F32)[:])
    s_mask = TT(nc.alloc_sbuf_tensor("s_mask", [128, 192], F32)[:])
    s_u01 = TT(nc.alloc_sbuf_tensor("s_u01", [128, 128], F32)[:])
    s_idb = TT(nc.alloc_sbuf_tensor("s_idb", [128, 128], BF16)[:])
    s_cb = TT(nc.alloc_sbuf_tensor("s_cb", [128, 16], F32)[:])
    AR = Arena(nc, "AR", 163 * 1024)

    psQs = [TT(nc.alloc_psum_tensor(f"psQ{i}", [128, 512], F32)[:])
            for i in range(2)]
    psKVs = [TT(nc.alloc_psum_tensor(f"psKV{i}", [128, 512], F32)[:])
             for i in range(2)]
    psOs = [TT(nc.alloc_psum_tensor(f"psO{i}", [128, 512], F32)[:])
            for i in range(2)]
    psWs = [TT(nc.alloc_psum_tensor(f"psW{i}", [128, 512], F32)[:])
            for i in range(2)]
    for w in psWs:
        w.bf = w.ap.bitcast(BF16)
    psQ, psKV, psO = psQs[0], psKVs[0], psOs[0]
    psTR = TT.__new__(TT); psTR.ap = psWs[0].bf; psTR.t = psWs[0].t
    psQT = psTR
    banks5 = [psQs[0], psKVs[0], psOs[0], psQs[1], psKVs[1]]

    B.dma("sync", s_cos, TT(d_cos, name="cosi"), iap=d_cos[:, :])
    B.dma("sync", s_sin, TT(d_sin, name="sini"), iap=d_sin[:, :])
    B.dma("sync", s_mask, TT(d_mask, name="mski"), iap=d_mask[:, :])
    B.dma("sync", s_u01, TT(d_u01, name="u01i"), iap=d_u01[:, :])
    B.dma("sync", s_idb, TT(d_idb, name="idbi"), iap=d_idb[:, :])
    B.dma("sync", s_cb, TT(d_cb, name="cbi"), iap=d_cb[:, :])

    qzw = [AR.carve([128, 2048], F32, top=True) for _ in range(1)]
    qzv = AR.carve([128, 2048], F32, top=True)
    qzacc = AR.carve([128, 2048], FP16, top=True)
    qzac2 = AR.carve([128, 2048], FP16, top=True)
    qzt = [AR.carve([128, 2048], FP16, top=True) for _ in range(2)]
    qzam = AR.carve([128, 64], F32, top=True)
    qzrec = AR.carve([128, 64], F32, top=True)
    qzdq = AR.carve([128, 2048], BF16, top=True)
    qztt = AR.carve([128, 2048], BF16, top=True)
    top_quant = AR.top
    s_wq = AR.carve([128, KC * NH * DH], BF16, top=True)
    s_wk = AR.carve([128, KC * DH], BF16, top=True)
    s_wv = AR.carve([128, KC * DH], BF16, top=True)
    wq_r = d_wq[:, :].rearrange("(kc p) f -> p kc f", p=128)
    wqin = TT(d_wq, name="wqi"); wkin = TT(d_wk, name="wki")
    wvin = TT(d_wv, name="wvi")
    wq_s = s_wq.ap.rearrange("p (kc f) -> p kc f", f=NH * DH)
    for j in range(4):
        B.dma("gpsimd", s_wq, wqin, oap=wq_s[:, j * 8:(j + 1) * 8, :],
              iap=wq_r[:, j * 8:(j + 1) * 8, :])
    B.dma("gpsimd", s_wk, wkin,
          oap=s_wk.ap.rearrange("p (kc f) -> p kc f", f=DH),
          iap=d_wk[:, :].rearrange("(kc p) f -> p kc f", p=128))
    B.dma("gpsimd", s_wv, wvin,
          oap=s_wv.ap.rearrange("p (kc f) -> p kc f", f=DH),
          iap=d_wv[:, :].rearrange("(kc p) f -> p kc f", p=128))


    def ln_phase(src_d, src_tt, dst_d, dst_tt, kc_major):
        bot0 = AR.bot
        xt = [AR.carve([128, DM], F32) for _ in range(2)]
        sq = AR.carve([128, DM], F32)
        st = AR.carve([128, 16], F32)
        xlb = AR.carve([128, DM], BF16)
        xlt = AR.carve([128, DM], BF16)
        c_s1, c_s2, c_m, c_mm, c_v, c_rs, c_rl, c_b2 = (
            st.v(st.ap[:, i:i + 1]) for i in range(8))
        for tt in range(8):
            x = xt[tt % 2]
            B.dma("sync", x, src_tt, iap=src_d[tt * 128:(tt + 1) * 128, :])
            B.red(c_s1, x, OP.add)
            B.act(sq, x, AF.Square, accum=c_s2.ap, aout=c_s2)
            B.ts(c_m, c_s1, 1.0 / DM, None, OP.mult)
            B.ts(c_v, c_s2, 1.0 / DM, None, OP.mult)
            B.tt(c_mm, c_m, c_m, OP.mult)
            B.tt(c_v, c_v, c_mm, OP.subtract)
            B.act(c_rl, c_v, AF.Ln, bias=s_cb.ap[:, 0:1], sdep=[s_cb])
            B.act(c_rs, c_rl, AF.Exp, scale=-0.5)
            B.stt(c_b2, c_m, -1.0, c_rs, OP.mult, OP.mult)
            B.act(xlb, x, AF.Identity, scale=c_rs.ap, bias=c_b2.ap,
                  sdep=[c_rs, c_b2])
            for grp in range(4):
                pairs = [(xlt, xlt.ap[:, (grp * 8 + j) * 128:(grp * 8 + j + 1) * 128],
                          xlb, xlb.ap[:, (grp * 8 + j) * 128:(grp * 8 + j + 1) * 128])
                         for j in range(8)]
                B.trans8(psTR, pairs, s_idb)
            if kc_major:
                dstr = dst_d[:, :].rearrange("p (kc q) -> p kc q", q=1024)
                srcr = xlt.ap.rearrange("p (kc t) -> p kc t", t=128)
                for g4 in range(4):
                    B.dma("scalar", dst_tt.sub((tt, g4)), xlt,
                          oap=dstr[:, g4 * 8:(g4 + 1) * 8, tt * 128:(tt + 1) * 128],
                          iap=srcr[:, g4 * 8:(g4 + 1) * 8, :])
            else:
                B.dma("scalar", dst_tt.sub(tt), xlt,
                      oap=dst_d[:, tt * DM:(tt + 1) * DM], iap=xlt.ap)
        AR.bot = bot0

    ln_phase(d_x, xin, d_ag1i, ag1i, kc_major=False)
    B.do("gpsimd", [ag1i], [ag1o],
         lambda h: h.collective_compute(
             "AllGather", OP.bypass, replica_groups=[list(range(NCORES))],
             ins=[d_ag1i.ap().opt()], outs=[d_ag1o.ap().opt()]))

    def quant_tiles():
        for dsrc, wtt, dqd, ddq in ((d_w1, w1d, dq1d, d_dq1),
                                    (d_w2, w2d, dq2d, d_dq2)):
            for hc in range(OFC):
                for half in range(2):
                    yield ("w12", dsrc, wtt, dqd, ddq, hc, half)
        for dc in range(KC):
            yield ("w3", d_w3, w3d, dq3d, d_dq3, dc, 0)
    qgen = quant_tiles()
    qstate = {"i": 0, "done": False, "w12_left": 2 * OFC * 2}

    def emit_quant_tile(w12_only=False):
        if qstate["done"]:
            return False
        if w12_only and qstate["w12_left"] <= 0:
            return False
        try:
            wsel, dsrc, wtt, dqd, ddq, rc, half = next(qgen)
        except StopIteration:
            qstate["done"] = True
            return False
        if wsel == "w12":
            qstate["w12_left"] -= 1
        i = qstate["i"]; qstate["i"] += 1
        wbuf = qzw[0]
        ncol = 2048 if wsel == "w12" else HC
        nblk = ncol // 64
        B.dma("sync", wbuf, wtt, oap=wbuf.ap[:, 0:ncol],
              iap=dsrc[rc * 128:(rc + 1) * 128, half * 2048: half * 2048 + ncol])
        w3v = wbuf.ap[:, 0:ncol].rearrange("p (g e) -> p g e", e=64)
        B.red(qzam, wbuf, OP.max, oap=qzam.ap[:, 0:nblk], iap=w3v, absval=True)
        B.ts(qzam, qzam, 1e-30, None, OP.max, oap=qzam.ap[:, 0:nblk],
             iap=qzam.ap[:, 0:nblk])
        B.recip(qzrec, qzam, oap=qzrec.ap[:, 0:nblk], iap=qzam.ap[:, 0:nblk])
        rb = qzrec.ap[:, 0:nblk][:, :, None].broadcast_to([128, nblk, 64])
        B.tt(qzv, wbuf, qzrec, OP.mult,
             oap=qzv.ap[:, 0:ncol].rearrange("p (g e) -> p g e", e=64),
             aap=w3v, bap=rb)
        vv = qzv.ap[:, 0:ncol]
        ne = 0
        for bi in range(0, 15, 2):
            b_, dl = NF4_BOUNDS[bi], NF4_DELTAS[bi]
            tb = qzt[0]
            B.ts(tb, qzv, b_, dl, OP.is_gt, OP.mult,
                 oap=tb.ap[:, 0:ncol], iap=vv)
            if ne == 0:
                B.copy("vector", qzacc, tb, oap=qzacc.ap[:, 0:ncol],
                       iap=tb.ap[:, 0:ncol])
            else:
                B.tt(qzacc, qzacc, tb, OP.add, oap=qzacc.ap[:, 0:ncol],
                     aap=qzacc.ap[:, 0:ncol], bap=tb.ap[:, 0:ncol])
            ne += 1
            if bi + 1 < 15:
                b2, dl2 = NF4_BOUNDS[bi + 1], NF4_DELTAS[bi + 1]
                tb2 = qzt[1]
                B.act(tb2, qzv, AF.Sign,
                      bias=s_cb.ap[:, 2 + bi // 2:3 + bi // 2],
                      oap=tb2.ap[:, 0:ncol], iap=vv, sdep=[s_cb])
                if ne == 1:
                    B.ts(qzac2, tb2, dl2 / 2.0, None, OP.mult,
                         oap=qzac2.ap[:, 0:ncol], iap=tb2.ap[:, 0:ncol])
                else:
                    B.stt(qzac2, tb2, dl2 / 2.0, qzac2, OP.mult, OP.add,
                          oap=qzac2.ap[:, 0:ncol], i0ap=tb2.ap[:, 0:ncol],
                          i1ap=qzac2.ap[:, 0:ncol])
        B.tt(qzacc, qzacc, qzac2, OP.add, oap=qzacc.ap[:, 0:ncol],
             aap=qzacc.ap[:, 0:ncol], bap=qzac2.ap[:, 0:ncol])
        offs = -1.0 + sum(NF4_DELTAS[bi] / 2.0 for bi in range(1, 15, 2))
        amb = qzam.ap[:, 0:nblk][:, :, None].broadcast_to([128, nblk, 64])
        B.stt(qzdq, qzacc, offs, qzam, OP.add, OP.mult,
              oap=qzdq.ap[:, 0:ncol].rearrange("p (g e) -> p g e", e=64),
              i0ap=qzacc.ap[:, 0:ncol].rearrange("p (g e) -> p g e", e=64),
              i1ap=amb)
        nj = ncol // 128
        for grp in range((nj + 7) // 8):
            jn = min(8, nj - grp * 8)
            fns = [lambda h, j=j, jj=grp * 8 + j: h.transpose(
                psQT.ap[:, j * 128:(j + 1) * 128],
                qzdq.ap[:, jj * 128:(jj + 1) * 128], s_idb.ap)
                for j in range(jn)]
            B.group("tensor", [qzdq, s_idb], [psQT], fns)
            B.copy("scalar", qztt, psQT,
                   oap=qztt.ap[:, grp * 1024:grp * 1024 + jn * 128],
                   iap=psQT.ap[:, 0:jn * 128])
        if wsel == "w12":
            dstr = ddq[:, :].rearrange("p (kc q) -> p kc q", q=HC)
            B.dma("scalar", dqd.sub((rc, half)), qztt,
                  oap=dstr[:, half * 16:(half + 1) * 16, rc * 128:(rc + 1) * 128],
                  iap=qztt.ap.rearrange("p (j t) -> p j t", t=128)[:, 0:16, :])
        else:
            dstr = ddq[:, :].rearrange("p (hc q) -> p hc q", q=DM)
            B.dma("scalar", dqd.sub((rc, 0)), qztt,
                  oap=dstr[:, 0:OFC, rc * 128:(rc + 1) * 128],
                  iap=qztt.ap.rearrange("p (j t) -> p j t", t=128)[:, 0:OFC, :])
        return True

    if not do_quant:
        qstate["done"] = True
    if upto >= 1:
        bot0 = AR.bot
        xfm = [AR.carve([128, DM], BF16) for _ in range(2)]

        def mk_stream():
            import types
            T = types.SimpleNamespace()
            T.qtok = AR.carve([128, 512], F32)
            T.ktok = AR.carve([128, 128], F32)
            T.vtok = [AR.carve([128, 128], BF16) for _ in range(2)]
            T.skfm = [AR.carve([128, 128], BF16) for _ in range(2)]
            T.red = AR.carve([128, 16], F32)
            T.ksn = AR.carve([128, 128], F32)
            T.skf = AR.carve([128, 128], F32)
            T.gst = AR.carve([128, 128], F32)
            T.lsg = [AR.carve([128, 128], F32) for _ in range(2)]
            T.sqf = AR.carve([128, 512], F32)
            T.rtmp = AR.carve([128, 512], F32)
            T.qsb = AR.carve([128, 512], BF16)
            T.sqb = AR.carve([128, 512], BF16)
            T.ksb = AR.carve([128, 128], BF16)
            T.gsb = AR.carve([128, 128], BF16)
            T.skb = AR.carve([128, 128], BF16)
            T.qsfm = AR.carve([128, 512], BF16)
            T.sqfm = AR.carve([128, 512], BF16)
            T.ksfm = AR.carve([128, 128], BF16)
            T.gsfm = AR.carve([128, 128], BF16)
            T.Gfm = AR.carve([128, 128], F32)
            T.tG = AR.carve([128, 128], F32)
            T.eG = AR.carve([128, 128], F32)
            T.eNG = AR.carve([128, 128], F32)
            T.e2t = AR.carve([128, 128], F32)
            T.qtfm = AR.carve([128, 512], BF16)
            T.kbfm = AR.carve([128, 128], BF16)
            T.khfm = AR.carve([128, 128], BF16)
            T.khtok = AR.carve([128, 128], BF16)
            T.Sf = AR.carve([128, 128], F32)
            T.Sb = AR.carve([128, 128], BF16)
            T.Ssc = AR.carve([128, 128], F32)
            T.pglat = AR.carve([128, 512], BF16)
            T.scm = AR.carve([128, 384], F32)
            T.pswa = AR.carve([128, 384], BF16)
            T.ptc = AR.carve([128, 128], BF16)
            T.ptc2 = AR.carve([128, 128], BF16)
            T.ptp = AR.carve([128, 128], BF16)
            T.ptp2 = AR.carve([128, 128], BF16)
            T.obf = AR.carve([128, 512], BF16)
            return T

        TS = [mk_stream() for _ in range(nb)]
        globals()["DBG_A"] = {"TS": TS, "xfm": xfm}
        psWb = []
        for w in psWs:
            v = TT.__new__(TT); v.ap = w.bf; v.t = w.t
            psWb.append(v)

        def st0(b, m, T, x):
            psQb, psKVb = psQs[b], psKVs[b]
            t0 = b * N_ + m * 128
            rank, tti = t0 // TPC, (t0 % TPC) // 128
            B.dma("sync", x, ag1o,
                  iap=d_ag1o[rank * 128:(rank + 1) * 128,
                             tti * DM:(tti + 1) * DM])
            xv = x.ap.rearrange("p (kc t) -> p kc t", t=128)
            B.group("tensor", [x, s_wq], [psQb],
                    [lambda h, kc=kc, xv=xv, o=psQb.ap: h.matmul(
                        o, lhsT=xv[:, kc, :],
                        rhs=s_wq.ap[:, kc * 512:(kc + 1) * 512],
                        start=(kc == 0), stop=(kc == KC - 1))
                     for kc in range(KC)])
            fns = [lambda h, kc=kc, xv=xv, o=psKVb.ap: h.matmul(
                o[:, 0:128], lhsT=xv[:, kc, :],
                rhs=s_wk.ap[:, kc * 128:(kc + 1) * 128],
                start=(kc == 0), stop=(kc == KC - 1)) for kc in range(KC)]
            fns += [lambda h, kc=kc, xv=xv, o=psKVb.ap: h.matmul(
                o[:, 128:256], lhsT=xv[:, kc, :],
                rhs=s_wv.ap[:, kc * 128:(kc + 1) * 128],
                start=(kc == 0), stop=(kc == KC - 1)) for kc in range(KC)]
            B.group("tensor", [x, s_wk, s_wv], [psKVb], fns)
            B.copy("vector", T.qtok, psQb)
            B.copy("vector", T.ktok, psKVb, iap=psKVb.ap[:, 0:128])
            B.copy("vector", T.vtok[m % 2], psKVb, iap=psKVb.ap[:, 128:256])

        def st1a(b, m, T):
            qtok, ktok, red = T.qtok, T.ktok, T.red
            q4 = qtok.ap.rearrange("p (hh t) -> p hh t", t=128)
            for hh in range(4):
                B.act(T.qsb, qtok, AF.Exp,
                      accum=red.ap[:, 4 + hh:5 + hh],
                      oap=T.qsb.ap[:, hh * 128:(hh + 1) * 128],
                      iap=qtok.ap[:, hh * 128:(hh + 1) * 128], aout=red)
            B.act(T.ksn, ktok, AF.Exp,
                  accum=red.ap[:, 13:14], aout=red)
            B.recip(red, red, oap=red.ap[:, 8:12], iap=red.ap[:, 4:8])
            B.recip(red, red, oap=red.ap[:, 14:15], iap=red.ap[:, 13:14])
            cosc = s_cos.ap[:, m * 128:(m + 1) * 128]
            sinc = s_sin.ap[:, m * 128:(m + 1) * 128]
            cb4 = cosc[:, None, :].broadcast_to([128, 4, 128])
            s4 = T.sqf.ap.rearrange("p (hh t) -> p hh t", t=128)
            B.tt(T.sqf, qtok, s_cos, OP.mult, oap=s4, aap=q4, bap=cb4)
            q4h = qtok.ap.rearrange("p (hh hf t) -> p hh hf t", hf=2, t=64)
            r4h = T.rtmp.ap.rearrange("p (hh hf t) -> p hh hf t", hf=2, t=64)
            sinh = sinc.rearrange("p (hf t) -> p hf t", t=64)
            for hf in range(2):
                sv = sinh[:, hf, :][:, None, :].broadcast_to([128, 4, 64])
                B.tt(T.rtmp, qtok, s_sin, OP.mult, oap=r4h[:, :, hf, :],
                     aap=q4h[:, :, 1 - hf, :], bap=sv)
            B.tt(T.sqb, T.sqf, T.rtmp, OP.add)
            B.act(T.lsg[0], ktok, AF.Abs)
            B.act(T.lsg[1], T.lsg[0], AF.Exp, scale=-1.0)
            B.act(T.lsg[1], T.lsg[1], AF.Ln, bias=s_cb.ap[:, 1:2], sdep=[s_cb])
            B.ts(T.gst, ktok, 0.0, None, OP.min)

        def st1b(b, m, T):
            qtok, ktok, red = T.qtok, T.ktok, T.red
            for hh in range(4):
                B.act(T.qsb, T.qsb, AF.Identity, scale=red.ap[:, 8 + hh:9 + hh],
                      oap=T.qsb.ap[:, hh * 128:(hh + 1) * 128],
                      iap=T.qsb.ap[:, hh * 128:(hh + 1) * 128], sdep=[red])
            B.act(T.ksb, T.ksn, AF.Identity, scale=red.ap[:, 14:15], sdep=[red])
            B.tt(T.gsb, T.gst, T.lsg[1], OP.subtract)
            cosc = s_cos.ap[:, m * 128:(m + 1) * 128]
            sinc = s_sin.ap[:, m * 128:(m + 1) * 128]
            sinh = sinc.rearrange("p (hf t) -> p hf t", t=64)
            B.tt(T.skf, ktok, s_cos, OP.mult, bap=cosc)
            for hf in range(2):
                B.tt(T.rtmp, ktok, s_sin, OP.mult,
                     oap=T.rtmp.ap[:, hf * 64:(hf + 1) * 64],
                     aap=ktok.ap[:, (1 - hf) * 64:(2 - hf) * 64],
                     bap=sinh[:, hf, :])
            B.tt(T.skb, T.skf, T.rtmp, OP.add, bap=T.rtmp.ap[:, 0:128])

        def st2a(b, m, T):
            W = psWb[b]
            cur = m % 2
            B.trans8(W, [(T.qsfm, T.qsfm.ap[:, j * 128:(j + 1) * 128],
                          T.qsb, T.qsb.ap[:, j * 128:(j + 1) * 128])
                         for j in range(4)] +
                        [(T.sqfm, T.sqfm.ap[:, j * 128:(j + 1) * 128],
                          T.sqb, T.sqb.ap[:, j * 128:(j + 1) * 128])
                         for j in range(4)], s_idb)

        def st2b(b, m, T):
            W = psWb[b]
            cur = m % 2
            B.trans8(W, [(T.ksfm, T.ksfm.ap, T.ksb, T.ksb.ap),
                         (T.skfm[cur], T.skfm[cur].ap, T.skb, T.skb.ap),
                         (T.gsfm, T.gsfm.ap, T.gsb, T.gsb.ap)], s_idb)
            B.scan_add(T.Gfm, T.gsfm)

        def st2c(b, m, T):
            W = psWb[b]
            B.act(T.eG, T.Gfm, AF.Exp, scale=1.0 / 16.0)
            B.act(T.eNG, T.Gfm, AF.Exp, scale=-1.0 / 16.0)
            B.ts(T.tG, T.Gfm, T.Gfm.ap[:, 127:128], 1.0, OP.subtract, OP.mult,
                 sdep=[T.Gfm])
            B.act(T.e2t, T.tG, AF.Exp, scale=-1.0 / 16.0)
            eGb = T.eG.ap[:, None, :].broadcast_to([128, 4, 128])
            B.stt(T.qtfm, T.qsfm, 0.5, T.eG, OP.mult, OP.mult,
                  oap=T.qtfm.ap.rearrange("p (hh t) -> p hh t", t=128),
                  i0ap=T.qsfm.ap.rearrange("p (hh t) -> p hh t", t=128),
                  i1ap=eGb)
            B.tt(T.kbfm, T.ksfm, T.eNG, OP.mult)
            B.tt(T.khfm, T.ksfm, T.e2t, OP.mult)
            B.trans8(W, [(T.khtok, T.khtok.ap, T.khfm, T.khfm.ap)], s_idb)

        def st3p(b, m, T):
            W = psWs[b]
            B.group("tensor", [T.kbfm, T.qtfm], [W],
                    [lambda h, j=j, o=W.ap, kb=T.kbfm.ap, qt=T.qtfm.ap:
                     h.matmul(o[:, j * 128:(j + 1) * 128], lhsT=kb,
                              rhs=qt[:, j * 128:(j + 1) * 128],
                              start=True, stop=True) for j in range(4)])
            u4 = s_u01.ap[:, None, :].broadcast_to([128, 4, 128])
            B.tt(T.pglat, W, s_u01, OP.mult,
                 oap=T.pglat.ap.rearrange("p (hh t) -> p hh t", t=128),
                 aap=W.ap.rearrange("p (hh t) -> p hh t", t=128), bap=u4)

        def st3h(b, m, T, pp):
            W, Wv, Ob = psWs[b], psWb[b], psOs[b]
            cur, prv = m % 2, 1 - m % 2
            h0 = 2 * pp
            fns = []
            ins = [T.vtok[cur], T.pglat]
            first = [True]
            for hh in (h0, h0 + 1):
                osli = Ob.ap[:, hh * 128:(hh + 1) * 128]
                if m > 0:
                    fns.append(lambda h, o=osli, hh=hh, sb=T.Sb.ap,
                               qt=T.qtfm.ap, st=first[0]: h.matmul(
                                   o, lhsT=sb,
                                   rhs=qt[:, hh * 128:(hh + 1) * 128],
                                   start=st, stop=False))
                    first[0] = False
                fns.append(lambda h, o=osli, hh=hh, v=T.vtok[cur].ap,
                           pg=T.pglat.ap, st=first[0]: h.matmul(
                               o, lhsT=v, rhs=pg[:, hh * 128:(hh + 1) * 128],
                               start=st, stop=False))
                first[0] = False
            if m > 0:
                ins += [T.Sb, T.qtfm]
            B.group("tensor", ins, [Ob], fns)
            specs = []
            ins2 = [T.sqfm, T.skfm[cur]]
            for j, hh in enumerate((h0, h0 + 1)):
                specs.append((j * 192 + 64, j * 192 + 192, hh, False))
            if m > 0:
                for j, hh in enumerate((h0, h0 + 1)):
                    specs.append((j * 192, j * 192 + 64, hh, True))
                ins2.append(T.skfm[prv])
            fns = []
            for si, (c0, c1, hh, use_prev) in enumerate(specs):
                fns.append(lambda h, c0=c0, c1=c1, hh=hh, up=use_prev,
                           st=(si == 0), sp=(si == len(specs) - 1),
                           sk=T.skfm[cur].ap, skp=T.skfm[prv].ap,
                           sq=T.sqfm.ap, o=W.ap: h.matmul(
                               o[:, c0:c1],
                               lhsT=sq[:, hh * 128:(hh + 1) * 128],
                               rhs=(skp[:, 64:128] if up else sk),
                               start=st, stop=sp))
            B.group("tensor", ins2, [W], fns)
            lo = 64 if m == 0 else 0
            red = T.red
            w2v = W.ap.rearrange("p (j c) -> p j c", c=192) if False else None
            scm2 = T.scm.ap.rearrange("p (j c) -> p j c", c=192)
            wv2 = W.ap[:, 0:384].rearrange("p (j c) -> p j c", c=192)
            m2 = s_mask.ap[:, None, :].broadcast_to([128, 2, 192])
            B.tt(T.scm, W, s_mask, OP.add,
                 oap=scm2[:, :, lo:192], aap=wv2[:, :, lo:192],
                 bap=m2[:, :, lo:192])
            B.act(T.scm, T.scm, AF.Exp, scale=INV_SQRT_D,
                  oap=T.scm.ap[:, 0:384] if lo == 0 else T.scm.ap[:, 0:0],
                  iap=T.scm.ap[:, 0:384] if lo == 0 else T.scm.ap[:, 0:0])                 if False else None
            if lo == 0:
                B.act(T.scm, T.scm, AF.Exp, scale=INV_SQRT_D,
                      oap=T.scm.ap[:, 0:384], iap=T.scm.ap[:, 0:384])
            else:
                for j in range(2):
                    B.act(T.scm, T.scm, AF.Exp, scale=INV_SQRT_D,
                          oap=T.scm.ap[:, j * 192 + 64:(j + 1) * 192],
                          iap=T.scm.ap[:, j * 192 + 64:(j + 1) * 192])
            B.red(red, T.scm, OP.add, oap=red.ap[:, 2:4],
                  iap=scm2[:, :, lo:192])
            B.recip(red, red, oap=red.ap[:, 4:6], iap=red.ap[:, 2:4])
            r2 = red.ap[:, 4:6][:, :, None].broadcast_to([128, 2, 192 - lo])
            B.stt(T.pswa, T.scm, 0.5, red, OP.mult, OP.mult,
                  oap=T.pswa.ap.rearrange("p (j c) -> p j c", c=192)[:, :, lo:192],
                  i0ap=scm2[:, :, lo:192], i1ap=r2)
            tl = [(T.ptc, T.ptc.ap, T.pswa, T.pswa.ap[:, 64:192]),
                  (T.ptc, T.ptc2.ap, T.pswa, T.pswa.ap[:, 256:384])]
            if m > 0:
                tl.append((T.ptp, T.ptp.ap[64:128, :], T.pswa,
                           T.pswa.ap[:, 0:64]))
                tl.append((T.ptp, T.ptp2.ap[64:128, :], T.pswa,
                           T.pswa.ap[:, 192:256]))
            B.trans8(Wv, tl, s_idb)
            fns = []
            ins3 = [T.vtok[cur], T.ptc, T.ptc]
            nmm = 2 if m == 0 else 4
            k = [0]
            for j, hh in enumerate((h0, h0 + 1)):
                osli = Ob.ap[:, hh * 128:(hh + 1) * 128]
                ptc = T.ptc.ap if j == 0 else T.ptc2.ap
                ptp = T.ptp.ap if j == 0 else T.ptp2.ap
                k[0] += 1
                fns.append(lambda h, o=osli, v=T.vtok[cur].ap, pt=ptc,
                           sp=(k[0] == nmm): h.matmul(
                               o, lhsT=v, rhs=pt, start=False, stop=sp))
                if m > 0:
                    k[0] += 1
                    fns.append(lambda h, o=osli, vp=T.vtok[prv].ap, pt=ptp,
                               sp=(k[0] == nmm): h.matmul(
                                   o, lhsT=vp[64:128, :], rhs=pt[64:128, :],
                                   start=False, stop=sp))
            if m > 0:
                ins3 += [T.vtok[prv], T.ptp]
            B.group("tensor", ins3, [Ob], fns)

        def st4(b, m, T):
            W, Ob = psWs[b], psOs[b]
            cur = m % 2
            t0 = b * N_ + m * 128
            rank, tis = t0 // TPC, t0 % TPC
            B.copy("vector", T.obf, Ob)
            dsto = d_osl[rank * 128:(rank + 1) * 128, :].rearrange(
                "p (hh c) -> p hh c", c=TPC)
            B.dma("scalar", osl.sub((b, m)), T.obf,
                  oap=dsto[:, :, tis:tis + 128],
                  iap=T.obf.ap.rearrange("p (hh t) -> p hh t", t=128))
            if m > 0:
                B.act(T.Ssc, T.Sf, AF.Identity, scale=T.eG.ap[:, 127:128],
                      sdep=[T.eG])
            B.group("tensor", [T.khtok, T.vtok[cur]], [W],
                    [lambda h, v=T.vtok[cur].ap, kh=T.khtok.ap, o=W.ap:
                     h.matmul(o[:, 0:128], lhsT=kh, rhs=v,
                              start=True, stop=True)])
            if m > 0:
                B.tt(T.Sf, W, T.Ssc, OP.add, aap=W.ap[:, 0:128])
            else:
                B.copy("vector", T.Sf, W, iap=W.ap[:, 0:128])
            B.copy("scalar", T.Sb, T.Sf)

        qi = 0
        for m in range(nchunks):
            for b in range(nb):
                st0(b, m, TS[b], xfm[(m * nb + b) % 2])
            for b in range(nb):
                st1a(b, m, TS[b])
            for b in range(nb):
                st1b(b, m, TS[b])
            for b in range(nb):
                st2a(b, m, TS[b])
            for b in range(nb):
                st2b(b, m, TS[b])
            for b in range(nb):
                st2c(b, m, TS[b])
            for b in range(nb):
                st3p(b, m, TS[b])
            for pp in range(2):
                for b in range(nb):
                    st3h(b, m, TS[b], pp)
            for b in range(nb):
                st4(b, m, TS[b])
            emit_quant_tile(w12_only=True)
        while emit_quant_tile(w12_only=True):
            pass
        AR.bot = bot0
        if do_a2a:
            B.do("gpsimd", [osl], [oex],
                 lambda h: _a2a(h, d_osl, d_oex))

    if upto >= 2:
        B.barrier()
        AR.bot = 0
        AR.top = top_quant
        ores = AR.carve([128, KC * 1024], BF16)
        woq = AR.carve([128, KC * 512], BF16)
        xq = [AR.carve([128, 512], F32) for _ in range(2)]
        x2t = [AR.carve([128, 512], F32) for _ in range(2)]
        for hidx in range(KC):
            B.dma("sync", ores.sub(hidx), oex,
                  oap=ores.ap[:, hidx * 1024:(hidx + 1) * 1024],
                  iap=d_oex[(hidx // 4) * 128:(hidx // 4 + 1) * 128,
                            (hidx % 4) * 1024:((hidx % 4) + 1) * 1024])
        for e in range(8):
            for kc in range(KC):
                B.dma("gpsimd", woq.sub((e, kc)), wod,
                      oap=woq.ap[:, kc * 512:(kc + 1) * 512],
                      iap=d_wo[kc * 128:(kc + 1) * 128,
                               e * 512:(e + 1) * 512])
            for tt in range(8):
                emit_quant_tile()
                xg = xq[tt % 2]
                B.dma("sync", xg, xin,
                      iap=d_x[tt * 128:(tt + 1) * 128, e * 512:(e + 1) * 512])
                bank = psQ if tt % 2 == 0 else psKV
                B.group("tensor", [ores, woq], [bank],
                        [lambda h, kc=kc, bank=bank, tt=tt:
                         h.matmul(bank.ap,
                                  lhsT=ores.ap[:, kc * 1024 + tt * 128:
                                               kc * 1024 + (tt + 1) * 128],
                                  rhs=woq.ap[:, kc * 512:(kc + 1) * 512],
                                  start=(kc == 0), stop=(kc == KC - 1))
                         for kc in range(KC)])
                xo = x2t[tt % 2]
                B.tt(xo, bank, xg, OP.add, aap=bank.ap, bap=xg.ap, bdep=[xg])
                B.dma("scalar", x2d.sub((e, tt)), xo,
                      oap=d_x2[tt * 128:(tt + 1) * 128, e * 512:(e + 1) * 512])
        B.barrier()
        AR.bot = 0
        while emit_quant_tile():
            pass
        ln_phase(d_x2, x2d, d_ag2i, ag2i, kc_major=True)
        B.do("gpsimd", [ag2i], [ag2o],
             lambda h: h.collective_compute(
                 "AllGather", OP.bypass, replica_groups=[list(range(NCORES))],
                 ins=[d_ag2i.ap().opt()], outs=[d_ag2o.ap().opt()]))

    if upto >= 3:
        B.barrier()
        AR.bot = 0
        AR.top = AR.nbytes
        sdq = AR.carve([128, KC * HC], BF16)
        xsp = [AR.carve([128, KC * 512], BF16) for _ in range(2)]
        gtile = AR.carve([128, 512], BF16)
        sgt = AR.carve([128, 512], F32)
        htile = AR.carve([128, 512], BF16)

        def up_pass(ddq, dqtt, is_up):
            B.dma("sync", sdq, dqtt, iap=ddq[:, :])
            for s in range(16):
                xg = xsp[s % 2]
                r = s // 2
                off = (s % 2) * 512
                srcr = d_ag2o[r * 128:(r + 1) * 128, :].rearrange(
                    "p (kc q) -> p kc q", q=1024)
                for g8 in range(4):
                    B.dma("sync", xg.sub(g8), ag2o,
                          oap=xg.ap.rearrange("p (kc q) -> p kc q", q=512)
                          [:, g8 * 8:(g8 + 1) * 8, :],
                          iap=srcr[:, g8 * 8:(g8 + 1) * 8, off:off + 512])
                for of in range(OFC):
                    bank = banks5[of % 5]
                    B.group("tensor", [sdq, xg], [bank],
                            [lambda h, kc=kc, of=of, bank=bank, xg=xg: h.matmul(
                                bank.ap,
                                lhsT=sdq.ap[:, kc * HC + of * 128:
                                            kc * HC + (of + 1) * 128],
                                rhs=xg.ap[:, kc * 512:(kc + 1) * 512],
                                start=(kc == 0), stop=(kc == KC - 1))
                             for kc in range(KC)])
                    col = of * TOKS + s * 512
                    if not is_up:
                        B.copy("vector", gtile, bank)
                        B.dma("scalar", gated.sub((of, s)), gtile,
                              oap=d_gate[:, col:col + 512])
                    else:
                        B.dma("sync", gtile, gated,
                              iap=d_gate[:, col:col + 512])
                        B.act(sgt, gtile, AF.Silu)
                        B.tt(htile, bank, sgt, OP.mult, aap=bank.ap)
                        B.dma("scalar", hd.sub((of, s)), htile,
                              oap=d_h[:, col:col + 512])

        up_pass(d_dq1, dq1d, is_up=False)
        up_pass(d_dq2, dq2d, is_up=True)

        B.barrier()
        AR.bot = 0
        sdq3 = AR.carve([128, OFC * DM], BF16)
        hb = [AR.carve([128, OFC * 128], BF16) for _ in range(2)]
        pt = [AR.carve([128, 1024], F32) for _ in range(2)]
        B.dma("sync", sdq3, dq3d, iap=d_dq3[:, :])
        for q in range(4):
            if q > 0 and DOWN_QBAR:
                B.barrier()
            for tt2 in range(64):
                hg = hb[tt2 % 2]
                B.dma("sync", hg, hd,
                      oap=hg.ap.rearrange("p (hc t) -> p hc t", t=128),
                      iap=d_h[:, :].rearrange("p (hc q2) -> p hc q2", q2=TOKS)
                      [:, :, tt2 * 128:(tt2 + 1) * 128])
                for half, bank in ((0, psQ), (1, psKV)):
                    B.group("tensor", [sdq3, hg], [bank],
                            [lambda h, hc=hc, half=half, bank=bank, hg=hg, q=q:
                             h.matmul(bank.ap,
                                      lhsT=hg.ap[:, hc * 128:(hc + 1) * 128],
                                      rhs=sdq3.ap[:, hc * DM + q * 1024 + half * 512:
                                                  hc * DM + q * 1024 + (half + 1) * 512],
                                      start=(hc == 0), stop=(hc == OFC - 1))
                             for hc in range(OFC)])
                po = pt[tt2 % 2]
                B.copy("vector", po, psQ, oap=po.ap[:, 0:512])
                B.copy("vector", po, psKV, oap=po.ap[:, 512:1024])
                B.dma("scalar", partd[q].sub(tt2), po,
                      oap=d_part[q][tt2 * 128:(tt2 + 1) * 128, :])
            B.do("gpsimd", [partd[q]], [rsod[q]],
                 lambda h, q=q: h.collective_compute(
                     "ReduceScatter", OP.add,
                     replica_groups=[list(range(NCORES))],
                     ins=[d_part[q].ap().opt()], outs=[d_rso[q].ap().opt()]))

    if upto >= 4:
        B.barrier()
        AR.bot = 0
        x2f = [AR.carve([128, DM], F32) for _ in range(2)]
        yf = [AR.carve([128, DM], F32) for _ in range(2)]
        of_ = [AR.carve([128, DM], F32) for _ in range(2)]
        for tt in range(8):
            xg, yg, og = x2f[tt % 2], yf[tt % 2], of_[tt % 2]
            B.dma("sync", xg, x2d, iap=d_x2[tt * 128:(tt + 1) * 128, :])
            for q in range(4):
                B.dma("sync", yg.sub(q), rsod[q],
                      oap=yg.ap[:, q * 1024:(q + 1) * 1024],
                      iap=d_rso[q][tt * 128:(tt + 1) * 128, :])
            B.tt(og, xg, yg, OP.add)
            B.dma("scalar", outd.sub(tt), og,
                  oap=d_out[tt * 128:(tt + 1) * 128, :])

    B.barrier()
    tapmap = {"ag1i": (d_ag1i, ag1i), "ag1o": (d_ag1o, ag1o),
              "osl": (d_osl, osl), "oex": (d_oex, oex),
              "x2": (d_x2, x2d), "ag2i": (d_ag2i, ag2i), "ag2o": (d_ag2o, ag2o),
              "dq1": (d_dq1, dq1d), "dq2": (d_dq2, dq2d), "dq3": (d_dq3, dq3d),
              "gate": (d_gate, gated), "h": (d_h, hd),
              "part0": (d_part[0], partd[0]), "rso0": (d_rso[0], rsod[0]),
              "rso1": (d_rso[1], rsod[1]), "rso2": (d_rso[2], rsod[2]),
              "rso3": (d_rso[3], rsod[3]), "part1": (d_part[1], partd[1])}
    for name in taps:
        src, stt = tapmap[name]
        tout = nc.dram_tensor(f"tap_{name}", list(src.shape), src.dtype,
                              kind="ExternalOutput")
        B.dma("gpsimd", TT(tout, name=f"tap{name}"), stt,
              oap=tout[:, :], iap=src[:, :])
    for s in B.all_sems:
        if s.n:
            B.wait("gpsimd", (s, s.n, None))
    B.run_block()
    B.close()
    return nc


def _a2a(h, d_in, d_out):
    with allow_ici_all_to_all():
        return h.collective_compute(
            "AllToAll", OP.bypass, replica_groups=[list(range(NCORES))],
            ins=[d_in.ap().opt()], outs=[d_out.ap().opt()])


def _constants():
    pos = np.arange(N_, dtype=np.float64)
    half = np.arange(64, dtype=np.float64)
    inv = 1.0 / (1e4 ** (2.0 * half / 128.0))
    fr = pos[:, None] * inv[None, :]
    emb = np.concatenate([fr, fr], axis=-1)
    cos = np.cos(emb)
    sin = np.sin(emb)
    sin2 = sin.copy()
    sin2[:, 0:64] = -sin[:, 0:64]
    cos_t = cos.reshape(NCHUNK, 128, 128).transpose(1, 0, 2).reshape(128, NCHUNK * 128)
    sin_t = sin2.reshape(NCHUNK, 128, 128).transpose(1, 0, 2).reshape(128, NCHUNK * 128)
    ii = np.arange(128)[:, None]
    jj = np.arange(192)[None, :]
    rel = ii - (jj - 64)
    mask = np.where((rel >= 0) & (rel <= 64), 0.0, -1e30)
    jm = np.arange(128)
    u01 = (jm[:, None] <= jm[None, :]).astype(np.float32)
    cb = np.zeros((128, 16), np.float32)
    cb[:, 0] = 1e-5
    cb[:, 1] = 1.0
    for bi in range(1, 15, 2):
        cb[:, 2 + bi // 2] = -NF4_BOUNDS[bi]
    return (cos_t.astype(np.float32), sin_t.astype(np.float32),
            mask.astype(np.float32), u01, np.eye(128, dtype=ml_dtypes.bfloat16),
            cb)


def prepare_inputs(inputs):
    x = np.ascontiguousarray(
        inputs["L0_attn_hidden_states"].reshape(TOKS, DM).astype(np.float32))
    Wq = inputs["L0_attn_Wq"].astype(np.float32)
    Wk = inputs["L0_attn_Wk"].astype(np.float32)
    Wv = inputs["L0_attn_Wv"].astype(np.float32)
    Wo = np.ascontiguousarray(inputs["L0_attn_Wo"].astype(np.float32))
    w1 = inputs["L0_ffn_w1"].astype(np.float32)
    w2 = inputs["L0_ffn_w2"].astype(np.float32)
    w3 = inputs["L0_ffn_w3"].astype(np.float32)
    w1p = np.vstack([w1, np.zeros((HC * NCORES - w1.shape[0], DM), np.float32)])
    w2p = np.vstack([w2, np.zeros((HC * NCORES - w2.shape[0], DM), np.float32)])
    w3p = np.hstack([w3, np.zeros((DM, HC * NCORES - w3.shape[1]), np.float32)])
    cos_t, sin_t, mask, u01, idb, cb = _constants()
    in_maps = []
    for c in range(NCORES):
        in_maps.append({
            "xop": np.ascontiguousarray(x[c * TPC:(c + 1) * TPC]),
            "wq_sl": np.ascontiguousarray(Wq[:, c * NH * DH:(c + 1) * NH * DH]),
            "wk_sl": np.ascontiguousarray(Wk[:, c * DH:(c + 1) * DH]),
            "wv_sl": np.ascontiguousarray(Wv[:, c * DH:(c + 1) * DH]),
            "wo": Wo,
            "w1_sl": np.ascontiguousarray(w1p[c * HC:(c + 1) * HC]),
            "w2_sl": np.ascontiguousarray(w2p[c * HC:(c + 1) * HC]),
            "w3_sl": np.ascontiguousarray(w3p[:, c * HC:(c + 1) * HC]),
            "cos_t": cos_t, "sin2_t": sin_t, "mask_swa": mask,
            "u01": u01, "identb": idb, "cbias": cb,
        })
    return in_maps


_CACHED = {}


def kernel(**inputs):
    in_maps = prepare_inputs(inputs)
    if "nc" not in _CACHED:
        _CACHED["nc"] = build_program(upto=9)
    nc = _CACHED["nc"]
    res = run_bass_kernel_spmd(nc, in_maps, core_ids=list(range(NCORES)))
    out = np.concatenate([res.results[c]["out_own"] for c in range(NCORES)],
                         axis=0)
    return out.reshape(B_, N_, DM)
